# version 25
# baseline (speedup 1.0000x reference)
import numpy as np
from contextlib import ExitStack

import concourse.bass as bass
import concourse.mybir as mybir
from concourse.bass_utils import run_bass_kernel_spmd

F32 = mybir.dt.float32
BF16 = mybir.dt.bfloat16
I32 = mybir.dt.int32
U32 = mybir.dt.uint32
AF = mybir.ActivationFunctionType
ALU = mybir.AluOpType
AX = mybir.AxisListType

D = 1024
H = 16
DH = 64
HGN = 8
NG = 2
NE = 8
NB = 2
NCORES = 8
LN_EPS = 1e-5
LNPI = float(np.log(np.pi))
LN2PI = float(np.log(2.0 * np.pi))
PI = float(np.pi)


class Sem:
    __slots__ = ("h", "n", "name")


class Prog:
    ENGS = ("sync", "vector", "scalar", "tensor", "gpsimd")

    def __init__(self, nc, ctx):
        self.nc = nc
        self.ctx = ctx
        self.steps = {e: [] for e in self.ENGS}

    def sem(self, name):
        s = Sem()
        s.h = self.ctx.enter_context(self.nc.semaphore(name))
        s.n = 0
        s.name = name
        return s

    def do(self, eng, fn, wait=(), inc=None, drain=False):
        w = [(s.h, int(v)) for (s, v) in wait if v > 0]
        incs = []
        if inc is not None:
            if not isinstance(inc, list):
                inc = [inc]
            for (s, k) in inc:
                s.n += k
                incs.append((s.h, k))
        self.steps[eng].append((fn, w, incs, drain))

    def emit(self):
        blk = self.ctx.enter_context(self.nc.Block())

        def mk(name):
            def body(eng):
                for fn, waits, incs, drain in self.steps[name]:
                    for (h, v) in waits:
                        eng.wait_ge(h, v)
                    if drain:
                        eng.drain()
                    inst = fn(eng)
                    for (h, k) in incs:
                        inst = inst.then_inc(h, k)
            return body

        blk.sync(mk("sync"))
        blk.vector(mk("vector"))
        blk.scalar(mk("scalar"))
        blk.tensor(mk("tensor"))
        blk.gpsimd(mk("gpsimd"))


def build(n_tiles=16, taps=(), stop_after='all', accum_mode='add'):
    NT = n_tiles
    N = 128 * NT
    nc = bass.Bass("TRN2", target_bir_lowering=False)
    ctx = ExitStack()
    p = Prog(nc, ctx)

    dx = nc.dram_tensor("x", [NB, N, D], F32, kind="ExternalInput").ap()
    dwqkv = nc.dram_tensor("wqkv", [3 * D, D], F32, kind="ExternalInput").ap()
    dwo = nc.dram_tensor("wo", [D, D], F32, kind="ExternalInput").ap()
    dlxi = nc.dram_tensor("log_xi", [H], F32, kind="ExternalInput").ap()
    dpgl = nc.dram_tensor("pi_gate", [H], F32, kind="ExternalInput").ap()
    degl = nc.dram_tensor("e_gate", [H], F32, kind="ExternalInput").ap()
    drw = nc.dram_tensor("router_w", [D, NE], F32, kind="ExternalInput").ap()
    drb = nc.dram_tensor("router_b", [NE], F32, kind="ExternalInput").ap()
    dew = nc.dram_tensor("expert_w", [NE, D, D], F32, kind="ExternalInput").ap()
    dscr = nc.dram_tensor("wex_scr", [NE, 128, 8 * D], BF16, kind="Internal").ap()
    dwscr = nc.dram_tensor("wts_scr", [NG, 128, 16384], BF16, kind="Internal").ap()
    deb = nc.dram_tensor("expert_b", [NE, D], F32, kind="ExternalInput").ap()
    dout = nc.dram_tensor("out", [NB, N, D], F32, kind="ExternalOutput").ap()
    dx_f = dx.rearrange("b n d -> (b n) d")
    dout_f = dout.rearrange("b n d -> (b n) d")

    tap_t = {}
    for tname, tshape, tdt in taps:
        tap_t[tname] = nc.dram_tensor("tap_" + tname, tshape, tdt,
                                      kind="ExternalOutput").ap()

    AL = 64
    left0 = (nc._sbuf_addr_for_side("left") + 63) // 64 * 64
    cur = [0]
    offs = {}

    def place(name, shape, dtype, at=None):
        nbytes = int(np.prod(shape[1:])) * mybir.dt.size(dtype)
        if at is None:
            off = cur[0]
            cur[0] = off + ((nbytes + AL - 1) // AL) * AL
        else:
            off = at
        offs[name] = off
        return nc.alloc_sbuf_tensor_at(name, list(shape), dtype, offset=left0 + off)

    ident = place("ident", [128, 128], BF16)
    mask_st = place("mask_st", [128, 128], BF16)
    iota8 = place("iota8", [128, NE], F32)
    ones1 = place("ones1", [1, 128], BF16)
    rw_bf = place("rw_bf", [128, 8 * NE], BF16)
    rb_row = place("rb_row", [1, NE], BF16)
    lxi = place("lxi", [H, 1], F32)
    pgl = place("pgl", [H, 1], F32)
    egl = place("egl", [H, 1], F32)
    eg_s = place("eg_s", [H, 1], F32)
    xinv = place("xinv", [H, 1], F32)
    pig = place("pig", [H, 1], F32)
    c_vec = place("c_vec", [H, 1], F32)
    ones_f = place("ones_f", [1, 128], F32)
    sin_tab = place("sin_tab", [128, NT * 512], BF16)
    cos_tab = place("cos_tab", [128, NT * 512], BF16)
    wqT = place("wqT", [128, 8 * 512], BF16)
    xnT = place("xnT", [128, 8 * N], BF16)
    attnout = place("attnout", [128, 4 * N], BF16)
    regR = cur[0]
    sel = place("sel", [16, 512], F32, at=regR)
    kc_all = place("kc_all", [H, N], F32, at=regR)
    kciota = place("kciota", [16, N], F32, at=regR + 8192)
    kc_hibf = place("kc_hibf", [16, N], BF16, at=regR + 8192)
    eb_bf = place("eb_bf", [NE, D], BF16, at=regR + 16384)
    io2 = place("io2", [128, 128], F32, at=regR + 18432)
    utmp = place("utmp", [128, 512], F32, at=regR + 18944)
    itmp = place("itmp", [128, 512], I32, at=regR + 20992)
    ftmp = place("ftmp", [128, 512], F32, at=regR + 23040)
    freq_rep = place("freq_rep", [128, 512], F32, at=offs["attnout"] + 8192)
    gtmp = place("gtmp", [128, 512], F32, at=regR + 27136)
    f_iota = place("f_iota", [1, 512], F32, at=regR + 27136)
    freq_row = place("freq_row", [1, 512], F32, at=regR + 28160)
    x_sb = [place("x0", [128, D], F32, at=regR + 18944),
            place("x1", [128, D], F32, at=regR + 23040)]
    rs_stage = place("rs_stage", [1, NT * 1024], BF16, at=regR)
    cur[0] = regR + max(NT * 1024 * 2, 32768)

    regA = cur[0]
    acc = place("acc", [128, NT * D], BF16, at=regA)
    k_aug = place("k_aug", [68, HGN * N], BF16, at=regA)
    cur[0] = regA + NT * D * 2

    regB = cur[0]
    wex = [place("wex0", [128, 8 * D], BF16, at=regB),
           place("wex1", [128, 8 * D], BF16, at=regB + 8 * D * 2)]
    o = regB
    v_aug = place("v_aug", [128, NT * HGN * 65], BF16, at=o); o += NT * HGN * 65 * 2
    attnT = [place("attnT0", [128, 2048], BF16, at=o),
             place("attnT1", [128, 2048], BF16, at=o + 4096)]
    o += 8192
    q_aug = place("q_aug", [68, HGN * 128], BF16, at=o); o += HGN * 128 * 2
    q12 = place("q12", [128, 544], BF16, at=o); o += 1088
    r12 = place("r12", [128, 512], BF16, at=o); o += 1024
    t1b = place("t1b", [128, 256], BF16, at=o)
    tsq = place("tsq", [128, 512], BF16, at=o); o += 512
    t2b = place("t2b", [128, 256], BF16, at=o); o += 512
    ta = place("ta", [128, 256], BF16, at=o); o += 512
    tb = place("tb", [128, 256], BF16, at=o); o += 512
    tc = place("tc", [128, 256], BF16, at=o); o += 512
    td = place("td", [128, 256], BF16, at=o); o += 512
    cur[0] = max(o, regB + 2 * 8 * D * 2)

    regC = cur[0]
    wkvT = place("wkvT", [128, 8 * D], BF16, at=regC)
    woT = place("woT", [128, 4 * D], BF16, at=regC)
    out_sb = [place("outsb0", [128, D], F32, at=regC + 4 * D * 2),
              place("outsb1", [128, D], F32, at=regC + 4 * D * 2 + D * 4)]
    cur[0] = regC + 8 * D * 2

    regD = cur[0]
    stage = [place("stage0", [128, D], BF16, at=regD),
             place("stage1", [128, D], BF16, at=regD + D * 2)]
    xn_tok = [place("xntok0", [128, D], BF16, at=regD),
              place("xntok1", [128, D], BF16, at=regD + D * 2)]
    cur[0] = regD + 2 * D * 2

    lnstats = place("lnstats", [128, 12], F32)
    lnmv = place("lnmv", [128, 2], F32)
    lnv = place("lnv", [128, 1], F32)
    rstd = place("rstd", [128, 1], F32)
    nmr = place("nmr", [128, 1], F32)
    smalls0 = cur[0]
    cur[0] += 96
    mstats = place("mstats", [128, 1], F32, at=smalls0)
    srow_bf = place("srow_bf", [1, 8], BF16, at=smalls0 + 32)
    m2b = place("m2b", [8, 1], BF16, at=smalls0 + 64)
    utmp2 = place("utmp2", [128, 512], F32, at=offs["attnout"])
    itmp2 = place("itmp2", [128, 512], I32, at=offs["attnout"] + 2048)
    ftmp2 = place("ftmp2", [128, 512], F32, at=offs["attnout"] + 4096)
    gtmp2 = place("gtmp2", [128, 512], F32, at=offs["attnout"] + 6144)
    kmax = place("kmax", [128, 8], F32)
    ktmp = place("ktmp", [128, 8], F32)
    km_b = place("km_b", [128, 8], BF16)
    crow_f = place("crow_f", [1, H], F32)
    qn2 = place("qn2", [128, 8], F32)
    bq2 = place("bq2", [128, 8], F32)
    tB = place("tB", [128, 8], F32)
    bmc = place("bmc", [128, 16], F32)
    dummy1 = place("dummy1", [1, 1], F32)
    tokc = place("tokc", [128, NT], F32)
    idf16 = place("idf16", [16, 16], F32)
    pexp = place("pexp", [128, NE], F32, at=regR + 29184)
    sume = place("sume", [128, 1], F32, at=regR + 29248)
    gate = place("gate", [128, 1], F32, at=regR + 29312)
    mx8 = place("mx8", [128, NE], F32, at=regR + 29376)
    idx8 = place("idx8", [128, NE], U32, at=regR + 29440)
    idxf = place("idxf", [128, 1], F32, at=regR + 29504)
    G_bf = place("G_bf", [128, NT * NE], BF16, at=regR + 28160)
    gt_sb = [place("gt0", [8, 128], BF16, at=regR + 28160 + 512),
             place("gt1", [8, 128], BF16, at=regR + 28160 + 768)]
    cb_eps = place("cb_eps", [128, 1], F32)
    cb_ph = place("cb_ph", [128, 1], F32)
    cb_fr = place("cb_fr", [128, 1], F32)

    assert cur[0] <= 212200, f"SBUF over budget: {cur[0]}"
    nc.alloc_sbuf_tensor("arena_reserve", [128, (cur[0] + 3) // 4], I32)

    pU0 = nc.alloc_psum_tensor("pU0", [128, 1024], F32)
    pU1 = nc.alloc_psum_tensor("pU1", [128, 1024], F32)
    pU2 = nc.alloc_psum_tensor("pU2", [128, 1024], F32)
    pOut = nc.alloc_psum_tensor("pOut", [128, 512], F32)
    pQ = nc.alloc_psum_tensor("pQ", [128, 512], F32)
    pU2b = pU2.bitcast(BF16)
    pOutb = pOut.bitcast(BF16)
    pQb = pQ.bitcast(BF16)
    units = []
    for t_, c0 in ((pU0, 0), (pU0, 512), (pU1, 0), (pU1, 512), (pU2, 0), (pU2, 512)):
        units.append({"t": t_, "c": c0, "free": None})

    sT = p.sem("sT")
    sV = p.sem("sV")
    sS = p.sem("sS")
    sG = p.sem("sG")
    dX = p.sem("dX")
    dx01 = [p.sem("dx0"), p.sem("dx1")]
    dw01 = [p.sem("dw0"), p.sem("dw1")]
    dst01 = [p.sem("dst0"), p.sem("dst1")]
    dkc = p.sem("dkc")
    dwb = p.sem("dwb")
    dox01 = [p.sem("dox0"), p.sem("dox1")]
    doa01 = [p.sem("doa0"), p.sem("doa1")]
    dtap = p.sem("dtap")

    def tap(name, src_ap, wait):
        if name in tap_t:
            d = tap_t[name]
            p.do("scalar", lambda e, d=d, s=src_ap: e.dma_start(out=d, in_=s),
                 wait=wait, inc=(dtap, 16))

    xnT_r = xnT.ap().rearrange("p (r c) -> p r c", r=8)
    k_aug_r = k_aug.ap().rearrange("p (h c) -> p h c", h=HGN)
    v_aug_r = v_aug.ap().rearrange("p (j h c) -> p j h c", j=NT, h=HGN)
    attnout_r = attnout.ap().rearrange("p (b c) -> p b c", b=4)
    pQb_r = pQb.ap().rearrange("p (h c) -> p h c", h=8)
    wqT_r = wqT.ap().rearrange("p (r c) -> p r c", r=8)
    wkvT_r = wkvT.ap().rearrange("p (r c) -> p r c", r=8)
    woT_r = woT.ap().rearrange("p (pr c) -> p pr c", pr=4)

    p.do("sync", lambda e: e.dma_start(out=lxi[:], in_=dlxi.rearrange("(h o) -> h o", o=1)), inc=(dX, 16))
    p.do("sync", lambda e: e.dma_start(out=pgl[:], in_=dpgl.rearrange("(h o) -> h o", o=1)), inc=(dX, 16))
    p.do("sync", lambda e: e.dma_start(out=egl[:], in_=degl.rearrange("(h o) -> h o", o=1)), inc=(dX, 16))
    dX_head = dX.n
    p.do("gpsimd", lambda e: e.dma_start(out=rw_bf.ap().rearrange("p (c e) -> p c e", c=8),
                                         in_=drw.rearrange("(c p) e -> p c e", p=128)), inc=(dkc, 16))
    p.do("gpsimd", lambda e: e.dma_start(out=rb_row.ap().rearrange("p (o e) -> p o e", o=1),
                                         in_=drb.rearrange("(o e) -> o e", o=1)), inc=(dkc, 16))
    dkc_w = dkc.n

    p.do("gpsimd", lambda e: e.iota(io2[:], pattern=[[1, 128]], base=0, channel_multiplier=-1,
                                    allow_small_or_imprecise_dtypes=True), inc=(sG, 1))
    p.do("gpsimd", lambda e: e.iota(iota8[:], pattern=[[1, NE]], base=0, channel_multiplier=0,
                                    allow_small_or_imprecise_dtypes=True), inc=(sG, 1))
    p.do("gpsimd", lambda e: e.iota(f_iota[:], pattern=[[0, 16], [1, 32]], base=0, channel_multiplier=0,
                                    allow_small_or_imprecise_dtypes=True), inc=(sG, 1))
    p.do("gpsimd", lambda e: e.iota(sel[:], pattern=[[1, 512]], base=0, channel_multiplier=-32,
                                    allow_small_or_imprecise_dtypes=True), inc=(sG, 1))
    sG_iotas = sG.n

    p.do("vector", lambda e: e.tensor_scalar(ident[:], io2[:], 0.0, None, ALU.is_equal),
         wait=[(sG, 1)], inc=(sV, 1))
    p.do("vector", lambda e: e.tensor_scalar(mask_st[:], io2[:], 0.0, None, ALU.is_ge), inc=(sV, 1))
    p.do("vector", lambda e: e.tensor_scalar(idf16[:], io2[0:16, 0:16], 0.0, None, ALU.is_equal),
         inc=(sV, 1))
    p.do("vector", lambda e: e.memset(ones1[:], 1.0), inc=(sV, 1))
    p.do("vector", lambda e: e.tensor_scalar(ftmp[0:16, :], sel[:], 0.0, None, ALU.is_ge),
         wait=[(sG, sG_iotas)], inc=(sV, 1))
    p.do("vector", lambda e: e.scalar_tensor_tensor(sel[:], sel[:], 32.0, ftmp[0:16, :],
                                                    ALU.is_lt, ALU.mult),
         inc=(sV, 1), drain=True)
    p.do("vector", lambda e: e.memset(cb_eps[:], LN_EPS), inc=(sV, 1))
    p.do("vector", lambda e: e.memset(cb_ph[:], PI / 2), inc=(sV, 1))
    p.do("vector", lambda e: e.memset(cb_fr[:], LNPI - LN2PI), inc=(sV, 1))
    p.do("vector", lambda e: e.memset(ones_f[:], 1.0), inc=(sV, 1))
    sV_consts = sV.n

    p.do("scalar", lambda e: e.activation(out=eg_s[:], in_=egl[:], func=AF.Sigmoid),
         wait=[(dX, dX_head)], inc=(sS, 1))
    p.do("scalar", lambda e: e.activation(out=xinv[:], in_=lxi[:], func=AF.Exp, scale=-1.0), inc=(sS, 1))
    p.do("scalar", lambda e: e.activation(out=pig[:], in_=pgl[:], func=AF.Sigmoid), inc=(sS, 1))
    sS_heads = sS.n
    p.do("vector", lambda e: e.tensor_mul(c_vec[:], eg_s[:], xinv[:]),
         wait=[(sS, sS_heads)], inc=(sV, 1))
    sV_kc = sV.n
    p.do("gpsimd", lambda e: e.iota(tokc[:], pattern=[[128, NT]], base=0,
                                    channel_multiplier=1,
                                    allow_small_or_imprecise_dtypes=True), inc=(sG, 1))
    p.do("tensor", lambda e: e.matmul(pOut[0:1, 0:16], c_vec[:], idf16[:],
                                      start=True, stop=True, skip_group_check=True),
         wait=[(sV, sV_kc)], inc=(sT, 1))
    p.do("scalar", lambda e: e.activation(out=crow_f[:], in_=pOut[0:1, 0:16], func=AF.Copy),
         wait=[(sT, sT.n)], inc=(sS, 1))
    sS_crow = sS.n

    p.do("tensor", lambda e: e.matmul(pQ[0:1, 0:512], pig[:], sel[:], start=True, stop=True),
         wait=[(sS, sS_heads), (sV, sV_consts)], inc=(sT, 1))
    sT_pig = sT.n
    p.do("scalar", lambda e: e.activation(out=ftmp[0:1, 0:512], in_=f_iota[:], func=AF.Exp,
                                          scale=-LNPI / 32.0, bias=cb_fr[0:1, :]),
         wait=[(sG, sG_iotas), (sV, sV_consts)], inc=(sS, 1))
    sS_pif = sS.n
    p.do("vector", lambda e: e.tensor_mul(freq_row[0:1, :], ftmp[0:1, 0:512], pQ[0:1, 0:512]),
         wait=[(sT, sT_pig), (sS, sS_pif)], inc=(sV, 1))
    p.do("tensor", lambda e: e.matmul(pU0[:, 0:512], ones_f[:], freq_row[:],
                                      start=True, stop=True),
         wait=[(sV, sV.n)], inc=(sT, 1))
    p.do("vector", lambda e: e.tensor_copy(freq_rep[:], pU0[:, 0:512]),
         wait=[(sT, sT.n)], inc=(sV, 1))
    sV_freq = sV.n


    def build_wT(src_ap_fn, n_src, width, evac_fn, extra_first_wait=(),
                 extra_tr_wait=()):
        nblk = width // 128
        for s in range(n_src):
            buf = s % 2
            w_load = [(sV, sV.n)] if s >= 2 else list(extra_first_wait)
            p.do("gpsimd", lambda e, s=s, buf=buf, fn=src_ap_fn, width=width: e.dma_start(
                out=stage[buf][:, 0:width], in_=fn(s)),
                 wait=w_load, inc=(dst01[buf], 16))
            for r in range(nblk):
                w_ = [(dst01[buf], dst01[buf].n)] if r == 0 else []
                if r == 0:
                    w_.append((sV, sV.n))
                    if s == 0:
                        w_ += list(extra_tr_wait)
                p.do("tensor", lambda e, buf=buf, r=r: e.transpose(
                    pU2b[:, 128 * r:128 * (r + 1)], stage[buf][:, 128 * r:128 * (r + 1)], ident[:]),
                     wait=w_, inc=(sT, 1) if r == nblk - 1 else None)
            p.do("vector", lambda e, s=s, fn=evac_fn: e.tensor_copy(fn(s), pU2b[:, 0:width]),
                 wait=[(sT, sT.n)], inc=(sV, 1))



    def emit_tab(i, first_wait=()):
        p.do("vector", lambda e, i=i: e.tensor_scalar(utmp2[:], freq_rep[:],
                                                      tokc[:, i:i + 1], None, ALU.mult),
             wait=list(first_wait) + [(sS, sS.n), (sV, sV.n)], inc=(sV, 1))
        p.do("vector", lambda e: e.tensor_copy(itmp2[:], utmp2[:]),
             wait=[(sV, sV.n)], inc=(sV, 1))
        p.do("vector", lambda e: e.tensor_copy(ftmp2[:], itmp2[:]),
             wait=[(sV, sV.n)], inc=(sV, 1))
        p.do("vector", lambda e: e.tensor_sub(ftmp2[:], utmp2[:], ftmp2[:]),
             wait=[(sV, sV.n)], inc=(sV, 1))
        p.do("vector", lambda e: e.tensor_scalar(gtmp2[:], ftmp2[:], 0.5, None, ALU.is_gt),
             wait=[(sV, sV.n)], inc=(sV, 1))
        p.do("vector", lambda e: e.tensor_sub(ftmp2[:], ftmp2[:], gtmp2[:]),
             wait=[(sV, sV.n)], inc=(sV, 1))
        p.do("scalar", lambda e, i=i: e.activation(out=sin_tab[:, 512 * i:512 * (i + 1)],
                                                   in_=ftmp2[:], func=AF.Sin, scale=2 * PI),
             wait=[(sV, sV.n)], inc=(sS, 1))
        p.do("vector", lambda e: e.tensor_scalar(itmp2[:], utmp2[:], 0.25, None, ALU.add),
             wait=[(sS, sS.n)], inc=(sV, 1))
        p.do("vector", lambda e: e.tensor_copy(ftmp2[:], itmp2[:]),
             wait=[(sV, sV.n)], inc=(sV, 1))
        p.do("vector", lambda e: e.tensor_sub(ftmp2[:], utmp2[:], ftmp2[:]),
             wait=[(sV, sV.n)], inc=(sV, 1))
        p.do("vector", lambda e: e.tensor_scalar(gtmp2[:], ftmp2[:], 0.25, None, ALU.is_gt),
             wait=[(sV, sV.n)], inc=(sV, 1))
        p.do("vector", lambda e: e.tensor_sub(ftmp2[:], ftmp2[:], gtmp2[:]),
             wait=[(sV, sV.n)], inc=(sV, 1))
        p.do("scalar", lambda e, i=i: e.activation(out=cos_tab[:, 512 * i:512 * (i + 1)],
                                                   in_=ftmp2[:], func=AF.Sin, scale=2 * PI,
                                                   bias=cb_ph[:]),
             wait=[(sV, sV.n)], inc=(sS, 1))

    wex_ready = {}
    echo_cnt = {}

    def load_expert(b, e_i, extra_wait):
        buf = e_i % 2
        if b == 0:
            p.do("gpsimd", lambda e, e_i=e_i, buf=buf: e.dma_start(
                out=wex[buf].ap().rearrange("p (c d) -> p c d", c=8),
                in_=dew[e_i].rearrange("(c p) d -> p c d", p=128)),
                 wait=extra_wait, inc=(dw01[buf], 16))
            wex_ready[e_i] = (dw01[buf], dw01[buf].n)
            p.do("sync", lambda e, e_i=e_i, buf=buf: e.dma_start(
                out=dscr[e_i], in_=wex[buf][:]),
                 wait=[(dw01[buf], dw01[buf].n)], inc=(dwb, 16))
            echo_cnt[e_i] = dwb.n
        else:
            p.do("sync", lambda e, e_i=e_i, buf=buf: e.dma_start(
                out=wex[buf][:], in_=dscr[e_i]),
                 wait=list(extra_wait) + [(dwb, dwb.n)], inc=(dwb, 16))
            wex_ready[e_i] = (dwb, dwb.n)

    def emit_ln(b):
        for i in range(NT):
            buf = i % 2
            row0 = b * N + 128 * i
            w_ = [(sS, sS.n)]
            if i < 2:
                w_ += [(sT, sT.n), (sV, sV.n), (dtap, dtap.n)]
            p.do("sync", lambda e, row0=row0, buf=buf: e.dma_start(
                out=x_sb[buf][:], in_=dx_f[row0:row0 + 128, :]),
                 wait=w_, inc=(dx01[buf], 16))
            p.do("vector", lambda e, buf=buf: e.bn_stats(lnstats[:, 0:6], x_sb[buf][:, 0:512]),
                 wait=[(dx01[buf], dx01[buf].n), (sS, sS.n)], inc=(sV, 1))
            p.do("vector", lambda e, buf=buf: e.bn_stats(lnstats[:, 6:12], x_sb[buf][:, 512:1024]),
                 inc=(sV, 1))
            p.do("vector", lambda e: e.bn_aggr(lnmv[:], lnstats[:]), inc=(sV, 1), drain=True)
            p.do("scalar", lambda e: e.activation(out=lnv[:], in_=lnmv[:, 1:2], func=AF.Ln, bias=cb_eps[:]),
                 wait=[(sV, sV.n)], inc=(sS, 1))
            p.do("scalar", lambda e: e.activation(out=rstd[:], in_=lnv[:], func=AF.Exp, scale=-0.5),
                 inc=(sS, 1), drain=True)
            p.do("vector", lambda e: e.tensor_scalar(nmr[:], lnmv[:, 0:1], rstd[:], -1.0,
                                                     ALU.mult, ALU.mult),
                 wait=[(sS, sS.n)], inc=(sV, 1), drain=True)
            p.do("scalar", lambda e, buf=buf: e.activation(out=xn_tok[buf][:], in_=x_sb[buf][:],
                                                           func=AF.Identity, scale=rstd[:], bias=nmr[:]),
                 wait=[(sV, sV.n)], inc=(sS, 1), drain=True)
            for r in range(8):
                w_ = [(sS, sS.n), (sV, sV.n)] if r == 0 else []
                p.do("tensor", lambda e, buf=buf, r=r: e.transpose(
                    pQb[:, 128 * r:128 * (r + 1)], xn_tok[buf][:, 128 * r:128 * (r + 1)], ident[:]),
                     wait=w_, inc=(sT, 1) if r == 7 else None)
            p.do("vector", lambda e, i=i: e.tensor_copy(
                xnT_r[:, :, 128 * i:128 * (i + 1)],
                pQb.ap().rearrange("p (r c) -> p r c", r=8)),
                 wait=[(sT, sT.n)], inc=(sV, 1))

    def emit_ffn(b):
        p.do("gpsimd", lambda e: e.dma_start(out=eb_bf[:], in_=deb),
             wait=[(sT, sT.n), (sV, sV.n), (dtap, dtap.n), (dkc, dkc.n)], inc=(dkc, 16))
        dkc_eb = dkc.n
        bev_hist = {}
        gtev_hist = {}
        pendB = [None]
        for i in range(NT):
            PL = [pOut, pQ][i % 2]
            PLb = [pOutb, pQb][i % 2]
            for r in range(8):
                w_ = [(sV, sV.n), (dkc, dkc_w)] if i == 0 and r == 0 else []
                if r == 0 and i >= 2:
                    w_.append((sV, gtev_hist[i - 2]))
                p.do("tensor", lambda e, i=i, r=r, PL=PL: e.matmul(
                    PL[:, 0:NE], xnT_r[:, r, 128 * i:128 * (i + 1)], rw_bf[:, NE * r:NE * (r + 1)],
                    start=(r == 0), stop=False), wait=w_)
            p.do("tensor", lambda e, PL=PL: e.matmul(PL[:, 0:NE], ones1[:], rb_row[:],
                                                     start=False, stop=True), inc=(sT, 1))
            sT_lg = sT.n
            if pendB[0] is not None:
                pendB[0]()
                pendB[0] = None
            p.do("vector", lambda e, PL=PL: e.tensor_reduce(mstats[:, 0:1], PL[:, 0:NE], AX.X,
                                                            ALU.max, negate=True),
                 wait=[(sT, sT_lg)], inc=(sV, 1), drain=True)
            p.do("scalar", lambda e, PL=PL: e.activation(out=pexp[:], in_=PL[:, 0:NE], func=AF.Exp,
                                                         bias=mstats[:, 0:1], accum_out=sume[:]),
                 wait=[(sV, sV.n), (sT, sT_lg)], inc=(sS, 1))
            p.do("vector", lambda e: e.reciprocal(gate[:], sume[:]),
                 wait=[(sS, sS.n)], inc=(sV, 1))
            p.do("vector", lambda e: e.max(mx8[:], pexp[:]), inc=(sV, 1), drain=True)
            p.do("vector", lambda e: e.max_index(idx8[:], mx8[:], pexp[:]), inc=(sV, 1), drain=True)
            p.do("vector", lambda e: e.tensor_copy(idxf[:], idx8[:, 0:1]), inc=(sV, 1), drain=True)
            p.do("vector", lambda e, i=i: e.tensor_scalar(G_bf[:, NE * i:NE * (i + 1)], iota8[:],
                                                          idxf[:], gate[:], ALU.is_equal, ALU.mult),
                 inc=(sV, 1), drain=True)
            sV_g = sV.n

            def emit_B(i=i, sV_g=sV_g, PLb=PLb):
                gbuf = i % 2
                p.do("tensor", lambda e, i=i, PLb=PLb: e.transpose(
                    PLb[0:NE, 896:1024], G_bf[:, NE * i:NE * (i + 1)], ident[:]),
                     wait=[(sV, sV_g)], inc=(sT, 1))
                p.do("vector", lambda e, gbuf=gbuf, PLb=PLb: e.tensor_copy(
                    gt_sb[gbuf][:], PLb[0:NE, 896:1024]),
                     wait=[(sT, sT.n)], inc=(sV, 1))
                gtev_hist[i] = sV.n
                pB = [pU0, pU1][i % 2]
                for c in range(2):
                    w_ = [(sV, sV.n), (dkc, dkc_eb)] if c == 0 else []
                    if c == 0 and i >= 2:
                        w_.append((sS, bev_hist[i - 2]))
                    p.do("tensor", lambda e, gbuf=gbuf, c=c, pB=pB: e.matmul(
                        pB[:, 512 * c:512 * (c + 1)], gt_sb[gbuf][:],
                        eb_bf[:, 512 * c:512 * (c + 1)],
                        start=True, stop=True), wait=w_, inc=(sT, 1) if c == 1 else None)
                p.do("scalar", lambda e, i=i, pB=pB: e.activation(out=acc[:, D * i:D * (i + 1)],
                                                                  in_=pB[:], func=AF.Copy),
                     wait=[(sT, sT.n)], inc=(sS, 1))
                bev_hist[i] = sS.n

            pendB[0] = emit_B
        pendB[0]()
        pendB[0] = None
        tab_left = list(range(NT)) if b == 0 else []
        sV_stt = {}

        def x_load(i):
            buf = i % 2
            p.do("sync", lambda e, i=i, buf=buf: e.dma_start(
                out=x_sb[buf][:], in_=dx_f[b * N + 128 * i:b * N + 128 * i + 128, :]),
                 wait=[(dox01[buf], dox01[buf].n)], inc=(dx01[buf], 16))

        def out_hook(j):
            if j == 0:
                x_load(0)
            if j + 1 < NT:
                x_load(j + 1)
            buf = j % 2
            p.do("vector", lambda e, buf=buf, j=j: e.tensor_add(
                x_sb[buf][:], x_sb[buf][:], acc[:, D * j:D * (j + 1)]),
                 wait=[(dx01[buf], dx01[buf].n), (sV, sV.n)], inc=(sV, 1), drain=(j == 0))
            p.do("sync", lambda e, j=j, buf=buf: e.dma_start(
                out=dout_f[b * N + 128 * j:b * N + 128 * j + 128, :], in_=x_sb[buf][:]),
                 wait=[(sV, sV.n)], inc=(dox01[buf], 16))
        for e_i in range(NE):
            buf = e_i % 2
            for i in range(NT):
                pEx = [pU1, pU2][i % 2]
                w0 = [wex_ready[e_i]]
                key = (e_i, i - 2) if i >= 2 else (e_i - 1, NT - 2 + i)
                if key in sV_stt:
                    w0.append((sV, sV_stt[key]))
                else:
                    w0.append((sV, sV.n))
                    w0.append((sS, sS.n))
                for r in range(8):
                    for c in range(2):
                        p.do("tensor", lambda e, i=i, r=r, c=c, buf=buf, pEx=pEx: e.matmul(
                            pEx[:, 512 * c:512 * (c + 1)],
                            xnT_r[:, r, 128 * i:128 * (i + 1)],
                            wex[buf][:, D * r + 512 * c: D * r + 512 * (c + 1)],
                            start=(r == 0), stop=(r == 7), skip_group_check=True),
                             wait=w0 if (r == 0 and c == 0) else [],
                             inc=(sT, 1) if (r == 7 and c == 1) else None)
                p.do("vector", lambda e, i=i, e_i=e_i, pEx=pEx: e.scalar_tensor_tensor(
                    acc[:, D * i:D * (i + 1)], pEx[:], G_bf[:, NE * i + e_i:NE * i + e_i + 1],
                    acc[:, D * i:D * (i + 1)], ALU.mult, ALU.add),
                     wait=[(sT, sT.n)], inc=(sV, 1), drain=(i == 0))
                sV_stt[(e_i, i)] = sV.n
                if b == 0 and i in (NT // 3, (2 * NT) // 3) and tab_left:
                    ti = tab_left.pop(0)
                    emit_tab(ti, [(sV, sV_freq)] if ti == 0 else ())
                if e_i == NE - 1:
                    out_hook(i)
            if e_i + 2 < NE:
                w_le = [(sT, sT.n)]
                if b == 0:
                    w_le.append((dwb, dwb.n))
                load_expert(b, e_i + 2, w_le)
        kv_gate[0] = sV_stt[(NE - 1, NT - 1)]

    def emit_rot(i, g, dst, sub_wait=None):
        cs = cos_tab.ap()[:, 512 * i + 256 * g: 512 * i + 256 * (g + 1)]
        sn = sin_tab.ap()[:, 512 * i + 256 * g: 512 * i + 256 * (g + 1)]
        dr = dst.ap().rearrange("p (h f) -> p h f", h=8)
        ta_r = ta.ap().rearrange("p (h f) -> p h f", h=8)
        tb_r = tb.ap().rearrange("p (h f) -> p h f", h=8)
        tc_r = tc.ap().rearrange("p (h f) -> p h f", h=8)
        td_r = td.ap().rearrange("p (h f) -> p h f", h=8)
        sS0 = sS.n
        p.do("gpsimd", lambda e: e.tensor_mul(tc[:], t2b[:], cs),
             wait=[(sS, sS0), (sV, sV.n)], inc=(sG, 1))
        p.do("gpsimd", lambda e: e.tensor_mul(td[:], t1b[:], sn), inc=(sG, 1))
        sG_rot = sG.n
        p.do("vector", lambda e: e.tensor_mul(ta[:], t1b[:], cs),
             wait=[(sS, sS0)], inc=(sV, 1))
        p.do("vector", lambda e: e.tensor_mul(tb[:], t2b[:], sn), inc=(sV, 1))
        c_tb = sV.n
        w_sub = [(sV, c_tb)] + ([sub_wait] if sub_wait else [])
        p.do("vector", lambda e: e.tensor_sub(dr[:, :, 0:32], ta_r, tb_r),
             wait=w_sub, inc=(sV, 1))
        p.do("vector", lambda e: e.tensor_add(dr[:, :, 32:64], tc_r, td_r),
             wait=[(sG, sG_rot)], inc=(sV, 1))

    w_ready = {"q": [], "kv": [], "wo": []}

    def build_wqT(g, wT=None, b=0):
        if b == 1:
            p.do("sync", lambda e, g=g: e.dma_start(out=wqT[:], in_=dwscr[g, :, 0:4096]),
                 wait=[(sT, sT.n), (dwb, dwb.n)], inc=(dwb, 16))
            w_ready["q"] = [(dwb, dwb.n)]
            return
        def srcq(s):
            return dwqkv[512 * g + 128 * s:512 * g + 128 * (s + 1), :]

        def evq(s):
            return wqT_r[:, :, 128 * s:128 * (s + 1)]

        build_wT(srcq, 4, D, evq,
                 extra_first_wait=[(sT, wT if wT is not None else sT.n),
                                   (sV, sV_ln[0]), (dwb, dwb.n)])
        p.do("sync", lambda e, g=g: e.dma_start(out=dwscr[g, :, 0:4096], in_=wqT[:]),
             wait=[(sV, sV.n)], inc=(dwb, 16))
        w_ready["q"] = []

    def build_wkvT(g, wT=None, b=0):
        if b == 1:
            p.do("sync", lambda e, g=g: e.dma_start(out=wkvT[:], in_=dwscr[g, :, 4096:12288]),
                 wait=[(sT, sT.n), (dwb, dwb.n), (doa01[0], doa01[0].n),
                       (doa01[1], doa01[1].n)], inc=(dwb, 16))
            w_ready["kv"] = [(dwb, dwb.n)]
            return
        def src(s):
            part = 0 if s < 4 else 1
            sl = s % 4
            src_row = D + part * D + 512 * g + 128 * sl
            return dwqkv[src_row:src_row + 128, :]

        def ev(s):
            part = 0 if s < 4 else 1
            sl = s % 4
            coff = 512 * part + 128 * sl
            return wkvT_r[:, :, coff:coff + 128]

        build_wT(src, 8, D, ev,
                 extra_first_wait=[(sT, wT if wT is not None else sT.n),
                                   (sV, sV_ln[0]), (dwb, dwb.n)],
                 extra_tr_wait=[(doa01[0], doa01[0].n), (doa01[1], doa01[1].n)])
        p.do("sync", lambda e, g=g: e.dma_start(out=dwscr[g, :, 4096:12288], in_=wkvT[:]),
             wait=[(sV, sV.n)], inc=(dwb, 16))
        w_ready["kv"] = []

    def build_woT(g, b=0):
        if b == 1:
            p.do("sync", lambda e, g=g: e.dma_start(out=woT[:], in_=dwscr[g, :, 12288:16384]),
                 wait=[(sT, sT.n), (dwb, dwb.n)], inc=(dwb, 16))
            w_ready["wo"] = [(dwb, dwb.n)]
            return
        build_wT(lambda s: dwo[128 * s:128 * (s + 1), 512 * g:512 * (g + 1)], 8, 512,
                 lambda s: woT_r[:, :, 128 * s:128 * (s + 1)],
                 extra_first_wait=[(sT, sT.n), (sV, sV_ln[0]), (dwb, dwb.n)])
        p.do("sync", lambda e, g=g: e.dma_start(out=dwscr[g, :, 12288:16384], in_=woT[:]),
             wait=[(sV, sV.n)], inc=(dwb, 16))
        w_ready["wo"] = []

    q12_r3 = q12.ap().rearrange("p (h f) -> p h f", h=8)
    t1b_r3 = t1b.ap().rearrange("p (h f) -> p h f", h=8)
    t2b_r3 = t2b.ap().rearrange("p (h f) -> p h f", h=8)
    ksq4 = attnT[0].ap()[:, 0:512].rearrange("p (h x f) -> p x h f", h=8, x=2)
    ksq3 = attnT[0].ap()[:, 0:512].rearrange("p (h f) -> p h f", h=8)
    qsq4 = r12.ap().rearrange("p (h x f) -> p x h f", h=8, x=2)
    qsq3 = r12.ap().rearrange("p (h f) -> p h f", h=8)

    kv_gate = [0]

    def emit_attn_prep(b):
        p.do("vector", lambda e: e.memset(v_aug_r[:, :, :, 64:65], 1.0),
             wait=[(sT, sT.n), (dtap, dtap.n), (dwb, dwb.n)], inc=(sV, 1))
        p.do("vector", lambda e: e.memset(q12_r3[:, :, 65:67], 1.0), inc=(sV, 1))

    def emit_kv(b, g, tile_hook=None):
        pPP = [pU0, pU1, pU2]
        hist = {}
        sV_kv0 = sV.n
        p.do("vector", lambda e: e.memset(kmax[:], 0.0),
             wait=[(sV, sV.n)], inc=(sV, 1))
        sse = {}
        c_rot = {}
        c_sq = {}
        c_red = {}
        c_tp = {}
        sT_grp = sT.n

        def kv_transposes(j):
            for h in range(8):
                w_ = [(sV, hist[j - 1][1] if j >= 1 else sV.n)] if h == 0 else []
                p.do("tensor", lambda e, h=h: e.transpose(
                    pQb[0:64, 128 * h:128 * (h + 1)], r12[:, 64 * h:64 * (h + 1)], ident[:]),
                     wait=w_, inc=(sT, 1) if h == 7 else None)
            c_tp[j] = sT.n

        def kv_copy(j):
            w_ = [(sT, c_tp[j])]
            if j == 0:
                w_.append((sV, sV_kv0))
            p.do("vector", lambda e, j=j: e.tensor_copy(
                k_aug_r[0:64, :, 128 * j:128 * (j + 1)], pQb_r[0:64, :, :]),
                 wait=w_, inc=(sV, 1))
            hist[j] = (sse[j], sV.n)

        for j in range(NT):
            pk = pPP[j % 3]
            w0 = ([(sS, sS.n), (sV, kv_gate[0] if g == 0 else sV.n), (dwb, dwb.n)]
                  + list(w_ready["kv"])) if j < 3 else \
                 [(sS, hist[j - 3][0]), (sV, hist[j - 3][1])]
            for r in range(8):
                for c in range(2):
                    p.do("tensor", lambda e, j=j, r=r, c=c, pk=pk: e.matmul(
                        pk[:, 512 * c:512 * (c + 1)],
                        xnT_r[:, r, 128 * j:128 * (j + 1)],
                        wkvT[:, D * r + 512 * c:D * r + 512 * (c + 1)],
                        start=(r == 0), stop=(r == 7), skip_group_check=True),
                         wait=w0 if (r == 0 and c == 0) else [],
                         inc=(sT, 1) if (r == 7 and c == 1) else None)
            sT_kv = sT.n
            wv_ = [(sT, sT_kv)]
            if j >= 1:
                wv_.append((sV, c_rot[j - 1]))
                wv_.append((sG, c_sq[j - 1]))
            else:
                wv_.append((sG, sG.n))
            p.do("scalar", lambda e, j=j, pk=pk: e.activation(
                out=v_aug_r[:, j, :, 0:64],
                in_=pk.ap()[:, 512:1024].rearrange("p (h c) -> p h c", h=8), func=AF.Copy),
                 wait=wv_, inc=(sS, 1))
            p.do("scalar", lambda e, pk=pk: e.activation(out=t1b[:], in_=pk.ap()[:, 0:512:2],
                                                         func=AF.Copy), inc=(sS, 1))
            p.do("scalar", lambda e, pk=pk: e.activation(out=t2b[:], in_=pk.ap()[:, 1:512:2],
                                                         func=AF.Copy), inc=(sS, 1))
            sse[j] = sS.n
            if j >= 1:
                kv_transposes(j - 1)
            emit_rot(j, g, r12,
                     sub_wait=(sT, c_tp[j - 1]) if j >= 1 else None)
            c_rot[j] = sV.n
            wsq = [(sS, sse[j])]
            if j >= 1:
                wsq.append((sV, c_red[j - 1]))
            else:
                wsq.append((sT, sT_grp))
            p.do("gpsimd", lambda e: e.tensor_tensor(out=ksq4[:, 0], in0=t1b_r3[:],
                                                     in1=t1b_r3[:], op=ALU.mult),
                 wait=wsq, inc=(sG, 1))
            p.do("gpsimd", lambda e: e.tensor_tensor(out=ksq4[:, 1], in0=t2b_r3[:],
                                                     in1=t2b_r3[:], op=ALU.mult),
                 inc=(sG, 1))
            c_sq[j] = sG.n
            p.do("vector", lambda e: e.tensor_reduce(ktmp[:], ksq3[:], AX.X, ALU.add),
                 wait=[(sG, c_sq[j])], inc=(sV, 1))
            c_red[j] = sV.n
            p.do("vector", lambda e: e.tensor_tensor(out=kmax[:], in0=kmax[:], in1=ktmp[:],
                                                     op=ALU.max),
                 wait=[(sV, c_red[j])], inc=(sV, 1))
            if j >= 1:
                kv_copy(j - 1)
        kv_transposes(NT - 1)
        kv_copy(NT - 1)
        p.do("vector", lambda e: e.memset(k_aug[64:68, :], 1.0),
             wait=[(sT, sT.n), (sV, sV.n), (dtap, dtap.n)], inc=(sV, 1))
        p.do("gpsimd", lambda e: e.iota(kciota[:], pattern=[[1, N]], base=0, channel_multiplier=0,
                                        allow_small_or_imprecise_dtypes=True),
             wait=[(sT, sT.n), (sV, sV.n), (dtap, dtap.n)], inc=(sG, 1))
        p.do("vector", lambda e: e.tensor_scalar(
            kc_all[:], kciota[:], c_vec[:], None, ALU.mult),
             wait=[(sG, sG.n), (sV, sV_kc)], inc=(sV, 1), drain=True)
        p.do("vector", lambda e: e.tensor_copy(kc_hibf[:], kc_all[:]),
             inc=(sV, 1), drain=True)
        p.do("vector", lambda e: e.tensor_sub(kc_all[:], kc_all[:], kc_hibf[:]),
             inc=(sV, 1), drain=True)
        p.do("gpsimd", lambda e, g=g: e.dma_start(
            out=k_aug.ap()[65:66, :].rearrange("p (h c) -> p h c", h=HGN),
            in_=kc_hibf.ap()[HGN * g:HGN * (g + 1), :]),
             wait=[(sV, sV.n), (sT, sT.n), (dkc, dkc.n)], inc=(dkc, 16))
        p.do("gpsimd", lambda e, g=g: e.dma_start(
            out=k_aug.ap()[66:67, :].rearrange("p (h c) -> p h c", h=HGN),
            in_=kc_all.ap()[HGN * g:HGN * (g + 1), :]),
             wait=[(dkc, dkc.n)], inc=(dkc, 16))
        p.do("vector", lambda e: e.tensor_copy(km_b[:], kmax[:]), inc=(sV, 1), drain=True)
        p.do("tensor", lambda e: e.transpose(pQb[0:8, 0:128], km_b[:], ident[:]),
             wait=[(sV, sV.n)], inc=(sT, 1))
        p.do("vector", lambda e: e.tensor_reduce(m2b[:], pQb[0:8, 0:128], AX.X, ALU.max),
             wait=[(sT, sT.n)], inc=(sV, 1))
        p.do("tensor", lambda e: e.transpose(pQb[0:1, 0:8], m2b[:], ident[0:8, 0:8]),
             wait=[(sV, sV.n)], inc=(sT, 1))
        p.do("scalar", lambda e: e.activation(out=srow_bf[:], in_=pQb[0:1, 0:8], func=AF.Copy),
             wait=[(sT, sT.n)], inc=(sS, 1))
        p.do("tensor", lambda e: e.matmul(pOut[:, 0:8], ones1[:], srow_bf[:],
                                          start=True, stop=True),
             wait=[(sS, sS.n)], inc=(sT, 1))
        p.do("tensor", lambda e, g=g: e.matmul(pOut[:, 8:16], ones_f[:],
                                               crow_f[0:1, HGN * g:HGN * (g + 1)],
                                               start=True, stop=True,
                                               skip_group_check=True),
             wait=[(sS, sS_crow)], inc=(sT, 1))
        p.do("scalar", lambda e: e.activation(out=bmc[:, 0:8], in_=pOut[:, 0:8], func=AF.Copy),
             wait=[(sT, sT.n)], inc=(sS, 1))
        p.do("scalar", lambda e: e.activation(out=bmc[:, 8:16], in_=pOut[:, 8:16],
                                              func=AF.Copy, scale=1.0001), inc=(sS, 1))

    def emit_attn(b, g):
        unit_i = [0]

        def next_unit():
            u = units[unit_i[0] % 6]
            unit_i[0] += 1
            return u

        first = [True]
        pend = [None]
        prev_av_end = [{}]
        qa = {}
        qev = {}
        rotv = {}
        sqg = {}
        red = {}
        tBc = {}
        c67 = {}
        g64 = {}
        sqc = {}
        tp = {}
        st7 = {}

        def q_mms(i):
            if i == 0:
                w0 = [(sS, sS.n), (sV, sV.n), (sG, sG.n)] + list(w_ready["q"])
            else:
                w0 = [(sV, qa[i - 1]), (sS, qev[i - 1])]
            for r in range(8):
                p.do("tensor", lambda e, i=i, r=r: e.matmul(
                    pQ[:], xnT_r[:, r, 128 * i:128 * (i + 1)],
                    wqT[:, 512 * r: 512 * (r + 1)],
                    start=(r == 0), stop=(r == 7)),
                     wait=w0 if r == 0 else [], inc=(sT, 1) if r == 7 else None)

        def q_evacs(i):
            w1 = [(sT, sT.n)]
            if i >= 1:
                w1 += [(sV, rotv[i - 1]), (sG, sqg[i - 1])]
            else:
                w1 += [(sV, sV.n), (sG, sG.n)]
            p.do("scalar", lambda e: e.activation(out=t1b[:], in_=pQ.ap()[:, 0:512:2],
                                                  func=AF.Copy, scale=0.125),
                 wait=w1, inc=(sS, 1))
            p.do("scalar", lambda e: e.activation(out=t2b[:], in_=pQ.ap()[:, 1:512:2],
                                                  func=AF.Copy, scale=0.125), inc=(sS, 1))
            qev[i] = sS.n

        def q_tail1(i):
            wkt = [(sS, qev[i])]
            if i >= 1:
                wkt.append((sV, tBc[i - 1]))
            p.do("vector", lambda e, i=i: e.tensor_scalar(
                ktmp[:], bmc[:, 8:16], tokc[:, i:i + 1], None, ALU.mult),
                 wait=wkt, inc=(sV, 1))
            emit_rot(i, g, q12,
                     sub_wait=(sT, tp[i - 1]) if i >= 1 else None)
            rotv[i] = sV.n
            wsq = [(sS, qev[i])]
            if i >= 1:
                wsq.append((sV, red[i - 1]))
            else:
                wsq.append((sT, sT.n))
            p.do("gpsimd", lambda e: e.tensor_tensor(out=qsq4[:, 0], in0=t1b_r3[:],
                                                     in1=t1b_r3[:], op=ALU.mult),
                 wait=wsq, inc=(sG, 1))
            p.do("gpsimd", lambda e: e.tensor_tensor(out=qsq4[:, 1], in0=t2b_r3[:],
                                                     in1=t2b_r3[:], op=ALU.mult),
                 inc=(sG, 1))
            sqg[i] = sG.n

        def q_tail2(i):
            wrd = [(sG, sqg[i])]
            if i >= 1:
                wrd += [(sS, sqc[i - 1]), (sV, tBc[i - 1])]
            p.do("vector", lambda e: e.tensor_reduce(qn2[:], qsq3[:], AX.X, ALU.add),
                 wait=wrd, inc=(sV, 1))
            red[i] = sV.n
            wbq = [(sV, red[i])]
            if i >= 1:
                wbq.append((sS, sqc[i - 1]))
            p.do("gpsimd", lambda e: e.tensor_tensor(out=bq2[:], in0=qn2[:], in1=bmc[:, 0:8],
                                                     op=ALU.mult),
                 wait=wbq, inc=(sG, 1))
            p.do("scalar", lambda e: e.activation(out=qn2[:], in_=bq2[:], func=AF.Sqrt),
                 wait=[(sG, sG.n)], inc=(sS, 1))
            p.do("scalar", lambda e: e.activation(out=dummy1[:], in_=cb_eps[0:1, :],
                                                  func=AF.Exp), inc=(sS, 1))
            sqc[i] = sS.n
            wtb = [(sS, sqc[i])]
            if i >= 1:
                wtb += [(sG, g64[i - 1]), (sV, c67[i - 1])]
            p.do("vector", lambda e: e.scalar_tensor_tensor(
                tB[:], qn2[:], -1.05, ktmp[:], ALU.mult, ALU.subtract),
                 wait=wtb, inc=(sV, 1))
            tBc[i] = sV.n
            wc64 = [(sV, tBc[i])]
            if i >= 1:
                wc64.append((sT, tp[i - 1]))
            p.do("gpsimd", lambda e: e.tensor_copy(q12_r3[:, :, 64:65], tB[:]),
                 wait=wc64, inc=(sG, 1))
            g64[i] = sG.n
            wc67 = [(sV, tBc[i]), (sG, g64[i])]
            if i >= 1:
                wc67.append((sT, tp[i - 1]))
            p.do("vector", lambda e: e.tensor_sub(q12_r3[:, :, 67:68], tB[:],
                                                  q12_r3[:, :, 64:65]),
                 wait=wc67, inc=(sV, 1))
            c67[i] = sV.n
            for h in range(8):
                w_ = [(sV, c67[i]), (sS, qev[i])] if h == 0 else []
                p.do("tensor", lambda e, h=h: e.transpose(
                    pQb[0:68, 128 * h:128 * (h + 1)], q12[:, 68 * h:68 * (h + 1)], ident[:]),
                     wait=w_, inc=(sT, 1) if h == 7 else None)
            tp[i] = sT.n

        def q_copy(i):
            w_ = [(sT, tp[i])]
            if i >= 1:
                w_.append((sT, st7[i - 1]))
            p.do("vector", lambda e: e.tensor_copy(q_aug[0:68, :], pQb.ap()[0:68, 0:1024]),
                 wait=w_, inc=(sV, 1))
            qa[i] = sV.n

        q_mms(0)
        q_evacs(0)
        q_tail1(0)
        q_tail2(0)
        q_copy(0)

        for i in range(NT):
            L = 128 * (i + 1)
            nu = (L + 511) // 512
            sS_exp = {}
            sG_mask = {}
            av_end = {}
            sV_ev = [sV.n]

            def emit_st(h, sS_exp=sS_exp, sG_mask=sG_mask, av_end=av_end, i=i, L=L, nu=nu):
                ab = attnT[(i * 8 + h) % 2]
                for u_i in range(nu):
                    c0 = 512 * u_i
                    w_ = min(512, L - c0)
                    u = next_unit()
                    for jj in range(w_ // 128):
                        j = (c0 // 128) + jj
                        wl = []
                        if u_i == 0 and jj == 0:
                            wl.append((sV, qa[i]))
                            if first[0]:
                                wl.append((dkc, dkc.n))
                                wl.append((dtap, dtap.n))
                                first[0] = False
                        if jj == 0 and u["free"]:
                            wl.append(u["free"])
                        p.do("tensor", lambda e, u=u, h=h, j=j, jj=jj: e.matmul(
                            u["t"][:, u["c"] + 128 * jj: u["c"] + 128 * (jj + 1)],
                            k_aug[0:68, N * h + 128 * j: N * h + 128 * (j + 1)],
                            q_aug[0:68, 128 * h:128 * (h + 1)],
                            start=True, stop=True), wait=wl,
                             inc=(sT, 1) if jj == w_ // 128 - 1 else None)
                    wl2 = [(sT, sT.n)]
                    if u_i == 0 and h >= 2 and (h - 2) in av_end:
                        wl2.append((sT, av_end[h - 2]))
                    if u_i == 0 and h == 1 and 7 in prev_av_end[0]:
                        wl2.append((sT, prev_av_end[0][7]))
                    p.do("scalar", lambda e, u=u, ab=ab, c0=c0, w_=w_: e.activation(
                        out=ab[:, c0:c0 + w_], in_=u["t"][:, u["c"]:u["c"] + w_], func=AF.Exp),
                         wait=wl2, inc=(sS, 1))
                    u["free"] = (sS, sS.n)
                sS_exp[h] = sS.n
                p.do("gpsimd", lambda e, ab=ab, L=L: e.tensor_mul(
                    ab[:, L - 128:L], ab[:, L - 128:L], mask_st[:]),
                     wait=[(sS, sS_exp[h])], inc=(sG, 1))
                sG_mask[h] = sG.n

            def emit_av(h, sS_exp=sS_exp, sG_mask=sG_mask, av_end=av_end,
                        sV_ev=sV_ev, i=i, L=L, nu=nu):
                ab = attnT[(i * 8 + h) % 2]
                for j in range(i + 1):
                    wl = [(sG, sG_mask[h]), (sS, sS_exp[h]), (sV, sV_ev[0])] if j == 0 else []
                    p.do("tensor", lambda e, j=j, h=h, i=i, ab=ab: e.matmul(
                        pOut[0:65, 128 * (h % 4):128 * (h % 4 + 1)],
                        v_aug.ap()[:, (j * HGN + h) * 65:(j * HGN + h) * 65 + 65],
                        ab[:, 128 * j:128 * (j + 1)],
                        start=(j == 0), stop=(j == i)), wait=wl,
                         inc=(sT, 1) if j == i else None)
                av_end[h] = sT.n
                if h % 4 == 3:
                    sT_av = sT.n
                    hg = h // 4
                    p0 = (h - 3) // 2
                    pOut_g = pOut.ap()[0:64, :].rearrange("p (pr a c) -> p pr a c", pr=2, a=2)
                    p.do("vector", lambda e, p0=p0, i=i, pOut_g=pOut_g: e.tensor_copy(
                        attnout_r[0:64, p0:p0 + 2, 128 * i:128 * (i + 1)], pOut_g[:, :, 0, :]),
                         wait=[(sT, sT_av)], inc=(sV, 1))
                    p.do("vector", lambda e, p0=p0, i=i, pOut_g=pOut_g: e.tensor_copy(
                        attnout_r[64:128, p0:p0 + 2, 128 * i:128 * (i + 1)], pOut_g[:, :, 1, :]),
                         inc=(sV, 1))
                    p.do("vector", lambda e, i=i, hg=hg: e.tensor_copy(
                        rs_stage.ap()[0:1, 1024 * i + 512 * hg:1024 * i + 512 * (hg + 1)],
                        pOut[64:65, 0:512]),
                         wait=[(dox01[0], dox01[0].n), (dox01[1], dox01[1].n)],
                         inc=(sV, 1))
                    sV_ev[0] = sV.n

            for h in range(8):
                emit_st(h)
                if h == 7:
                    st7[i] = sT.n
                if h == 0 and pend[0] is not None:
                    pend[0]()
                    pend[0] = None
                    sV_ev[0] = sV.n
                if i + 1 < NT:
                    if h == 1:
                        q_mms(i + 1)
                    elif h == 2:
                        q_evacs(i + 1)
                    elif h == 3:
                        q_tail1(i + 1)
                    elif h == 5:
                        q_tail2(i + 1)
                if h >= 1:
                    emit_av(h - 1)
            if i + 1 < NT:
                q_copy(i + 1)
            pend[0] = lambda emit_av=emit_av: emit_av(7)
            prev_av_end[0] = av_end
        if pend[0] is not None:
            pend[0]()
            pend[0] = None

    def emit_norm_wo(b, g):
        sV_n0 = sV.n
        nm = (NT + 7) // 8
        half = min(8, NT) * 1024
        p.do("scalar", lambda e: e.activation(out=rs_stage[0:1, 0:half],
                                              in_=rs_stage[0:1, 0:half], func=AF.Ln),
             wait=[(sV, sV.n), (dtap, dtap.n)], inc=(sS, 1))
        p.do("scalar", lambda e: e.activation(out=rs_stage[0:1, 0:half],
                                              in_=rs_stage[0:1, 0:half], func=AF.Exp,
                                              scale=-1.0), inc=(sS, 1), drain=True)
        sS_rsm = {0: sS.n}
        if nm > 1:
            p.do("scalar", lambda e: e.activation(out=rs_stage[0:1, half:NT * 1024],
                                                  in_=rs_stage[0:1, half:NT * 1024],
                                                  func=AF.Ln),
                 inc=(sS, 1), drain=True)
            p.do("scalar", lambda e: e.activation(out=rs_stage[0:1, half:NT * 1024],
                                                  in_=rs_stage[0:1, half:NT * 1024],
                                                  func=AF.Exp, scale=-1.0),
                 inc=(sS, 1), drain=True)
        sS_rsm[1] = sS.n
        doa_entry = [doa01[0].n, doa01[1].n]
        sV_mult = {}
        sV_m = {}
        rs5 = rs_stage.ap().rearrange("p (i g w q) -> p i g w q", i=NT, g=2, w=4)
        seq = 0
        for m in range(nm):
            for h in range(8):
                hg = h // 4
                w4 = h % 4
                rows = slice(0, 64) if h % 2 == 0 else slice(64, 128)
                pblk = h // 2
                pnorm = [pU1, pU2][seq % 2]
                wcols = min(1024, 128 * NT - 1024 * m)
                for c in range((wcols + 511) // 512):
                    i0 = 8 * m + 4 * c
                    nb = min(4, NT - i0)
                    rhs = rs5[0:1, i0:i0 + nb, hg, w4, :]
                    wl = []
                    if c == 0:
                        if seq % 8 < 2:
                            wl = [(sV, sV_n0), (sS, sS_rsm[m]), (sT, sT.n), (dtap, dtap.n)]
                        if seq >= 2:
                            wl.append((sV, sV_mult[seq - 2]))
                    p.do("tensor", lambda e, rhs=rhs, c=c, pnorm=pnorm, nb=nb: e.matmul(
                        pnorm[0:64, 512 * c:512 * c + 128 * nb], ones1[0:1, 0:64], rhs,
                        start=True, stop=True), wait=wl,
                         inc=(sT, 1) if 512 * c + 128 * nb >= wcols else None)
                p.do("vector", lambda e, rows=rows, pblk=pblk, m=m, pnorm=pnorm, wcols=wcols: e.tensor_mul(
                    attnout_r[rows, pblk, 1024 * m:1024 * m + wcols],
                    attnout_r[rows, pblk, 1024 * m:1024 * m + wcols], pnorm[0:64, 0:wcols]),
                     wait=[(sT, sT.n)], inc=(sV, 1), drain=(seq == 0))
                sV_mult[seq] = sV.n
                seq += 1
            sV_m[m] = sV.n
        sV_norm = sV.n
        if stop_after == "normmul":
            return
        wev_hist = {}
        for i in range(NT):
            buf = i % 2
            tgts = [pU0[:, 0:512], pU0[:, 512:1024]] if i % 2 == 0 else \
                   [pOut[:, 0:512], pQ[:, 0:512]]
            for pr in range(4):
                for c in range(2):
                    wl = []
                    if pr == 0 and c == 0:
                        wl.append((sV, sV_m[min(i // 8, nm - 1)]))
                        wl += list(w_ready["wo"]) if i == 0 else []
                        if i >= 2:
                            wl.append((sS, wev_hist[i - 2]))
                    p.do("tensor", lambda e, i=i, pr=pr, c=c, tgts=tgts: e.matmul(
                        tgts[c],
                        attnout_r[:, pr, 128 * i:128 * (i + 1)],
                        woT.ap()[:, D * pr + 512 * c:D * pr + 512 * (c + 1)],
                        start=(pr == 0), stop=(pr == 3),
                        skip_group_check=True), wait=wl,
                         inc=(sT, 1) if (pr == 3 and c == 1) else None)
            wl = [(sT, sT.n), (doa01[buf], doa01[buf].n)]
            p.do("scalar", lambda e, buf=buf, tgts=tgts: e.activation(
                out=out_sb[buf][:, 0:512], in_=tgts[0], func=AF.Copy),
                 wait=wl, inc=(sS, 1))
            p.do("scalar", lambda e, buf=buf, tgts=tgts: e.activation(
                out=out_sb[buf][:, 512:1024], in_=tgts[1], func=AF.Copy),
                 inc=(sS, 1))
            wev_hist[i] = sS.n
            row0 = b * N + 128 * i
            if accum_mode == "add":
                p.do("gpsimd", lambda e, row0=row0, buf=buf: e.dma_start(
                    out=dout_f[row0:row0 + 128, :], in_=out_sb[buf][:], accum_op=ALU.add),
                     wait=[(sS, sS.n), (dox01[0], dox01[0].n), (dox01[1], dox01[1].n),
                           (doa01[0], doa_entry[0]), (doa01[1], doa_entry[1])],
                     inc=(doa01[buf], 16))
            else:
                p.do("gpsimd", lambda e, row0=row0, buf=buf: e.dma_start(
                    out=dout_f[row0:row0 + 128, :], in_=out_sb[buf][:]),
                     wait=[(sS, sS.n), (dox01[0], dox01[0].n), (dox01[1], dox01[1].n),
                           (doa01[0], doa_entry[0]), (doa01[1], doa_entry[1])],
                     inc=(doa01[buf], 16))

    STOP = ["prep", "ln", "ffn", "attnprep", "wkv", "kv", "attn", "norm", "all"].index({"wobuild": "norm", "normmul": "norm"}.get(stop_after, stop_after))
    sV_ln = [0]
    for b in range(NB if STOP >= 8 else 1):
        if STOP < 1: break
        load_expert(b, 0, [(sT, sT.n), (sS, sS.n), (sV, sV.n)])
        load_expert(b, 1, [])
        emit_ln(b)
        sV_ln[0] = sV.n
        sT_ln = sT.n
        tap(f"xnT_b{b}", xnT.ap(), [(sV, sV.n)])
        if STOP < 2: break
        build_wqT(0, sT_ln, b)
        sT_wq = sT.n
        build_wkvT(0, sT_wq, b)
        sT_bld = sT.n
        emit_ffn(b)
        tap(f"acc_b{b}", acc.ap(), [(sV, sV.n)])
        tap(f"G_b{b}", G_bf.ap(), [(sV, sV.n)])
        if STOP < 3: break
        emit_attn_prep(b)
        for g in range(NG if STOP >= 8 else 1):
            if STOP < 4: break
            if g == 1:
                build_wqT(g, sT_bld, b)
                build_wkvT(g, sT.n, b)
            if STOP < 5: break
            emit_kv(b, g)
            tap(f"kaug_b{b}g{g}", k_aug.ap()[0:66, :], [(sV, sV.n), (dkc, dkc.n)])
            tap(f"vaug_b{b}g{g}", v_aug.ap(), [(sS, sS.n)])
            if STOP < 6: break
            build_woT(g, b)
            emit_attn(b, g)
            tap(f"qaug_b{b}g{g}", q_aug.ap()[0:66, :], [(sV, sV.n), (sS, sS.n)])
            tap(f"attnout_b{b}g{g}", attnout.ap(), [(sV, sV.n)])
            tap(f"rs_b{b}g{g}", rs_stage.ap(), [(sV, sV.n)])
            if STOP < 7: break
            if stop_after == "wobuild":
                break
            emit_norm_wo(b, g)
            tap(f"attnoutn_b{b}g{g}", attnout.ap(), [(sV, sV.n)])

    p.do("gpsimd", lambda e: e.nop(),
         wait=[(doa01[0], doa01[0].n), (doa01[1], doa01[1].n),
               (dox01[0], dox01[0].n), (dox01[1], dox01[1].n)] +
              ([(dtap, dtap.n)] if tap_t else []))
    p.emit()
    ctx.close()
    return nc


_BUILT = {}


def _get_nc(n_tiles, taps=()):
    key = (n_tiles, tuple(taps))
    if key not in _BUILT:
        _BUILT[key] = build(n_tiles, taps)
    return _BUILT[key]


def kernel(L0_attn_x, L0_attn_Wqkv, L0_attn_Wo, L0_attn_log_xi,
           L0_attn_pi_gate_logit, L0_attn_e_gate_logit,
           L0_ffn_router_w, L0_ffn_router_b, L0_ffn_expert_w, L0_ffn_expert_b):
    x = np.ascontiguousarray(np.asarray(L0_attn_x, dtype=np.float32))
    B, Nseq, Dm = x.shape
    nt = Nseq // 128
    nc = _get_nc(nt)
    common = {
        "wqkv": np.ascontiguousarray(np.asarray(L0_attn_Wqkv, dtype=np.float32)),
        "wo": np.ascontiguousarray(np.asarray(L0_attn_Wo, dtype=np.float32)),
        "log_xi": np.ascontiguousarray(np.asarray(L0_attn_log_xi, dtype=np.float32)),
        "pi_gate": np.ascontiguousarray(np.asarray(L0_attn_pi_gate_logit, dtype=np.float32)),
        "e_gate": np.ascontiguousarray(np.asarray(L0_attn_e_gate_logit, dtype=np.float32)),
        "router_w": np.ascontiguousarray(np.asarray(L0_ffn_router_w, dtype=np.float32)),
        "router_b": np.ascontiguousarray(np.asarray(L0_ffn_router_b, dtype=np.float32)),
        "expert_w": np.ascontiguousarray(np.asarray(L0_ffn_expert_w, dtype=np.float32)),
        "expert_b": np.ascontiguousarray(np.asarray(L0_ffn_expert_b, dtype=np.float32)),
    }
    ncr = B // NB
    in_maps = []
    for c in range(ncr):
        m = dict(common)
        m["x"] = np.ascontiguousarray(x[NB * c:NB * (c + 1)])
        in_maps.append(m)
    res = run_bass_kernel_spmd(nc, in_maps, core_ids=list(range(ncr)))
    out = np.concatenate([res.results[c]["out"] for c in range(ncr)], axis=0)
    return out.astype(np.float32)


# version 26 (final)
# speedup vs baseline: 1.0046x; 1.0046x over previous
import numpy as np
from contextlib import ExitStack

import concourse.bass as bass
import concourse.mybir as mybir
from concourse.bass_utils import run_bass_kernel_spmd

F32 = mybir.dt.float32
BF16 = mybir.dt.bfloat16
I32 = mybir.dt.int32
U32 = mybir.dt.uint32
AF = mybir.ActivationFunctionType
ALU = mybir.AluOpType
AX = mybir.AxisListType

D = 1024
H = 16
DH = 64
HGN = 8
NG = 2
NE = 8
NB = 2
NCORES = 8
LN_EPS = 1e-5
LNPI = float(np.log(np.pi))
LN2PI = float(np.log(2.0 * np.pi))
PI = float(np.pi)


class Sem:
    __slots__ = ("h", "n", "name")


class Prog:
    ENGS = ("sync", "vector", "scalar", "tensor", "gpsimd")

    def __init__(self, nc, ctx):
        self.nc = nc
        self.ctx = ctx
        self.steps = {e: [] for e in self.ENGS}

    def sem(self, name):
        s = Sem()
        s.h = self.ctx.enter_context(self.nc.semaphore(name))
        s.n = 0
        s.name = name
        return s

    def do(self, eng, fn, wait=(), inc=None, drain=False):
        w = [(s.h, int(v)) for (s, v) in wait if v > 0]
        incs = []
        if inc is not None:
            if not isinstance(inc, list):
                inc = [inc]
            for (s, k) in inc:
                s.n += k
                incs.append((s.h, k))
        self.steps[eng].append((fn, w, incs, drain))

    def emit(self):
        blk = self.ctx.enter_context(self.nc.Block())

        def mk(name):
            def body(eng):
                for fn, waits, incs, drain in self.steps[name]:
                    for (h, v) in waits:
                        eng.wait_ge(h, v)
                    if drain:
                        eng.drain()
                    inst = fn(eng)
                    for (h, k) in incs:
                        inst = inst.then_inc(h, k)
            return body

        blk.sync(mk("sync"))
        blk.vector(mk("vector"))
        blk.scalar(mk("scalar"))
        blk.tensor(mk("tensor"))
        blk.gpsimd(mk("gpsimd"))


def build(n_tiles=16, taps=(), stop_after='all', accum_mode='add'):
    NT = n_tiles
    N = 128 * NT
    nc = bass.Bass("TRN2", target_bir_lowering=False)
    ctx = ExitStack()
    p = Prog(nc, ctx)

    dx = nc.dram_tensor("x", [NB, N, D], F32, kind="ExternalInput").ap()
    dwqkv = nc.dram_tensor("wqkv", [3 * D, D], F32, kind="ExternalInput").ap()
    dwo = nc.dram_tensor("wo", [D, D], F32, kind="ExternalInput").ap()
    dlxi = nc.dram_tensor("log_xi", [H], F32, kind="ExternalInput").ap()
    dpgl = nc.dram_tensor("pi_gate", [H], F32, kind="ExternalInput").ap()
    degl = nc.dram_tensor("e_gate", [H], F32, kind="ExternalInput").ap()
    drw = nc.dram_tensor("router_w", [D, NE], F32, kind="ExternalInput").ap()
    drb = nc.dram_tensor("router_b", [NE], F32, kind="ExternalInput").ap()
    dew = nc.dram_tensor("expert_w", [NE, D, D], F32, kind="ExternalInput").ap()
    dscr = nc.dram_tensor("wex_scr", [NE, 128, 8 * D], BF16, kind="Internal").ap()
    dwscr = nc.dram_tensor("wts_scr", [NG, 128, 16384], BF16, kind="Internal").ap()
    deb = nc.dram_tensor("expert_b", [NE, D], F32, kind="ExternalInput").ap()
    dout = nc.dram_tensor("out", [NB, N, D], F32, kind="ExternalOutput").ap()
    dx_f = dx.rearrange("b n d -> (b n) d")
    dout_f = dout.rearrange("b n d -> (b n) d")

    tap_t = {}
    for tname, tshape, tdt in taps:
        tap_t[tname] = nc.dram_tensor("tap_" + tname, tshape, tdt,
                                      kind="ExternalOutput").ap()

    AL = 64
    left0 = (nc._sbuf_addr_for_side("left") + 63) // 64 * 64
    cur = [0]
    offs = {}

    def place(name, shape, dtype, at=None):
        nbytes = int(np.prod(shape[1:])) * mybir.dt.size(dtype)
        if at is None:
            off = cur[0]
            cur[0] = off + ((nbytes + AL - 1) // AL) * AL
        else:
            off = at
        offs[name] = off
        return nc.alloc_sbuf_tensor_at(name, list(shape), dtype, offset=left0 + off)

    ident = place("ident", [128, 128], BF16)
    mask_st = place("mask_st", [128, 128], BF16)
    iota8 = place("iota8", [128, NE], F32)
    ones1 = place("ones1", [1, 128], BF16)
    rw_bf = place("rw_bf", [128, 8 * NE], BF16)
    rb_row = place("rb_row", [1, NE], BF16)
    lxi = place("lxi", [H, 1], F32)
    pgl = place("pgl", [H, 1], F32)
    egl = place("egl", [H, 1], F32)
    eg_s = place("eg_s", [H, 1], F32)
    xinv = place("xinv", [H, 1], F32)
    pig = place("pig", [H, 1], F32)
    c_vec = place("c_vec", [H, 1], F32)
    ones_f = place("ones_f", [1, 128], F32)
    sin_tab = place("sin_tab", [128, NT * 512], BF16)
    cos_tab = place("cos_tab", [128, NT * 512], BF16)
    wqT = place("wqT", [128, 8 * 512], BF16)
    xnT = place("xnT", [128, 8 * N], BF16)
    attnout = place("attnout", [128, 4 * N], BF16)
    regR = cur[0]
    sel = place("sel", [16, 512], F32, at=regR)
    kc_all = place("kc_all", [H, N], F32, at=regR)
    kciota = place("kciota", [16, N], F32, at=regR + 8192)
    kc_hibf = place("kc_hibf", [16, N], BF16, at=regR + 8192)
    eb_bf = place("eb_bf", [NE, D], BF16, at=regR + 16384)
    io2 = place("io2", [128, 128], F32, at=regR + 18432)
    utmp = place("utmp", [128, 512], F32, at=regR + 18944)
    itmp = place("itmp", [128, 512], I32, at=regR + 20992)
    ftmp = place("ftmp", [128, 512], F32, at=regR + 23040)
    freq_rep = place("freq_rep", [128, 512], F32, at=offs["attnout"] + 8192)
    gtmp = place("gtmp", [128, 512], F32, at=regR + 27136)
    f_iota = place("f_iota", [1, 512], F32, at=regR + 27136)
    freq_row = place("freq_row", [1, 512], F32, at=regR + 28160)
    x_sb = [place("x0", [128, D], F32, at=regR + 18944),
            place("x1", [128, D], F32, at=regR + 23040)]
    rs_stage = place("rs_stage", [1, NT * 1024], BF16, at=regR)
    cur[0] = regR + max(NT * 1024 * 2, 32768)

    regA = cur[0]
    acc = place("acc", [128, NT * D], BF16, at=regA)
    k_aug = place("k_aug", [68, HGN * N], BF16, at=regA)
    cur[0] = regA + NT * D * 2

    regB = cur[0]
    wex = [place("wex0", [128, 8 * D], BF16, at=regB),
           place("wex1", [128, 8 * D], BF16, at=regB + 8 * D * 2)]
    o = regB
    v_aug = place("v_aug", [128, NT * HGN * 65], BF16, at=o); o += NT * HGN * 65 * 2
    attnT = [place("attnT0", [128, 2048], BF16, at=o),
             place("attnT1", [128, 2048], BF16, at=o + 4096)]
    o += 8192
    q_aug = place("q_aug", [68, HGN * 128], BF16, at=o); o += HGN * 128 * 2
    q12 = place("q12", [128, 544], BF16, at=o); o += 1088
    r12 = place("r12", [128, 512], BF16, at=o); o += 1024
    t1b = place("t1b", [128, 256], BF16, at=o)
    tsq = place("tsq", [128, 512], BF16, at=o); o += 512
    t2b = place("t2b", [128, 256], BF16, at=o); o += 512
    ta = place("ta", [128, 256], BF16, at=o); o += 512
    tb = place("tb", [128, 256], BF16, at=o); o += 512
    tc = place("tc", [128, 256], BF16, at=o); o += 512
    td = place("td", [128, 256], BF16, at=o); o += 512
    cur[0] = max(o, regB + 2 * 8 * D * 2)

    regC = cur[0]
    wkvT = place("wkvT", [128, 8 * D], BF16, at=regC)
    woT = place("woT", [128, 4 * D], BF16, at=regC)
    out_sb = [place("outsb0", [128, D], F32, at=regC + 4 * D * 2),
              place("outsb1", [128, D], F32, at=regC + 4 * D * 2 + D * 4)]
    cur[0] = regC + 8 * D * 2

    regD = cur[0]
    stage = [place("stage0", [128, D], BF16, at=regD),
             place("stage1", [128, D], BF16, at=regD + D * 2)]
    xn_tok = [place("xntok0", [128, D], BF16, at=regD),
              place("xntok1", [128, D], BF16, at=regD + D * 2)]
    cur[0] = regD + 2 * D * 2

    lnstats = place("lnstats", [128, 12], F32)
    lnmv = place("lnmv", [128, 2], F32)
    lnv = place("lnv", [128, 1], F32)
    rstd = place("rstd", [128, 1], F32)
    nmr = place("nmr", [128, 1], F32)
    smalls0 = cur[0]
    cur[0] += 96
    mstats = place("mstats", [128, 1], F32, at=smalls0)
    srow_bf = place("srow_bf", [1, 8], BF16, at=smalls0 + 32)
    m2b = place("m2b", [8, 1], BF16, at=smalls0 + 64)
    utmp2 = place("utmp2", [128, 512], F32, at=offs["attnout"])
    itmp2 = place("itmp2", [128, 512], I32, at=offs["attnout"] + 2048)
    ftmp2 = place("ftmp2", [128, 512], F32, at=offs["attnout"] + 4096)
    gtmp2 = place("gtmp2", [128, 512], F32, at=offs["attnout"] + 6144)
    kmax = place("kmax", [128, 8], F32)
    ktmp = place("ktmp", [128, 8], F32)
    km_b = place("km_b", [128, 8], BF16)
    crow_f = place("crow_f", [1, H], F32)
    qn2 = place("qn2", [128, 8], F32)
    bq2 = place("bq2", [128, 8], F32)
    tB = place("tB", [128, 8], F32)
    bmc = place("bmc", [128, 16], F32)
    dummy1 = place("dummy1", [1, 1], F32)
    tokc = place("tokc", [128, NT], F32)
    idf16 = place("idf16", [16, 16], F32)
    pexp = place("pexp", [128, NE], F32, at=regR + 29184)
    sume = place("sume", [128, 1], F32, at=regR + 29248)
    gate = place("gate", [128, 1], F32, at=regR + 29312)
    mx8 = place("mx8", [128, NE], F32, at=regR + 29376)
    idx8 = place("idx8", [128, NE], U32, at=regR + 29440)
    idxf = place("idxf", [128, 1], F32, at=regR + 29504)
    G_bf = place("G_bf", [128, NT * NE], BF16, at=regR + 28160)
    gt_sb = [place("gt0", [8, 128], BF16, at=regR + 28160 + 512),
             place("gt1", [8, 128], BF16, at=regR + 28160 + 768)]
    cb_eps = place("cb_eps", [128, 1], F32)
    cb_ph = place("cb_ph", [128, 1], F32)
    cb_fr = place("cb_fr", [128, 1], F32)

    assert cur[0] <= 212200, f"SBUF over budget: {cur[0]}"
    nc.alloc_sbuf_tensor("arena_reserve", [128, (cur[0] + 3) // 4], I32)

    pU0 = nc.alloc_psum_tensor("pU0", [128, 1024], F32)
    pU1 = nc.alloc_psum_tensor("pU1", [128, 1024], F32)
    pU2 = nc.alloc_psum_tensor("pU2", [128, 1024], F32)
    pOut = nc.alloc_psum_tensor("pOut", [128, 512], F32)
    pQ = nc.alloc_psum_tensor("pQ", [128, 512], F32)
    pU2b = pU2.bitcast(BF16)
    pOutb = pOut.bitcast(BF16)
    pQb = pQ.bitcast(BF16)
    units = []
    for t_, c0 in ((pU0, 0), (pU0, 512), (pU1, 0), (pU1, 512), (pU2, 0), (pU2, 512)):
        units.append({"t": t_, "c": c0, "free": None})

    sT = p.sem("sT")
    sV = p.sem("sV")
    sS = p.sem("sS")
    sG = p.sem("sG")
    dX = p.sem("dX")
    dx01 = [p.sem("dx0"), p.sem("dx1")]
    dw01 = [p.sem("dw0"), p.sem("dw1")]
    dst01 = [p.sem("dst0"), p.sem("dst1")]
    dkc = p.sem("dkc")
    dwb = p.sem("dwb")
    dox01 = [p.sem("dox0"), p.sem("dox1")]
    doa01 = [p.sem("doa0"), p.sem("doa1")]
    dtap = p.sem("dtap")

    def tap(name, src_ap, wait):
        if name in tap_t:
            d = tap_t[name]
            p.do("scalar", lambda e, d=d, s=src_ap: e.dma_start(out=d, in_=s),
                 wait=wait, inc=(dtap, 16))

    xnT_r = xnT.ap().rearrange("p (r c) -> p r c", r=8)
    k_aug_r = k_aug.ap().rearrange("p (h c) -> p h c", h=HGN)
    v_aug_r = v_aug.ap().rearrange("p (j h c) -> p j h c", j=NT, h=HGN)
    attnout_r = attnout.ap().rearrange("p (b c) -> p b c", b=4)
    pQb_r = pQb.ap().rearrange("p (h c) -> p h c", h=8)
    wqT_r = wqT.ap().rearrange("p (r c) -> p r c", r=8)
    wkvT_r = wkvT.ap().rearrange("p (r c) -> p r c", r=8)
    woT_r = woT.ap().rearrange("p (pr c) -> p pr c", pr=4)

    p.do("sync", lambda e: e.dma_start(out=lxi[:], in_=dlxi.rearrange("(h o) -> h o", o=1)), inc=(dX, 16))
    p.do("sync", lambda e: e.dma_start(out=pgl[:], in_=dpgl.rearrange("(h o) -> h o", o=1)), inc=(dX, 16))
    p.do("sync", lambda e: e.dma_start(out=egl[:], in_=degl.rearrange("(h o) -> h o", o=1)), inc=(dX, 16))
    dX_head = dX.n
    p.do("gpsimd", lambda e: e.dma_start(out=rw_bf.ap().rearrange("p (c e) -> p c e", c=8),
                                         in_=drw.rearrange("(c p) e -> p c e", p=128)), inc=(dkc, 16))
    p.do("gpsimd", lambda e: e.dma_start(out=rb_row.ap().rearrange("p (o e) -> p o e", o=1),
                                         in_=drb.rearrange("(o e) -> o e", o=1)), inc=(dkc, 16))
    dkc_w = dkc.n

    p.do("gpsimd", lambda e: e.iota(io2[:], pattern=[[1, 128]], base=0, channel_multiplier=-1,
                                    allow_small_or_imprecise_dtypes=True), inc=(sG, 1))
    p.do("gpsimd", lambda e: e.iota(iota8[:], pattern=[[1, NE]], base=0, channel_multiplier=0,
                                    allow_small_or_imprecise_dtypes=True), inc=(sG, 1))
    p.do("gpsimd", lambda e: e.iota(f_iota[:], pattern=[[0, 16], [1, 32]], base=0, channel_multiplier=0,
                                    allow_small_or_imprecise_dtypes=True), inc=(sG, 1))
    p.do("gpsimd", lambda e: e.iota(sel[:], pattern=[[1, 512]], base=0, channel_multiplier=-32,
                                    allow_small_or_imprecise_dtypes=True), inc=(sG, 1))
    sG_iotas = sG.n

    p.do("vector", lambda e: e.tensor_scalar(ident[:], io2[:], 0.0, None, ALU.is_equal),
         wait=[(sG, 1)], inc=(sV, 1))
    p.do("vector", lambda e: e.tensor_scalar(mask_st[:], io2[:], 0.0, None, ALU.is_ge), inc=(sV, 1))
    p.do("vector", lambda e: e.tensor_scalar(idf16[:], io2[0:16, 0:16], 0.0, None, ALU.is_equal),
         inc=(sV, 1))
    p.do("vector", lambda e: e.memset(ones1[:], 1.0), inc=(sV, 1))
    p.do("vector", lambda e: e.tensor_scalar(ftmp[0:16, :], sel[:], 0.0, None, ALU.is_ge),
         wait=[(sG, sG_iotas)], inc=(sV, 1))
    p.do("vector", lambda e: e.scalar_tensor_tensor(sel[:], sel[:], 32.0, ftmp[0:16, :],
                                                    ALU.is_lt, ALU.mult),
         inc=(sV, 1), drain=True)
    p.do("vector", lambda e: e.memset(cb_eps[:], LN_EPS), inc=(sV, 1))
    p.do("vector", lambda e: e.memset(cb_ph[:], PI / 2), inc=(sV, 1))
    p.do("vector", lambda e: e.memset(cb_fr[:], LNPI - LN2PI), inc=(sV, 1))
    p.do("vector", lambda e: e.memset(ones_f[:], 1.0), inc=(sV, 1))
    sV_consts = sV.n

    p.do("scalar", lambda e: e.activation(out=eg_s[:], in_=egl[:], func=AF.Sigmoid),
         wait=[(dX, dX_head)], inc=(sS, 1))
    p.do("scalar", lambda e: e.activation(out=xinv[:], in_=lxi[:], func=AF.Exp, scale=-1.0), inc=(sS, 1))
    p.do("scalar", lambda e: e.activation(out=pig[:], in_=pgl[:], func=AF.Sigmoid), inc=(sS, 1))
    sS_heads = sS.n
    p.do("vector", lambda e: e.tensor_mul(c_vec[:], eg_s[:], xinv[:]),
         wait=[(sS, sS_heads)], inc=(sV, 1))
    sV_kc = sV.n
    p.do("gpsimd", lambda e: e.iota(tokc[:], pattern=[[128, NT]], base=0,
                                    channel_multiplier=1,
                                    allow_small_or_imprecise_dtypes=True), inc=(sG, 1))
    p.do("tensor", lambda e: e.matmul(pOut[0:1, 0:16], c_vec[:], idf16[:],
                                      start=True, stop=True, skip_group_check=True),
         wait=[(sV, sV_kc)], inc=(sT, 1))
    p.do("scalar", lambda e: e.activation(out=crow_f[:], in_=pOut[0:1, 0:16], func=AF.Copy),
         wait=[(sT, sT.n)], inc=(sS, 1))
    sS_crow = sS.n

    p.do("tensor", lambda e: e.matmul(pQ[0:1, 0:512], pig[:], sel[:], start=True, stop=True),
         wait=[(sS, sS_heads), (sV, sV_consts)], inc=(sT, 1))
    sT_pig = sT.n
    p.do("scalar", lambda e: e.activation(out=ftmp[0:1, 0:512], in_=f_iota[:], func=AF.Exp,
                                          scale=-LNPI / 32.0, bias=cb_fr[0:1, :]),
         wait=[(sG, sG_iotas), (sV, sV_consts)], inc=(sS, 1))
    sS_pif = sS.n
    p.do("vector", lambda e: e.tensor_mul(freq_row[0:1, :], ftmp[0:1, 0:512], pQ[0:1, 0:512]),
         wait=[(sT, sT_pig), (sS, sS_pif)], inc=(sV, 1))
    p.do("tensor", lambda e: e.matmul(pU0[:, 0:512], ones_f[:], freq_row[:],
                                      start=True, stop=True),
         wait=[(sV, sV.n)], inc=(sT, 1))
    p.do("vector", lambda e: e.tensor_copy(freq_rep[:], pU0[:, 0:512]),
         wait=[(sT, sT.n)], inc=(sV, 1))
    sV_freq = sV.n


    def build_wT(src_ap_fn, n_src, width, evac_fn, extra_first_wait=(),
                 extra_tr_wait=()):
        nblk = width // 128
        for s in range(n_src):
            buf = s % 2
            w_load = [(sV, sV.n)] if s >= 2 else list(extra_first_wait)
            p.do("gpsimd", lambda e, s=s, buf=buf, fn=src_ap_fn, width=width: e.dma_start(
                out=stage[buf][:, 0:width], in_=fn(s)),
                 wait=w_load, inc=(dst01[buf], 16))
            for r in range(nblk):
                w_ = [(dst01[buf], dst01[buf].n)] if r == 0 else []
                if r == 0:
                    w_.append((sV, sV.n))
                    if s == 0:
                        w_ += list(extra_tr_wait)
                p.do("tensor", lambda e, buf=buf, r=r: e.transpose(
                    pU2b[:, 128 * r:128 * (r + 1)], stage[buf][:, 128 * r:128 * (r + 1)], ident[:]),
                     wait=w_, inc=(sT, 1) if r == nblk - 1 else None)
            p.do("vector", lambda e, s=s, fn=evac_fn: e.tensor_copy(fn(s), pU2b[:, 0:width]),
                 wait=[(sT, sT.n)], inc=(sV, 1))



    def emit_tab(i, first_wait=()):
        p.do("vector", lambda e, i=i: e.tensor_scalar(utmp2[:], freq_rep[:],
                                                      tokc[:, i:i + 1], None, ALU.mult),
             wait=list(first_wait) + [(sS, sS.n), (sV, sV.n)], inc=(sV, 1))
        p.do("vector", lambda e: e.tensor_copy(itmp2[:], utmp2[:]),
             wait=[(sV, sV.n)], inc=(sV, 1))
        p.do("vector", lambda e: e.tensor_copy(ftmp2[:], itmp2[:]),
             wait=[(sV, sV.n)], inc=(sV, 1))
        p.do("vector", lambda e: e.tensor_sub(ftmp2[:], utmp2[:], ftmp2[:]),
             wait=[(sV, sV.n)], inc=(sV, 1))
        p.do("vector", lambda e: e.tensor_scalar(gtmp2[:], ftmp2[:], 0.5, None, ALU.is_gt),
             wait=[(sV, sV.n)], inc=(sV, 1))
        p.do("vector", lambda e: e.tensor_sub(ftmp2[:], ftmp2[:], gtmp2[:]),
             wait=[(sV, sV.n)], inc=(sV, 1))
        p.do("scalar", lambda e, i=i: e.activation(out=sin_tab[:, 512 * i:512 * (i + 1)],
                                                   in_=ftmp2[:], func=AF.Sin, scale=2 * PI),
             wait=[(sV, sV.n)], inc=(sS, 1))
        p.do("vector", lambda e: e.tensor_scalar(itmp2[:], utmp2[:], 0.25, None, ALU.add),
             wait=[(sS, sS.n)], inc=(sV, 1))
        p.do("vector", lambda e: e.tensor_copy(ftmp2[:], itmp2[:]),
             wait=[(sV, sV.n)], inc=(sV, 1))
        p.do("vector", lambda e: e.tensor_sub(ftmp2[:], utmp2[:], ftmp2[:]),
             wait=[(sV, sV.n)], inc=(sV, 1))
        p.do("vector", lambda e: e.tensor_scalar(gtmp2[:], ftmp2[:], 0.25, None, ALU.is_gt),
             wait=[(sV, sV.n)], inc=(sV, 1))
        p.do("vector", lambda e: e.tensor_sub(ftmp2[:], ftmp2[:], gtmp2[:]),
             wait=[(sV, sV.n)], inc=(sV, 1))
        p.do("scalar", lambda e, i=i: e.activation(out=cos_tab[:, 512 * i:512 * (i + 1)],
                                                   in_=ftmp2[:], func=AF.Sin, scale=2 * PI,
                                                   bias=cb_ph[:]),
             wait=[(sV, sV.n)], inc=(sS, 1))

    wex_ready = {}
    echo_cnt = {}

    def load_expert(b, e_i, extra_wait):
        buf = e_i % 2
        if b == 0:
            p.do("gpsimd", lambda e, e_i=e_i, buf=buf: e.dma_start(
                out=wex[buf].ap().rearrange("p (c d) -> p c d", c=8),
                in_=dew[e_i].rearrange("(c p) d -> p c d", p=128)),
                 wait=extra_wait, inc=(dw01[buf], 16))
            wex_ready[e_i] = (dw01[buf], dw01[buf].n)
            p.do("sync", lambda e, e_i=e_i, buf=buf: e.dma_start(
                out=dscr[e_i], in_=wex[buf][:]),
                 wait=[(dw01[buf], dw01[buf].n)], inc=(dwb, 16))
            echo_cnt[e_i] = dwb.n
        else:
            p.do("sync", lambda e, e_i=e_i, buf=buf: e.dma_start(
                out=wex[buf][:], in_=dscr[e_i]),
                 wait=list(extra_wait) + [(dwb, dwb.n)], inc=(dwb, 16))
            wex_ready[e_i] = (dwb, dwb.n)

    def emit_ln(b):
        for i in range(NT):
            buf = i % 2
            row0 = b * N + 128 * i
            w_ = [(sS, sS.n)]
            if i < 2:
                w_ += [(sT, sT.n), (sV, sV.n), (dtap, dtap.n)]
            p.do("sync", lambda e, row0=row0, buf=buf: e.dma_start(
                out=x_sb[buf][:], in_=dx_f[row0:row0 + 128, :]),
                 wait=w_, inc=(dx01[buf], 16))
            p.do("vector", lambda e, buf=buf: e.bn_stats(lnstats[:, 0:6], x_sb[buf][:, 0:512]),
                 wait=[(dx01[buf], dx01[buf].n), (sS, sS.n)], inc=(sV, 1))
            p.do("vector", lambda e, buf=buf: e.bn_stats(lnstats[:, 6:12], x_sb[buf][:, 512:1024]),
                 inc=(sV, 1))
            p.do("vector", lambda e: e.bn_aggr(lnmv[:], lnstats[:]), inc=(sV, 1), drain=True)
            p.do("scalar", lambda e: e.activation(out=lnv[:], in_=lnmv[:, 1:2], func=AF.Ln, bias=cb_eps[:]),
                 wait=[(sV, sV.n)], inc=(sS, 1))
            p.do("scalar", lambda e: e.activation(out=rstd[:], in_=lnv[:], func=AF.Exp, scale=-0.5),
                 inc=(sS, 1), drain=True)
            p.do("vector", lambda e: e.tensor_scalar(nmr[:], lnmv[:, 0:1], rstd[:], -1.0,
                                                     ALU.mult, ALU.mult),
                 wait=[(sS, sS.n)], inc=(sV, 1), drain=True)
            p.do("scalar", lambda e, buf=buf: e.activation(out=xn_tok[buf][:], in_=x_sb[buf][:],
                                                           func=AF.Identity, scale=rstd[:], bias=nmr[:]),
                 wait=[(sV, sV.n)], inc=(sS, 1), drain=True)
            for r in range(8):
                w_ = [(sS, sS.n), (sV, sV.n)] if r == 0 else []
                p.do("tensor", lambda e, buf=buf, r=r: e.transpose(
                    pQb[:, 128 * r:128 * (r + 1)], xn_tok[buf][:, 128 * r:128 * (r + 1)], ident[:]),
                     wait=w_, inc=(sT, 1) if r == 7 else None)
            p.do("vector", lambda e, i=i: e.tensor_copy(
                xnT_r[:, :, 128 * i:128 * (i + 1)],
                pQb.ap().rearrange("p (r c) -> p r c", r=8)),
                 wait=[(sT, sT.n)], inc=(sV, 1))

    def emit_ffn(b):
        p.do("gpsimd", lambda e: e.dma_start(out=eb_bf[:], in_=deb),
             wait=[(sT, sT.n), (sV, sV.n), (dtap, dtap.n), (dkc, dkc.n)], inc=(dkc, 16))
        dkc_eb = dkc.n
        bev_hist = {}
        gtev_hist = {}
        pendB = [None]
        for i in range(NT):
            PL = [pOut, pQ][i % 2]
            PLb = [pOutb, pQb][i % 2]
            for r in range(8):
                w_ = [(sV, sV.n), (dkc, dkc_w)] if i == 0 and r == 0 else []
                if r == 0 and i >= 2:
                    w_.append((sV, gtev_hist[i - 2]))
                p.do("tensor", lambda e, i=i, r=r, PL=PL: e.matmul(
                    PL[:, 0:NE], xnT_r[:, r, 128 * i:128 * (i + 1)], rw_bf[:, NE * r:NE * (r + 1)],
                    start=(r == 0), stop=False), wait=w_)
            p.do("tensor", lambda e, PL=PL: e.matmul(PL[:, 0:NE], ones1[:], rb_row[:],
                                                     start=False, stop=True), inc=(sT, 1))
            sT_lg = sT.n
            if pendB[0] is not None:
                pendB[0]()
                pendB[0] = None
            p.do("vector", lambda e, PL=PL: e.tensor_reduce(mstats[:, 0:1], PL[:, 0:NE], AX.X,
                                                            ALU.max, negate=True),
                 wait=[(sT, sT_lg)], inc=(sV, 1), drain=True)
            p.do("scalar", lambda e, PL=PL: e.activation(out=pexp[:], in_=PL[:, 0:NE], func=AF.Exp,
                                                         bias=mstats[:, 0:1], accum_out=sume[:]),
                 wait=[(sV, sV.n), (sT, sT_lg)], inc=(sS, 1))
            p.do("vector", lambda e: e.reciprocal(gate[:], sume[:]),
                 wait=[(sS, sS.n)], inc=(sV, 1))
            p.do("vector", lambda e: e.max(mx8[:], pexp[:]), inc=(sV, 1), drain=True)
            p.do("vector", lambda e: e.max_index(idx8[:], mx8[:], pexp[:]), inc=(sV, 1), drain=True)
            p.do("vector", lambda e: e.tensor_copy(idxf[:], idx8[:, 0:1]), inc=(sV, 1), drain=True)
            p.do("vector", lambda e, i=i: e.tensor_scalar(G_bf[:, NE * i:NE * (i + 1)], iota8[:],
                                                          idxf[:], gate[:], ALU.is_equal, ALU.mult),
                 inc=(sV, 1), drain=True)
            sV_g = sV.n

            def emit_B(i=i, sV_g=sV_g, PLb=PLb):
                gbuf = i % 2
                p.do("tensor", lambda e, i=i, PLb=PLb: e.transpose(
                    PLb[0:NE, 896:1024], G_bf[:, NE * i:NE * (i + 1)], ident[:]),
                     wait=[(sV, sV_g)], inc=(sT, 1))
                p.do("vector", lambda e, gbuf=gbuf, PLb=PLb: e.tensor_copy(
                    gt_sb[gbuf][:], PLb[0:NE, 896:1024]),
                     wait=[(sT, sT.n)], inc=(sV, 1))
                gtev_hist[i] = sV.n
                pB = [pU0, pU1][i % 2]
                for c in range(2):
                    w_ = [(sV, sV.n), (dkc, dkc_eb)] if c == 0 else []
                    if c == 0 and i >= 2:
                        w_.append((sS, bev_hist[i - 2]))
                    p.do("tensor", lambda e, gbuf=gbuf, c=c, pB=pB: e.matmul(
                        pB[:, 512 * c:512 * (c + 1)], gt_sb[gbuf][:],
                        eb_bf[:, 512 * c:512 * (c + 1)],
                        start=True, stop=True), wait=w_, inc=(sT, 1) if c == 1 else None)
                p.do("scalar", lambda e, i=i, pB=pB: e.activation(out=acc[:, D * i:D * (i + 1)],
                                                                  in_=pB[:], func=AF.Copy),
                     wait=[(sT, sT.n)], inc=(sS, 1))
                bev_hist[i] = sS.n

            pendB[0] = emit_B
        pendB[0]()
        pendB[0] = None
        tab_left = list(range(NT)) if b == 0 else []
        sV_stt = {}

        def x_load(i):
            buf = i % 2
            p.do("sync", lambda e, i=i, buf=buf: e.dma_start(
                out=x_sb[buf][:], in_=dx_f[b * N + 128 * i:b * N + 128 * i + 128, :]),
                 wait=[(dox01[buf], dox01[buf].n)], inc=(dx01[buf], 16))

        def out_hook(j):
            if j == 0:
                x_load(0)
            if j + 1 < NT:
                x_load(j + 1)
            buf = j % 2
            p.do("vector", lambda e, buf=buf, j=j: e.tensor_add(
                x_sb[buf][:], x_sb[buf][:], acc[:, D * j:D * (j + 1)]),
                 wait=[(dx01[buf], dx01[buf].n), (sV, sV.n)], inc=(sV, 1), drain=(j == 0))
            p.do("sync", lambda e, j=j, buf=buf: e.dma_start(
                out=dout_f[b * N + 128 * j:b * N + 128 * j + 128, :], in_=x_sb[buf][:]),
                 wait=[(sV, sV.n)], inc=(dox01[buf], 16))
        for e_i in range(NE):
            buf = e_i % 2
            for i in range(NT):
                pEx = [pU1, pU2][i % 2]
                w0 = [wex_ready[e_i]]
                key = (e_i, i - 2) if i >= 2 else (e_i - 1, NT - 2 + i)
                if key in sV_stt:
                    w0.append((sV, sV_stt[key]))
                else:
                    w0.append((sV, sV.n))
                    w0.append((sS, sS.n))
                for r in range(8):
                    for c in range(2):
                        p.do("tensor", lambda e, i=i, r=r, c=c, buf=buf, pEx=pEx: e.matmul(
                            pEx[:, 512 * c:512 * (c + 1)],
                            xnT_r[:, r, 128 * i:128 * (i + 1)],
                            wex[buf][:, D * r + 512 * c: D * r + 512 * (c + 1)],
                            start=(r == 0), stop=(r == 7), skip_group_check=True),
                             wait=w0 if (r == 0 and c == 0) else [],
                             inc=(sT, 1) if (r == 7 and c == 1) else None)
                p.do("vector", lambda e, i=i, e_i=e_i, pEx=pEx: e.scalar_tensor_tensor(
                    acc[:, D * i:D * (i + 1)], pEx[:], G_bf[:, NE * i + e_i:NE * i + e_i + 1],
                    acc[:, D * i:D * (i + 1)], ALU.mult, ALU.add),
                     wait=[(sT, sT.n)], inc=(sV, 1), drain=(i == 0))
                sV_stt[(e_i, i)] = sV.n
                if b == 0 and i in (NT // 3, (2 * NT) // 3) and tab_left:
                    ti = tab_left.pop(0)
                    emit_tab(ti, [(sV, sV_freq)] if ti == 0 else ())
                if e_i == NE - 1:
                    out_hook(i)
            if e_i + 2 < NE:
                w_le = [(sT, sT.n)]
                if b == 0:
                    w_le.append((dwb, dwb.n))
                load_expert(b, e_i + 2, w_le)
        kv_gate[0] = sV_stt[(NE - 1, NT - 1)]

    def emit_rot(i, g, dst, sub_wait=None):
        cs = cos_tab.ap()[:, 512 * i + 256 * g: 512 * i + 256 * (g + 1)]
        sn = sin_tab.ap()[:, 512 * i + 256 * g: 512 * i + 256 * (g + 1)]
        dr = dst.ap().rearrange("p (h f) -> p h f", h=8)
        ta_r = ta.ap().rearrange("p (h f) -> p h f", h=8)
        tb_r = tb.ap().rearrange("p (h f) -> p h f", h=8)
        tc_r = tc.ap().rearrange("p (h f) -> p h f", h=8)
        td_r = td.ap().rearrange("p (h f) -> p h f", h=8)
        sS0 = sS.n
        p.do("gpsimd", lambda e: e.tensor_mul(tc[:], t2b[:], cs),
             wait=[(sS, sS0), (sV, sV.n)], inc=(sG, 1))
        p.do("gpsimd", lambda e: e.tensor_mul(td[:], t1b[:], sn), inc=(sG, 1))
        sG_rot = sG.n
        p.do("vector", lambda e: e.tensor_mul(ta[:], t1b[:], cs),
             wait=[(sS, sS0)], inc=(sV, 1))
        p.do("vector", lambda e: e.tensor_mul(tb[:], t2b[:], sn), inc=(sV, 1))
        c_tb = sV.n
        w_sub = [(sV, c_tb)] + ([sub_wait] if sub_wait else [])
        p.do("vector", lambda e: e.tensor_sub(dr[:, :, 0:32], ta_r, tb_r),
             wait=w_sub, inc=(sV, 1))
        p.do("vector", lambda e: e.tensor_add(dr[:, :, 32:64], tc_r, td_r),
             wait=[(sG, sG_rot)], inc=(sV, 1))

    w_ready = {"q": [], "kv": [], "wo": []}

    def build_wqT(g, wT=None, b=0):
        if b == 1:
            p.do("sync", lambda e, g=g: e.dma_start(out=wqT[:], in_=dwscr[g, :, 0:4096]),
                 wait=[(sT, sT.n), (dwb, dwb.n)], inc=(dwb, 16))
            w_ready["q"] = [(dwb, dwb.n)]
            return
        def srcq(s):
            return dwqkv[512 * g + 128 * s:512 * g + 128 * (s + 1), :]

        def evq(s):
            return wqT_r[:, :, 128 * s:128 * (s + 1)]

        build_wT(srcq, 4, D, evq,
                 extra_first_wait=[(sT, wT if wT is not None else sT.n),
                                   (sV, sV_ln[0]), (dwb, dwb.n)])
        p.do("sync", lambda e, g=g: e.dma_start(out=dwscr[g, :, 0:4096], in_=wqT[:]),
             wait=[(sV, sV.n)], inc=(dwb, 16))
        w_ready["q"] = []

    def build_wkvT(g, wT=None, b=0):
        if b == 1:
            p.do("sync", lambda e, g=g: e.dma_start(out=wkvT[:], in_=dwscr[g, :, 4096:12288]),
                 wait=[(sT, sT.n), (dwb, dwb.n), (doa01[0], doa01[0].n),
                       (doa01[1], doa01[1].n)], inc=(dwb, 16))
            w_ready["kv"] = [(dwb, dwb.n)]
            return
        def src(s):
            part = 0 if s < 4 else 1
            sl = s % 4
            src_row = D + part * D + 512 * g + 128 * sl
            return dwqkv[src_row:src_row + 128, :]

        def ev(s):
            part = 0 if s < 4 else 1
            sl = s % 4
            coff = 512 * part + 128 * sl
            return wkvT_r[:, :, coff:coff + 128]

        build_wT(src, 8, D, ev,
                 extra_first_wait=[(sT, wT if wT is not None else sT.n),
                                   (sV, sV_ln[0]), (dwb, dwb.n)],
                 extra_tr_wait=[(doa01[0], doa01[0].n), (doa01[1], doa01[1].n)])
        p.do("sync", lambda e, g=g: e.dma_start(out=dwscr[g, :, 4096:12288], in_=wkvT[:]),
             wait=[(sV, sV.n)], inc=(dwb, 16))
        w_ready["kv"] = []

    def build_woT(g, b=0):
        if b == 1:
            p.do("sync", lambda e, g=g: e.dma_start(out=woT[:], in_=dwscr[g, :, 12288:16384]),
                 wait=[(sT, sT.n), (dwb, dwb.n)], inc=(dwb, 16))
            w_ready["wo"] = [(dwb, dwb.n)]
            return
        build_wT(lambda s: dwo[128 * s:128 * (s + 1), 512 * g:512 * (g + 1)], 8, 512,
                 lambda s: woT_r[:, :, 128 * s:128 * (s + 1)],
                 extra_first_wait=[(sT, sT.n), (sV, sV_ln[0]), (dwb, dwb.n)])
        p.do("sync", lambda e, g=g: e.dma_start(out=dwscr[g, :, 12288:16384], in_=woT[:]),
             wait=[(sV, sV.n)], inc=(dwb, 16))
        w_ready["wo"] = []

    q12_r3 = q12.ap().rearrange("p (h f) -> p h f", h=8)
    t1b_r3 = t1b.ap().rearrange("p (h f) -> p h f", h=8)
    t2b_r3 = t2b.ap().rearrange("p (h f) -> p h f", h=8)
    ksq4 = attnT[0].ap()[:, 0:512].rearrange("p (h x f) -> p x h f", h=8, x=2)
    ksq3 = attnT[0].ap()[:, 0:512].rearrange("p (h f) -> p h f", h=8)
    qsq4 = r12.ap().rearrange("p (h x f) -> p x h f", h=8, x=2)
    qsq3 = r12.ap().rearrange("p (h f) -> p h f", h=8)

    kv_gate = [0]
    norm_gate = {}

    def emit_attn_prep(b):
        p.do("vector", lambda e: e.memset(v_aug_r[:, :, :, 64:65], 1.0),
             wait=[(sT, sT.n), (dtap, dtap.n), (dwb, dwb.n)], inc=(sV, 1))
        p.do("vector", lambda e: e.memset(q12_r3[:, :, 65:67], 1.0), inc=(sV, 1))

    def emit_kv(b, g, tile_hook=None):
        pPP = [pU0, pU1, pU2]
        hist = {}
        sV_kv0 = sV.n
        p.do("vector", lambda e: e.memset(kmax[:], 0.0),
             wait=[(sV, sV.n)], inc=(sV, 1))
        sse = {}
        c_rot = {}
        c_sq = {}
        c_red = {}
        c_tp = {}
        sT_grp = sT.n

        def kv_transposes(j):
            for h in range(8):
                w_ = [(sV, hist[j - 1][1] if j >= 1 else sV.n)] if h == 0 else []
                p.do("tensor", lambda e, h=h: e.transpose(
                    pQb[0:64, 128 * h:128 * (h + 1)], r12[:, 64 * h:64 * (h + 1)], ident[:]),
                     wait=w_, inc=(sT, 1) if h == 7 else None)
            c_tp[j] = sT.n

        def kv_copy(j):
            w_ = [(sT, c_tp[j])]
            if j == 0:
                w_.append((sV, sV_kv0))
            p.do("vector", lambda e, j=j: e.tensor_copy(
                k_aug_r[0:64, :, 128 * j:128 * (j + 1)], pQb_r[0:64, :, :]),
                 wait=w_, inc=(sV, 1))
            hist[j] = (sse[j], sV.n)

        for j in range(NT):
            pk = pPP[j % 3]
            if j < 3:
                if g == 0:
                    w0 = [(sS, sS.n), (sV, kv_gate[0])]
                else:
                    w0 = [(sS, norm_gate.get("s", sS.n)), (sV, norm_gate.get("v", sV.n))]
                w0 += [(dwb, dwb.n)] + list(w_ready["kv"])
            else:
                w0 = [(sS, hist[j - 3][0]), (sV, hist[j - 3][1])]
            for r in range(8):
                for c in range(2):
                    p.do("tensor", lambda e, j=j, r=r, c=c, pk=pk: e.matmul(
                        pk[:, 512 * c:512 * (c + 1)],
                        xnT_r[:, r, 128 * j:128 * (j + 1)],
                        wkvT[:, D * r + 512 * c:D * r + 512 * (c + 1)],
                        start=(r == 0), stop=(r == 7), skip_group_check=True),
                         wait=w0 if (r == 0 and c == 0) else [],
                         inc=(sT, 1) if (r == 7 and c == 1) else None)
            sT_kv = sT.n
            wv_ = [(sT, sT_kv)]
            if j >= 1:
                wv_.append((sV, c_rot[j - 1]))
                wv_.append((sG, c_sq[j - 1]))
            else:
                wv_.append((sG, sG.n))
            p.do("scalar", lambda e, j=j, pk=pk: e.activation(
                out=v_aug_r[:, j, :, 0:64],
                in_=pk.ap()[:, 512:1024].rearrange("p (h c) -> p h c", h=8), func=AF.Copy),
                 wait=wv_, inc=(sS, 1))
            p.do("scalar", lambda e, pk=pk: e.activation(out=t1b[:], in_=pk.ap()[:, 0:512:2],
                                                         func=AF.Copy), inc=(sS, 1))
            p.do("scalar", lambda e, pk=pk: e.activation(out=t2b[:], in_=pk.ap()[:, 1:512:2],
                                                         func=AF.Copy), inc=(sS, 1))
            sse[j] = sS.n
            if j >= 1:
                kv_transposes(j - 1)
            emit_rot(j, g, r12,
                     sub_wait=(sT, c_tp[j - 1]) if j >= 1 else None)
            c_rot[j] = sV.n
            wsq = [(sS, sse[j])]
            if j >= 1:
                wsq.append((sV, c_red[j - 1]))
            else:
                wsq.append((sT, sT_grp))
            p.do("gpsimd", lambda e: e.tensor_tensor(out=ksq4[:, 0], in0=t1b_r3[:],
                                                     in1=t1b_r3[:], op=ALU.mult),
                 wait=wsq, inc=(sG, 1))
            p.do("gpsimd", lambda e: e.tensor_tensor(out=ksq4[:, 1], in0=t2b_r3[:],
                                                     in1=t2b_r3[:], op=ALU.mult),
                 inc=(sG, 1))
            c_sq[j] = sG.n
            p.do("vector", lambda e: e.tensor_reduce(ktmp[:], ksq3[:], AX.X, ALU.add),
                 wait=[(sG, c_sq[j])], inc=(sV, 1))
            c_red[j] = sV.n
            p.do("vector", lambda e: e.tensor_tensor(out=kmax[:], in0=kmax[:], in1=ktmp[:],
                                                     op=ALU.max),
                 wait=[(sV, c_red[j])], inc=(sV, 1))
            if j >= 1:
                kv_copy(j - 1)
        kv_transposes(NT - 1)
        kv_copy(NT - 1)
        p.do("vector", lambda e: e.memset(k_aug[64:68, :], 1.0),
             wait=[(sT, sT.n), (sV, sV.n), (dtap, dtap.n)], inc=(sV, 1))
        p.do("gpsimd", lambda e: e.iota(kciota[:], pattern=[[1, N]], base=0, channel_multiplier=0,
                                        allow_small_or_imprecise_dtypes=True),
             wait=[(sT, sT.n), (sV, sV.n), (dtap, dtap.n)], inc=(sG, 1))
        p.do("vector", lambda e: e.tensor_scalar(
            kc_all[:], kciota[:], c_vec[:], None, ALU.mult),
             wait=[(sG, sG.n), (sV, sV_kc)], inc=(sV, 1), drain=True)
        p.do("vector", lambda e: e.tensor_copy(kc_hibf[:], kc_all[:]),
             inc=(sV, 1), drain=True)
        p.do("vector", lambda e: e.tensor_sub(kc_all[:], kc_all[:], kc_hibf[:]),
             inc=(sV, 1), drain=True)
        p.do("gpsimd", lambda e, g=g: e.dma_start(
            out=k_aug.ap()[65:66, :].rearrange("p (h c) -> p h c", h=HGN),
            in_=kc_hibf.ap()[HGN * g:HGN * (g + 1), :]),
             wait=[(sV, sV.n), (sT, sT.n), (dkc, dkc.n)], inc=(dkc, 16))
        p.do("gpsimd", lambda e, g=g: e.dma_start(
            out=k_aug.ap()[66:67, :].rearrange("p (h c) -> p h c", h=HGN),
            in_=kc_all.ap()[HGN * g:HGN * (g + 1), :]),
             wait=[(dkc, dkc.n)], inc=(dkc, 16))
        p.do("vector", lambda e: e.tensor_copy(km_b[:], kmax[:]), inc=(sV, 1), drain=True)
        p.do("tensor", lambda e: e.transpose(pQb[0:8, 0:128], km_b[:], ident[:]),
             wait=[(sV, sV.n)], inc=(sT, 1))
        p.do("vector", lambda e: e.tensor_reduce(m2b[:], pQb[0:8, 0:128], AX.X, ALU.max),
             wait=[(sT, sT.n)], inc=(sV, 1))
        p.do("tensor", lambda e: e.transpose(pQb[0:1, 0:8], m2b[:], ident[0:8, 0:8]),
             wait=[(sV, sV.n)], inc=(sT, 1))
        p.do("scalar", lambda e: e.activation(out=srow_bf[:], in_=pQb[0:1, 0:8], func=AF.Copy),
             wait=[(sT, sT.n)], inc=(sS, 1))
        p.do("tensor", lambda e: e.matmul(pOut[:, 0:8], ones1[:], srow_bf[:],
                                          start=True, stop=True),
             wait=[(sS, sS.n)], inc=(sT, 1))
        p.do("tensor", lambda e, g=g: e.matmul(pOut[:, 8:16], ones_f[:],
                                               crow_f[0:1, HGN * g:HGN * (g + 1)],
                                               start=True, stop=True,
                                               skip_group_check=True),
             wait=[(sS, sS_crow)], inc=(sT, 1))
        p.do("scalar", lambda e: e.activation(out=bmc[:, 0:8], in_=pOut[:, 0:8], func=AF.Copy),
             wait=[(sT, sT.n)], inc=(sS, 1))
        p.do("scalar", lambda e: e.activation(out=bmc[:, 8:16], in_=pOut[:, 8:16],
                                              func=AF.Copy, scale=1.0001), inc=(sS, 1))

    def emit_attn(b, g):
        unit_i = [0]

        def next_unit():
            u = units[unit_i[0] % 6]
            unit_i[0] += 1
            return u

        first = [True]
        pend = [None]
        prev_av_end = [{}]
        qa = {}
        qev = {}
        rotv = {}
        sqg = {}
        red = {}
        tBc = {}
        c67 = {}
        g64 = {}
        sqc = {}
        tp = {}
        st7 = {}

        def q_mms(i):
            if i == 0:
                w0 = [(sS, sS.n), (sV, sV.n), (sG, sG.n)] + list(w_ready["q"])
            else:
                w0 = [(sV, qa[i - 1]), (sS, qev[i - 1])]
            for r in range(8):
                p.do("tensor", lambda e, i=i, r=r: e.matmul(
                    pQ[:], xnT_r[:, r, 128 * i:128 * (i + 1)],
                    wqT[:, 512 * r: 512 * (r + 1)],
                    start=(r == 0), stop=(r == 7)),
                     wait=w0 if r == 0 else [], inc=(sT, 1) if r == 7 else None)

        def q_evacs(i):
            w1 = [(sT, sT.n)]
            if i >= 1:
                w1 += [(sV, rotv[i - 1]), (sG, sqg[i - 1])]
            else:
                w1 += [(sV, sV.n), (sG, sG.n)]
            p.do("scalar", lambda e: e.activation(out=t1b[:], in_=pQ.ap()[:, 0:512:2],
                                                  func=AF.Copy, scale=0.125),
                 wait=w1, inc=(sS, 1))
            p.do("scalar", lambda e: e.activation(out=t2b[:], in_=pQ.ap()[:, 1:512:2],
                                                  func=AF.Copy, scale=0.125), inc=(sS, 1))
            qev[i] = sS.n

        def q_tail1(i):
            wkt = [(sS, qev[i])]
            if i >= 1:
                wkt.append((sV, tBc[i - 1]))
            p.do("vector", lambda e, i=i: e.tensor_scalar(
                ktmp[:], bmc[:, 8:16], tokc[:, i:i + 1], None, ALU.mult),
                 wait=wkt, inc=(sV, 1))
            emit_rot(i, g, q12,
                     sub_wait=(sT, tp[i - 1]) if i >= 1 else None)
            rotv[i] = sV.n
            wsq = [(sS, qev[i])]
            if i >= 1:
                wsq.append((sV, red[i - 1]))
            else:
                wsq.append((sT, sT.n))
            p.do("gpsimd", lambda e: e.tensor_tensor(out=qsq4[:, 0], in0=t1b_r3[:],
                                                     in1=t1b_r3[:], op=ALU.mult),
                 wait=wsq, inc=(sG, 1))
            p.do("gpsimd", lambda e: e.tensor_tensor(out=qsq4[:, 1], in0=t2b_r3[:],
                                                     in1=t2b_r3[:], op=ALU.mult),
                 inc=(sG, 1))
            sqg[i] = sG.n

        def q_tail2(i):
            wrd = [(sG, sqg[i])]
            if i >= 1:
                wrd += [(sS, sqc[i - 1]), (sV, tBc[i - 1])]
            p.do("vector", lambda e: e.tensor_reduce(qn2[:], qsq3[:], AX.X, ALU.add),
                 wait=wrd, inc=(sV, 1))
            red[i] = sV.n
            wbq = [(sV, red[i])]
            if i >= 1:
                wbq.append((sS, sqc[i - 1]))
            p.do("gpsimd", lambda e: e.tensor_tensor(out=bq2[:], in0=qn2[:], in1=bmc[:, 0:8],
                                                     op=ALU.mult),
                 wait=wbq, inc=(sG, 1))
            p.do("scalar", lambda e: e.activation(out=qn2[:], in_=bq2[:], func=AF.Sqrt),
                 wait=[(sG, sG.n)], inc=(sS, 1))
            p.do("scalar", lambda e: e.activation(out=dummy1[:], in_=cb_eps[0:1, :],
                                                  func=AF.Exp), inc=(sS, 1))
            sqc[i] = sS.n
            wtb = [(sS, sqc[i])]
            if i >= 1:
                wtb += [(sG, g64[i - 1]), (sV, c67[i - 1])]
            p.do("vector", lambda e: e.scalar_tensor_tensor(
                tB[:], qn2[:], -1.05, ktmp[:], ALU.mult, ALU.subtract),
                 wait=wtb, inc=(sV, 1))
            tBc[i] = sV.n
            wc64 = [(sV, tBc[i])]
            if i >= 1:
                wc64.append((sT, tp[i - 1]))
            p.do("gpsimd", lambda e: e.tensor_copy(q12_r3[:, :, 64:65], tB[:]),
                 wait=wc64, inc=(sG, 1))
            g64[i] = sG.n
            wc67 = [(sV, tBc[i]), (sG, g64[i])]
            if i >= 1:
                wc67.append((sT, tp[i - 1]))
            p.do("vector", lambda e: e.tensor_sub(q12_r3[:, :, 67:68], tB[:],
                                                  q12_r3[:, :, 64:65]),
                 wait=wc67, inc=(sV, 1))
            c67[i] = sV.n
            for h in range(8):
                w_ = [(sV, c67[i]), (sS, qev[i])] if h == 0 else []
                p.do("tensor", lambda e, h=h: e.transpose(
                    pQb[0:68, 128 * h:128 * (h + 1)], q12[:, 68 * h:68 * (h + 1)], ident[:]),
                     wait=w_, inc=(sT, 1) if h == 7 else None)
            tp[i] = sT.n

        def q_copy(i):
            w_ = [(sT, tp[i])]
            if i >= 1:
                w_.append((sT, st7[i - 1]))
            p.do("vector", lambda e: e.tensor_copy(q_aug[0:68, :], pQb.ap()[0:68, 0:1024]),
                 wait=w_, inc=(sV, 1))
            qa[i] = sV.n

        q_mms(0)
        q_evacs(0)
        q_tail1(0)
        q_tail2(0)
        q_copy(0)

        for i in range(NT):
            L = 128 * (i + 1)
            nu = (L + 511) // 512
            sS_exp = {}
            sG_mask = {}
            av_end = {}
            sV_ev = [sV.n]

            def emit_st(h, sS_exp=sS_exp, sG_mask=sG_mask, av_end=av_end, i=i, L=L, nu=nu):
                ab = attnT[(i * 8 + h) % 2]
                for u_i in range(nu):
                    c0 = 512 * u_i
                    w_ = min(512, L - c0)
                    u = next_unit()
                    for jj in range(w_ // 128):
                        j = (c0 // 128) + jj
                        wl = []
                        if u_i == 0 and jj == 0:
                            wl.append((sV, qa[i]))
                            if first[0]:
                                wl.append((dkc, dkc.n))
                                wl.append((dtap, dtap.n))
                                first[0] = False
                        if jj == 0 and u["free"]:
                            wl.append(u["free"])
                        p.do("tensor", lambda e, u=u, h=h, j=j, jj=jj: e.matmul(
                            u["t"][:, u["c"] + 128 * jj: u["c"] + 128 * (jj + 1)],
                            k_aug[0:68, N * h + 128 * j: N * h + 128 * (j + 1)],
                            q_aug[0:68, 128 * h:128 * (h + 1)],
                            start=True, stop=True), wait=wl,
                             inc=(sT, 1) if jj == w_ // 128 - 1 else None)
                    wl2 = [(sT, sT.n)]
                    if u_i == 0 and h >= 2 and (h - 2) in av_end:
                        wl2.append((sT, av_end[h - 2]))
                    if u_i == 0 and h == 1 and 7 in prev_av_end[0]:
                        wl2.append((sT, prev_av_end[0][7]))
                    p.do("scalar", lambda e, u=u, ab=ab, c0=c0, w_=w_: e.activation(
                        out=ab[:, c0:c0 + w_], in_=u["t"][:, u["c"]:u["c"] + w_], func=AF.Exp),
                         wait=wl2, inc=(sS, 1))
                    u["free"] = (sS, sS.n)
                sS_exp[h] = sS.n
                p.do("gpsimd", lambda e, ab=ab, L=L: e.tensor_mul(
                    ab[:, L - 128:L], ab[:, L - 128:L], mask_st[:]),
                     wait=[(sS, sS_exp[h])], inc=(sG, 1))
                sG_mask[h] = sG.n

            def emit_av(h, sS_exp=sS_exp, sG_mask=sG_mask, av_end=av_end,
                        sV_ev=sV_ev, i=i, L=L, nu=nu):
                ab = attnT[(i * 8 + h) % 2]
                for j in range(i + 1):
                    wl = [(sG, sG_mask[h]), (sS, sS_exp[h]), (sV, sV_ev[0])] if j == 0 else []
                    p.do("tensor", lambda e, j=j, h=h, i=i, ab=ab: e.matmul(
                        pOut[0:65, 128 * (h % 4):128 * (h % 4 + 1)],
                        v_aug.ap()[:, (j * HGN + h) * 65:(j * HGN + h) * 65 + 65],
                        ab[:, 128 * j:128 * (j + 1)],
                        start=(j == 0), stop=(j == i)), wait=wl,
                         inc=(sT, 1) if j == i else None)
                av_end[h] = sT.n
                if h % 4 == 3:
                    sT_av = sT.n
                    hg = h // 4
                    p0 = (h - 3) // 2
                    pOut_g = pOut.ap()[0:64, :].rearrange("p (pr a c) -> p pr a c", pr=2, a=2)
                    p.do("vector", lambda e, p0=p0, i=i, pOut_g=pOut_g: e.tensor_copy(
                        attnout_r[0:64, p0:p0 + 2, 128 * i:128 * (i + 1)], pOut_g[:, :, 0, :]),
                         wait=[(sT, sT_av)], inc=(sV, 1))
                    p.do("vector", lambda e, p0=p0, i=i, pOut_g=pOut_g: e.tensor_copy(
                        attnout_r[64:128, p0:p0 + 2, 128 * i:128 * (i + 1)], pOut_g[:, :, 1, :]),
                         inc=(sV, 1))
                    p.do("vector", lambda e, i=i, hg=hg: e.tensor_copy(
                        rs_stage.ap()[0:1, 1024 * i + 512 * hg:1024 * i + 512 * (hg + 1)],
                        pOut[64:65, 0:512]),
                         wait=[(dox01[0], dox01[0].n), (dox01[1], dox01[1].n)],
                         inc=(sV, 1))
                    sV_ev[0] = sV.n

            for h in range(8):
                emit_st(h)
                if h == 7:
                    st7[i] = sT.n
                if h == 0 and pend[0] is not None:
                    pend[0]()
                    pend[0] = None
                    sV_ev[0] = sV.n
                if i + 1 < NT:
                    if h == 1:
                        q_mms(i + 1)
                    elif h == 2:
                        q_evacs(i + 1)
                    elif h == 3:
                        q_tail1(i + 1)
                    elif h == 5:
                        q_tail2(i + 1)
                if h >= 1:
                    emit_av(h - 1)
            if i + 1 < NT:
                q_copy(i + 1)
            pend[0] = lambda emit_av=emit_av: emit_av(7)
            prev_av_end[0] = av_end
        if pend[0] is not None:
            pend[0]()
            pend[0] = None

    def emit_norm_wo(b, g):
        sV_n0 = sV.n
        nm = (NT + 7) // 8
        half = min(8, NT) * 1024
        p.do("scalar", lambda e: e.activation(out=rs_stage[0:1, 0:half],
                                              in_=rs_stage[0:1, 0:half], func=AF.Ln),
             wait=[(sV, sV.n), (dtap, dtap.n)], inc=(sS, 1))
        p.do("scalar", lambda e: e.activation(out=rs_stage[0:1, 0:half],
                                              in_=rs_stage[0:1, 0:half], func=AF.Exp,
                                              scale=-1.0), inc=(sS, 1), drain=True)
        sS_rsm = {0: sS.n}
        if nm > 1:
            p.do("scalar", lambda e: e.activation(out=rs_stage[0:1, half:NT * 1024],
                                                  in_=rs_stage[0:1, half:NT * 1024],
                                                  func=AF.Ln),
                 inc=(sS, 1), drain=True)
            p.do("scalar", lambda e: e.activation(out=rs_stage[0:1, half:NT * 1024],
                                                  in_=rs_stage[0:1, half:NT * 1024],
                                                  func=AF.Exp, scale=-1.0),
                 inc=(sS, 1), drain=True)
        sS_rsm[1] = sS.n
        doa_entry = [doa01[0].n, doa01[1].n]
        sV_mult = {}
        sV_m = {}
        rs5 = rs_stage.ap().rearrange("p (i g w q) -> p i g w q", i=NT, g=2, w=4)
        seq = 0
        for m in range(nm):
            for h in range(8):
                hg = h // 4
                w4 = h % 4
                rows = slice(0, 64) if h % 2 == 0 else slice(64, 128)
                pblk = h // 2
                pnorm = [pU1, pU2][seq % 2]
                wcols = min(1024, 128 * NT - 1024 * m)
                for c in range((wcols + 511) // 512):
                    i0 = 8 * m + 4 * c
                    nb = min(4, NT - i0)
                    rhs = rs5[0:1, i0:i0 + nb, hg, w4, :]
                    wl = []
                    if c == 0:
                        if seq % 8 < 2:
                            wl = [(sV, sV_n0), (sS, sS_rsm[m]), (sT, sT.n), (dtap, dtap.n)]
                        if seq >= 2:
                            wl.append((sV, sV_mult[seq - 2]))
                    p.do("tensor", lambda e, rhs=rhs, c=c, pnorm=pnorm, nb=nb: e.matmul(
                        pnorm[0:64, 512 * c:512 * c + 128 * nb], ones1[0:1, 0:64], rhs,
                        start=True, stop=True), wait=wl,
                         inc=(sT, 1) if 512 * c + 128 * nb >= wcols else None)
                p.do("vector", lambda e, rows=rows, pblk=pblk, m=m, pnorm=pnorm, wcols=wcols: e.tensor_mul(
                    attnout_r[rows, pblk, 1024 * m:1024 * m + wcols],
                    attnout_r[rows, pblk, 1024 * m:1024 * m + wcols], pnorm[0:64, 0:wcols]),
                     wait=[(sT, sT.n)], inc=(sV, 1), drain=(seq == 0))
                sV_mult[seq] = sV.n
                seq += 1
            sV_m[m] = sV.n
        sV_norm = sV.n
        if stop_after == "normmul":
            return
        wev_hist = {}
        for i in range(NT):
            buf = i % 2
            tgts = [pU0[:, 0:512], pU0[:, 512:1024]] if i % 2 == 0 else \
                   [pOut[:, 0:512], pQ[:, 0:512]]
            for pr in range(4):
                for c in range(2):
                    wl = []
                    if pr == 0 and c == 0:
                        wl.append((sV, sV_m[min(i // 8, nm - 1)]))
                        wl += list(w_ready["wo"]) if i == 0 else []
                        if i >= 2:
                            wl.append((sS, wev_hist[i - 2]))
                    p.do("tensor", lambda e, i=i, pr=pr, c=c, tgts=tgts: e.matmul(
                        tgts[c],
                        attnout_r[:, pr, 128 * i:128 * (i + 1)],
                        woT.ap()[:, D * pr + 512 * c:D * pr + 512 * (c + 1)],
                        start=(pr == 0), stop=(pr == 3),
                        skip_group_check=True), wait=wl,
                         inc=(sT, 1) if (pr == 3 and c == 1) else None)
            wl = [(sT, sT.n), (doa01[buf], doa01[buf].n)]
            p.do("scalar", lambda e, buf=buf, tgts=tgts: e.activation(
                out=out_sb[buf][:, 0:512], in_=tgts[0], func=AF.Copy),
                 wait=wl, inc=(sS, 1))
            p.do("scalar", lambda e, buf=buf, tgts=tgts: e.activation(
                out=out_sb[buf][:, 512:1024], in_=tgts[1], func=AF.Copy),
                 inc=(sS, 1))
            wev_hist[i] = sS.n
            if i == NT - 2:
                norm_gate["s"] = sS.n
                norm_gate["v"] = sV_m[nm - 1]
            row0 = b * N + 128 * i
            if accum_mode == "add":
                p.do("gpsimd", lambda e, row0=row0, buf=buf: e.dma_start(
                    out=dout_f[row0:row0 + 128, :], in_=out_sb[buf][:], accum_op=ALU.add),
                     wait=[(sS, sS.n), (dox01[0], dox01[0].n), (dox01[1], dox01[1].n),
                           (doa01[0], doa_entry[0]), (doa01[1], doa_entry[1])],
                     inc=(doa01[buf], 16))
            else:
                p.do("gpsimd", lambda e, row0=row0, buf=buf: e.dma_start(
                    out=dout_f[row0:row0 + 128, :], in_=out_sb[buf][:]),
                     wait=[(sS, sS.n), (dox01[0], dox01[0].n), (dox01[1], dox01[1].n),
                           (doa01[0], doa_entry[0]), (doa01[1], doa_entry[1])],
                     inc=(doa01[buf], 16))

    STOP = ["prep", "ln", "ffn", "attnprep", "wkv", "kv", "attn", "norm", "all"].index({"wobuild": "norm", "normmul": "norm"}.get(stop_after, stop_after))
    sV_ln = [0]
    for b in range(NB if STOP >= 8 else 1):
        if STOP < 1: break
        load_expert(b, 0, [(sT, sT.n), (sS, sS.n), (sV, sV.n)])
        load_expert(b, 1, [])
        emit_ln(b)
        sV_ln[0] = sV.n
        sT_ln = sT.n
        tap(f"xnT_b{b}", xnT.ap(), [(sV, sV.n)])
        if STOP < 2: break
        build_wqT(0, sT_ln, b)
        sT_wq = sT.n
        build_wkvT(0, sT_wq, b)
        sT_bld = sT.n
        emit_ffn(b)
        tap(f"acc_b{b}", acc.ap(), [(sV, sV.n)])
        tap(f"G_b{b}", G_bf.ap(), [(sV, sV.n)])
        if STOP < 3: break
        emit_attn_prep(b)
        for g in range(NG if STOP >= 8 else 1):
            if STOP < 4: break
            if g == 1:
                build_wqT(g, sT_bld, b)
                build_wkvT(g, sT.n, b)
            if STOP < 5: break
            emit_kv(b, g)
            tap(f"kaug_b{b}g{g}", k_aug.ap()[0:66, :], [(sV, sV.n), (dkc, dkc.n)])
            tap(f"vaug_b{b}g{g}", v_aug.ap(), [(sS, sS.n)])
            if STOP < 6: break
            build_woT(g, b)
            emit_attn(b, g)
            tap(f"qaug_b{b}g{g}", q_aug.ap()[0:66, :], [(sV, sV.n), (sS, sS.n)])
            tap(f"attnout_b{b}g{g}", attnout.ap(), [(sV, sV.n)])
            tap(f"rs_b{b}g{g}", rs_stage.ap(), [(sV, sV.n)])
            if STOP < 7: break
            if stop_after == "wobuild":
                break
            emit_norm_wo(b, g)
            tap(f"attnoutn_b{b}g{g}", attnout.ap(), [(sV, sV.n)])

    p.do("gpsimd", lambda e: e.nop(),
         wait=[(doa01[0], doa01[0].n), (doa01[1], doa01[1].n),
               (dox01[0], dox01[0].n), (dox01[1], dox01[1].n)] +
              ([(dtap, dtap.n)] if tap_t else []))
    p.emit()
    ctx.close()
    return nc


_BUILT = {}


def _get_nc(n_tiles, taps=()):
    key = (n_tiles, tuple(taps))
    if key not in _BUILT:
        _BUILT[key] = build(n_tiles, taps)
    return _BUILT[key]


def kernel(L0_attn_x, L0_attn_Wqkv, L0_attn_Wo, L0_attn_log_xi,
           L0_attn_pi_gate_logit, L0_attn_e_gate_logit,
           L0_ffn_router_w, L0_ffn_router_b, L0_ffn_expert_w, L0_ffn_expert_b):
    x = np.ascontiguousarray(np.asarray(L0_attn_x, dtype=np.float32))
    B, Nseq, Dm = x.shape
    nt = Nseq // 128
    nc = _get_nc(nt)
    common = {
        "wqkv": np.ascontiguousarray(np.asarray(L0_attn_Wqkv, dtype=np.float32)),
        "wo": np.ascontiguousarray(np.asarray(L0_attn_Wo, dtype=np.float32)),
        "log_xi": np.ascontiguousarray(np.asarray(L0_attn_log_xi, dtype=np.float32)),
        "pi_gate": np.ascontiguousarray(np.asarray(L0_attn_pi_gate_logit, dtype=np.float32)),
        "e_gate": np.ascontiguousarray(np.asarray(L0_attn_e_gate_logit, dtype=np.float32)),
        "router_w": np.ascontiguousarray(np.asarray(L0_ffn_router_w, dtype=np.float32)),
        "router_b": np.ascontiguousarray(np.asarray(L0_ffn_router_b, dtype=np.float32)),
        "expert_w": np.ascontiguousarray(np.asarray(L0_ffn_expert_w, dtype=np.float32)),
        "expert_b": np.ascontiguousarray(np.asarray(L0_ffn_expert_b, dtype=np.float32)),
    }
    ncr = B // NB
    in_maps = []
    for c in range(ncr):
        m = dict(common)
        m["x"] = np.ascontiguousarray(x[NB * c:NB * (c + 1)])
        in_maps.append(m)
    res = run_bass_kernel_spmd(nc, in_maps, core_ids=list(range(ncr)))
    out = np.concatenate([res.results[c]["out"] for c in range(ncr)], axis=0)
    return out.astype(np.float32)


# version 28
# speedup vs baseline: 1.0114x; 1.0068x over previous
import numpy as np
from contextlib import ExitStack

import concourse.bass as bass
import concourse.mybir as mybir
from concourse.bass_utils import run_bass_kernel_spmd

F32 = mybir.dt.float32
BF16 = mybir.dt.bfloat16
I32 = mybir.dt.int32
U32 = mybir.dt.uint32
AF = mybir.ActivationFunctionType
ALU = mybir.AluOpType
AX = mybir.AxisListType

D = 1024
H = 16
DH = 64
HGN = 8
NG = 2
NE = 8
NB = 2
NCORES = 8
LN_EPS = 1e-5
LNPI = float(np.log(np.pi))
LN2PI = float(np.log(2.0 * np.pi))
PI = float(np.pi)


class Sem:
    __slots__ = ("h", "n", "name")


class Prog:
    ENGS = ("sync", "vector", "scalar", "tensor", "gpsimd")

    def __init__(self, nc, ctx):
        self.nc = nc
        self.ctx = ctx
        self.steps = {e: [] for e in self.ENGS}

    def sem(self, name):
        s = Sem()
        s.h = self.ctx.enter_context(self.nc.semaphore(name))
        s.n = 0
        s.name = name
        return s

    def do(self, eng, fn, wait=(), inc=None, drain=False):
        w = [(s.h, int(v)) for (s, v) in wait if v > 0]
        incs = []
        if inc is not None:
            if not isinstance(inc, list):
                inc = [inc]
            for (s, k) in inc:
                s.n += k
                incs.append((s.h, k))
        self.steps[eng].append((fn, w, incs, drain))

    def emit(self):
        blk = self.ctx.enter_context(self.nc.Block())

        def mk(name):
            def body(eng):
                for fn, waits, incs, drain in self.steps[name]:
                    for (h, v) in waits:
                        eng.wait_ge(h, v)
                    if drain:
                        eng.drain()
                    inst = fn(eng)
                    for (h, k) in incs:
                        inst = inst.then_inc(h, k)
            return body

        blk.sync(mk("sync"))
        blk.vector(mk("vector"))
        blk.scalar(mk("scalar"))
        blk.tensor(mk("tensor"))
        blk.gpsimd(mk("gpsimd"))


def build(n_tiles=16, taps=(), stop_after='all', accum_mode='add'):
    NT = n_tiles
    N = 128 * NT
    nc = bass.Bass("TRN2", target_bir_lowering=False)
    ctx = ExitStack()
    p = Prog(nc, ctx)

    dx = nc.dram_tensor("x", [NB, N, D], F32, kind="ExternalInput").ap()
    dwqkv = nc.dram_tensor("wqkv", [3 * D, D], F32, kind="ExternalInput").ap()
    dwo = nc.dram_tensor("wo", [D, D], F32, kind="ExternalInput").ap()
    dlxi = nc.dram_tensor("log_xi", [H], F32, kind="ExternalInput").ap()
    dpgl = nc.dram_tensor("pi_gate", [H], F32, kind="ExternalInput").ap()
    degl = nc.dram_tensor("e_gate", [H], F32, kind="ExternalInput").ap()
    drw = nc.dram_tensor("router_w", [D, NE], F32, kind="ExternalInput").ap()
    drb = nc.dram_tensor("router_b", [NE], F32, kind="ExternalInput").ap()
    dew = nc.dram_tensor("expert_w", [NE, D, D], F32, kind="ExternalInput").ap()
    dscr = nc.dram_tensor("wex_scr", [NE, 128, 8 * D], BF16, kind="Internal").ap()
    dwscr = nc.dram_tensor("wts_scr", [NG, 128, 16384], BF16, kind="Internal").ap()
    deb = nc.dram_tensor("expert_b", [NE, D], F32, kind="ExternalInput").ap()
    dout = nc.dram_tensor("out", [NB, N, D], F32, kind="ExternalOutput").ap()
    dx_f = dx.rearrange("b n d -> (b n) d")
    dout_f = dout.rearrange("b n d -> (b n) d")

    tap_t = {}
    for tname, tshape, tdt in taps:
        tap_t[tname] = nc.dram_tensor("tap_" + tname, tshape, tdt,
                                      kind="ExternalOutput").ap()

    AL = 64
    left0 = (nc._sbuf_addr_for_side("left") + 63) // 64 * 64
    cur = [0]
    offs = {}

    def place(name, shape, dtype, at=None):
        nbytes = int(np.prod(shape[1:])) * mybir.dt.size(dtype)
        if at is None:
            off = cur[0]
            cur[0] = off + ((nbytes + AL - 1) // AL) * AL
        else:
            off = at
        offs[name] = off
        return nc.alloc_sbuf_tensor_at(name, list(shape), dtype, offset=left0 + off)

    ident = place("ident", [128, 128], BF16)
    mask_st = place("mask_st", [128, 128], BF16)
    iota8 = place("iota8", [128, NE], F32)
    ones1 = place("ones1", [1, 128], BF16)
    rw_bf = place("rw_bf", [128, 8 * NE], BF16)
    rb_row = place("rb_row", [1, NE], BF16)
    lxi = place("lxi", [H, 1], F32)
    pgl = place("pgl", [H, 1], F32)
    egl = place("egl", [H, 1], F32)
    eg_s = place("eg_s", [H, 1], F32)
    xinv = place("xinv", [H, 1], F32)
    pig = place("pig", [H, 1], F32)
    c_vec = place("c_vec", [H, 1], F32)
    ones_f = place("ones_f", [1, 128], F32)
    sin_tab = place("sin_tab", [128, NT * 512], BF16)
    cos_tab = place("cos_tab", [128, NT * 512], BF16)
    wqT = place("wqT", [128, 8 * 512], BF16)
    xnT = place("xnT", [128, 8 * N], BF16)
    attnout = place("attnout", [128, 4 * N], BF16)
    regR = cur[0]
    sel = place("sel", [16, 512], F32, at=regR)
    kc_all = place("kc_all", [H, N], F32, at=regR)
    kciota = place("kciota", [16, N], F32, at=regR + 8192)
    kc_hibf = place("kc_hibf", [16, N], BF16, at=regR + 8192)
    eb_bf = place("eb_bf", [NE, D], BF16, at=regR + 16384)
    io2 = place("io2", [128, 128], F32, at=regR + 18432)
    utmp = place("utmp", [128, 512], F32, at=regR + 18944)
    itmp = place("itmp", [128, 512], I32, at=regR + 20992)
    ftmp = place("ftmp", [128, 512], F32, at=regR + 23040)
    freq_rep = place("freq_rep", [128, 512], F32, at=offs["attnout"] + 8192)
    gtmp = place("gtmp", [128, 512], F32, at=regR + 27136)
    f_iota = place("f_iota", [1, 512], F32, at=regR + 27136)
    freq_row = place("freq_row", [1, 512], F32, at=regR + 28160)
    x_sb = [place("x0", [128, D], F32, at=regR + 18944),
            place("x1", [128, D], F32, at=regR + 23040)]
    rs_stage = place("rs_stage", [1, NT * 1024], BF16, at=regR)
    cur[0] = regR + max(NT * 1024 * 2, 32768)

    regA = cur[0]
    acc = place("acc", [128, NT * D], BF16, at=regA)
    k_aug = place("k_aug", [68, HGN * N], BF16, at=regA)
    cur[0] = regA + NT * D * 2

    regB = cur[0]
    wex = [place("wex0", [128, 8 * D], BF16, at=regB),
           place("wex1", [128, 8 * D], BF16, at=regB + 8 * D * 2)]
    o = regB
    v_aug = place("v_aug", [128, NT * HGN * 65], BF16, at=o); o += NT * HGN * 65 * 2
    attnT = [place("attnT0", [128, 2048], BF16, at=o),
             place("attnT1", [128, 2048], BF16, at=o + 4096)]
    o += 8192
    q_aug = place("q_aug", [68, HGN * 128], BF16, at=o); o += HGN * 128 * 2
    q12 = place("q12", [128, 544], BF16, at=o); o += 1088
    r12 = place("r12", [128, 512], BF16, at=o); o += 1024
    t1b = place("t1b", [128, 256], BF16, at=o)
    tsq = place("tsq", [128, 512], BF16, at=o); o += 512
    t2b = place("t2b", [128, 256], BF16, at=o); o += 512
    ta = place("ta", [128, 256], BF16, at=o); o += 512
    tb = place("tb", [128, 256], BF16, at=o); o += 512
    tc = place("tc", [128, 256], BF16, at=o); o += 512
    td = place("td", [128, 256], BF16, at=o); o += 512
    cur[0] = max(o, regB + 2 * 8 * D * 2)

    regC = cur[0]
    wkvT = place("wkvT", [128, 8 * D], BF16, at=regC)
    woT = place("woT", [128, 4 * D], BF16, at=regC)
    out_sb = [place("outsb0", [128, D], F32, at=regC + 4 * D * 2),
              place("outsb1", [128, D], F32, at=regC + 4 * D * 2 + D * 4)]
    cur[0] = regC + 8 * D * 2

    regD = cur[0]
    stage = [place("stage0", [128, D], BF16, at=regD),
             place("stage1", [128, D], BF16, at=regD + D * 2)]
    xn_tok = [place("xntok0", [128, D], BF16, at=regD),
              place("xntok1", [128, D], BF16, at=regD + D * 2)]
    cur[0] = regD + 2 * D * 2

    lnstats = place("lnstats", [128, 12], F32)
    lnmv = place("lnmv", [128, 2], F32)
    lnv = place("lnv", [128, 1], F32)
    rstd = place("rstd", [128, 1], F32)
    nmr = place("nmr", [128, 1], F32)
    smalls0 = cur[0]
    cur[0] += 96
    mstats = place("mstats", [128, 1], F32, at=smalls0)
    srow_bf = place("srow_bf", [1, 8], BF16, at=smalls0 + 32)
    m2b = place("m2b", [8, 1], BF16, at=smalls0 + 64)
    utmp2 = place("utmp2", [128, 512], F32, at=offs["attnout"])
    itmp2 = place("itmp2", [128, 512], I32, at=offs["attnout"] + 2048)
    ftmp2 = place("ftmp2", [128, 512], F32, at=offs["attnout"] + 4096)
    gtmp2 = place("gtmp2", [128, 512], F32, at=offs["attnout"] + 6144)
    kmax = place("kmax", [128, 8], F32)
    ktmp = place("ktmp", [128, 8], F32)
    km_b = place("km_b", [128, 8], BF16)
    crow_f = place("crow_f", [1, H], F32)
    qn2 = place("qn2", [128, 8], F32)
    bq2 = place("bq2", [128, 8], F32)
    tB = place("tB", [128, 8], F32)
    bmc = place("bmc", [128, 16], F32)
    dummy1 = place("dummy1", [1, 1], F32)
    tokc = place("tokc", [128, NT], F32)
    idf16 = place("idf16", [16, 16], F32)
    pexp = place("pexp", [128, NE], F32, at=regR + 29184)
    sume = place("sume", [128, 1], F32, at=regR + 29248)
    gate = place("gate", [128, 1], F32, at=regR + 29312)
    mx8 = place("mx8", [128, NE], F32, at=regR + 29376)
    idx8 = place("idx8", [128, NE], U32, at=regR + 29440)
    idxf = place("idxf", [128, 1], F32, at=regR + 29504)
    G_bf = place("G_bf", [128, NT * NE], BF16, at=regR + 28160)
    gt_sb = [place("gt0", [8, 128], BF16, at=regR + 28160 + 512),
             place("gt1", [8, 128], BF16, at=regR + 28160 + 768)]
    cb_eps = place("cb_eps", [128, 1], F32)
    cb_ph = place("cb_ph", [128, 1], F32)
    cb_fr = place("cb_fr", [128, 1], F32)

    assert cur[0] <= 212200, f"SBUF over budget: {cur[0]}"
    nc.alloc_sbuf_tensor("arena_reserve", [128, (cur[0] + 3) // 4], I32)

    pU0 = nc.alloc_psum_tensor("pU0", [128, 1024], F32)
    pU1 = nc.alloc_psum_tensor("pU1", [128, 1024], F32)
    pU2 = nc.alloc_psum_tensor("pU2", [128, 1024], F32)
    pOut = nc.alloc_psum_tensor("pOut", [128, 512], F32)
    pQ = nc.alloc_psum_tensor("pQ", [128, 512], F32)
    pU2b = pU2.bitcast(BF16)
    pOutb = pOut.bitcast(BF16)
    pQb = pQ.bitcast(BF16)
    units = []
    for t_, c0 in ((pU0, 0), (pU0, 512), (pU1, 0), (pU1, 512), (pU2, 0), (pU2, 512)):
        units.append({"t": t_, "c": c0, "free": None})

    sT = p.sem("sT")
    sV = p.sem("sV")
    sS = p.sem("sS")
    sG = p.sem("sG")
    dX = p.sem("dX")
    dx01 = [p.sem("dx0"), p.sem("dx1")]
    dw01 = [p.sem("dw0"), p.sem("dw1")]
    dst01 = [p.sem("dst0"), p.sem("dst1")]
    dkc = p.sem("dkc")
    dwb = p.sem("dwb")
    dox01 = [p.sem("dox0"), p.sem("dox1")]
    doa01 = [p.sem("doa0"), p.sem("doa1")]
    dtap = p.sem("dtap")

    def tap(name, src_ap, wait):
        if name in tap_t:
            d = tap_t[name]
            p.do("scalar", lambda e, d=d, s=src_ap: e.dma_start(out=d, in_=s),
                 wait=wait, inc=(dtap, 16))

    xnT_r = xnT.ap().rearrange("p (r c) -> p r c", r=8)
    k_aug_r = k_aug.ap().rearrange("p (h c) -> p h c", h=HGN)
    v_aug_r = v_aug.ap().rearrange("p (j h c) -> p j h c", j=NT, h=HGN)
    attnout_r = attnout.ap().rearrange("p (b c) -> p b c", b=4)
    pQb_r = pQb.ap().rearrange("p (h c) -> p h c", h=8)
    wqT_r = wqT.ap().rearrange("p (r c) -> p r c", r=8)
    wkvT_r = wkvT.ap().rearrange("p (r c) -> p r c", r=8)
    woT_r = woT.ap().rearrange("p (pr c) -> p pr c", pr=4)

    p.do("sync", lambda e: e.dma_start(out=lxi[:], in_=dlxi.rearrange("(h o) -> h o", o=1)), inc=(dX, 16))
    p.do("sync", lambda e: e.dma_start(out=pgl[:], in_=dpgl.rearrange("(h o) -> h o", o=1)), inc=(dX, 16))
    p.do("sync", lambda e: e.dma_start(out=egl[:], in_=degl.rearrange("(h o) -> h o", o=1)), inc=(dX, 16))
    dX_head = dX.n
    p.do("gpsimd", lambda e: e.dma_start(out=rw_bf.ap().rearrange("p (c e) -> p c e", c=8),
                                         in_=drw.rearrange("(c p) e -> p c e", p=128)), inc=(dkc, 16))
    p.do("gpsimd", lambda e: e.dma_start(out=rb_row.ap().rearrange("p (o e) -> p o e", o=1),
                                         in_=drb.rearrange("(o e) -> o e", o=1)), inc=(dkc, 16))
    dkc_w = dkc.n

    p.do("gpsimd", lambda e: e.iota(io2[:], pattern=[[1, 128]], base=0, channel_multiplier=-1,
                                    allow_small_or_imprecise_dtypes=True), inc=(sG, 1))
    p.do("gpsimd", lambda e: e.iota(iota8[:], pattern=[[1, NE]], base=0, channel_multiplier=0,
                                    allow_small_or_imprecise_dtypes=True), inc=(sG, 1))
    p.do("gpsimd", lambda e: e.iota(f_iota[:], pattern=[[0, 16], [1, 32]], base=0, channel_multiplier=0,
                                    allow_small_or_imprecise_dtypes=True), inc=(sG, 1))
    p.do("gpsimd", lambda e: e.iota(sel[:], pattern=[[1, 512]], base=0, channel_multiplier=-32,
                                    allow_small_or_imprecise_dtypes=True), inc=(sG, 1))
    sG_iotas = sG.n

    p.do("vector", lambda e: e.tensor_scalar(ident[:], io2[:], 0.0, None, ALU.is_equal),
         wait=[(sG, 1)], inc=(sV, 1))
    p.do("vector", lambda e: e.tensor_scalar(mask_st[:], io2[:], 0.0, None, ALU.is_ge), inc=(sV, 1))
    p.do("vector", lambda e: e.tensor_scalar(idf16[:], io2[0:16, 0:16], 0.0, None, ALU.is_equal),
         inc=(sV, 1))
    p.do("vector", lambda e: e.memset(ones1[:], 1.0), inc=(sV, 1))
    p.do("vector", lambda e: e.tensor_scalar(ftmp[0:16, :], sel[:], 0.0, None, ALU.is_ge),
         wait=[(sG, sG_iotas)], inc=(sV, 1))
    p.do("vector", lambda e: e.scalar_tensor_tensor(sel[:], sel[:], 32.0, ftmp[0:16, :],
                                                    ALU.is_lt, ALU.mult),
         inc=(sV, 1), drain=True)
    p.do("vector", lambda e: e.memset(cb_eps[:], LN_EPS), inc=(sV, 1))
    p.do("vector", lambda e: e.memset(cb_ph[:], PI / 2), inc=(sV, 1))
    p.do("vector", lambda e: e.memset(cb_fr[:], LNPI - LN2PI), inc=(sV, 1))
    p.do("vector", lambda e: e.memset(ones_f[:], 1.0), inc=(sV, 1))
    sV_consts = sV.n

    p.do("scalar", lambda e: e.activation(out=eg_s[:], in_=egl[:], func=AF.Sigmoid),
         wait=[(dX, dX_head)], inc=(sS, 1))
    p.do("scalar", lambda e: e.activation(out=xinv[:], in_=lxi[:], func=AF.Exp, scale=-1.0), inc=(sS, 1))
    p.do("scalar", lambda e: e.activation(out=pig[:], in_=pgl[:], func=AF.Sigmoid), inc=(sS, 1))
    sS_heads = sS.n
    p.do("vector", lambda e: e.tensor_mul(c_vec[:], eg_s[:], xinv[:]),
         wait=[(sS, sS_heads)], inc=(sV, 1))
    sV_kc = sV.n
    p.do("gpsimd", lambda e: e.iota(tokc[:], pattern=[[128, NT]], base=0,
                                    channel_multiplier=1,
                                    allow_small_or_imprecise_dtypes=True), inc=(sG, 1))
    p.do("tensor", lambda e: e.matmul(pOut[0:1, 0:16], c_vec[:], idf16[:],
                                      start=True, stop=True, skip_group_check=True),
         wait=[(sV, sV_kc)], inc=(sT, 1))
    p.do("scalar", lambda e: e.activation(out=crow_f[:], in_=pOut[0:1, 0:16], func=AF.Copy),
         wait=[(sT, sT.n)], inc=(sS, 1))
    sS_crow = sS.n

    p.do("tensor", lambda e: e.matmul(pQ[0:1, 0:512], pig[:], sel[:], start=True, stop=True),
         wait=[(sS, sS_heads), (sV, sV_consts)], inc=(sT, 1))
    sT_pig = sT.n
    p.do("scalar", lambda e: e.activation(out=ftmp[0:1, 0:512], in_=f_iota[:], func=AF.Exp,
                                          scale=-LNPI / 32.0, bias=cb_fr[0:1, :]),
         wait=[(sG, sG_iotas), (sV, sV_consts)], inc=(sS, 1))
    sS_pif = sS.n
    p.do("vector", lambda e: e.tensor_mul(freq_row[0:1, :], ftmp[0:1, 0:512], pQ[0:1, 0:512]),
         wait=[(sT, sT_pig), (sS, sS_pif)], inc=(sV, 1))
    p.do("tensor", lambda e: e.matmul(pU0[:, 0:512], ones_f[:], freq_row[:],
                                      start=True, stop=True),
         wait=[(sV, sV.n)], inc=(sT, 1))
    p.do("vector", lambda e: e.tensor_copy(freq_rep[:], pU0[:, 0:512]),
         wait=[(sT, sT.n)], inc=(sV, 1))
    sV_freq = sV.n


    def build_wT(src_ap_fn, n_src, width, evac_fn, extra_first_wait=(),
                 extra_tr_wait=()):
        nblk = width // 128
        for s in range(n_src):
            buf = s % 2
            w_load = [(sV, sV.n)] if s >= 2 else list(extra_first_wait)
            p.do("gpsimd", lambda e, s=s, buf=buf, fn=src_ap_fn, width=width: e.dma_start(
                out=stage[buf][:, 0:width], in_=fn(s)),
                 wait=w_load, inc=(dst01[buf], 16))
            for r in range(nblk):
                w_ = [(dst01[buf], dst01[buf].n)] if r == 0 else []
                if r == 0:
                    w_.append((sV, sV.n))
                    if s == 0:
                        w_ += list(extra_tr_wait)
                p.do("tensor", lambda e, buf=buf, r=r: e.transpose(
                    pU2b[:, 128 * r:128 * (r + 1)], stage[buf][:, 128 * r:128 * (r + 1)], ident[:]),
                     wait=w_, inc=(sT, 1) if r == nblk - 1 else None)
            p.do("vector", lambda e, s=s, fn=evac_fn: e.tensor_copy(fn(s), pU2b[:, 0:width]),
                 wait=[(sT, sT.n)], inc=(sV, 1))



    def emit_tab(i, first_wait=()):
        p.do("vector", lambda e, i=i: e.tensor_scalar(utmp2[:], freq_rep[:],
                                                      tokc[:, i:i + 1], None, ALU.mult),
             wait=list(first_wait) + [(sS, sS.n), (sV, sV.n)], inc=(sV, 1))
        p.do("vector", lambda e: e.tensor_copy(itmp2[:], utmp2[:]),
             wait=[(sV, sV.n)], inc=(sV, 1))
        p.do("vector", lambda e: e.tensor_copy(ftmp2[:], itmp2[:]),
             wait=[(sV, sV.n)], inc=(sV, 1))
        p.do("vector", lambda e: e.tensor_sub(ftmp2[:], utmp2[:], ftmp2[:]),
             wait=[(sV, sV.n)], inc=(sV, 1))
        p.do("vector", lambda e: e.tensor_scalar(gtmp2[:], ftmp2[:], 0.5, None, ALU.is_gt),
             wait=[(sV, sV.n)], inc=(sV, 1))
        p.do("vector", lambda e: e.tensor_sub(ftmp2[:], ftmp2[:], gtmp2[:]),
             wait=[(sV, sV.n)], inc=(sV, 1))
        p.do("scalar", lambda e, i=i: e.activation(out=sin_tab[:, 512 * i:512 * (i + 1)],
                                                   in_=ftmp2[:], func=AF.Sin, scale=2 * PI),
             wait=[(sV, sV.n)], inc=(sS, 1))
        p.do("vector", lambda e: e.tensor_scalar(itmp2[:], utmp2[:], 0.25, None, ALU.add),
             wait=[(sS, sS.n)], inc=(sV, 1))
        p.do("vector", lambda e: e.tensor_copy(ftmp2[:], itmp2[:]),
             wait=[(sV, sV.n)], inc=(sV, 1))
        p.do("vector", lambda e: e.tensor_sub(ftmp2[:], utmp2[:], ftmp2[:]),
             wait=[(sV, sV.n)], inc=(sV, 1))
        p.do("vector", lambda e: e.tensor_scalar(gtmp2[:], ftmp2[:], 0.25, None, ALU.is_gt),
             wait=[(sV, sV.n)], inc=(sV, 1))
        p.do("vector", lambda e: e.tensor_sub(ftmp2[:], ftmp2[:], gtmp2[:]),
             wait=[(sV, sV.n)], inc=(sV, 1))
        p.do("scalar", lambda e, i=i: e.activation(out=cos_tab[:, 512 * i:512 * (i + 1)],
                                                   in_=ftmp2[:], func=AF.Sin, scale=2 * PI,
                                                   bias=cb_ph[:]),
             wait=[(sV, sV.n)], inc=(sS, 1))

    wex_ready = {}
    echo_cnt = {}

    def load_expert(b, e_i, extra_wait):
        buf = e_i % 2
        if b == 0:
            p.do("gpsimd", lambda e, e_i=e_i, buf=buf: e.dma_start(
                out=wex[buf].ap().rearrange("p (c d) -> p c d", c=8),
                in_=dew[e_i].rearrange("(c p) d -> p c d", p=128)),
                 wait=extra_wait, inc=(dw01[buf], 16))
            wex_ready[e_i] = (dw01[buf], dw01[buf].n)
            p.do("sync", lambda e, e_i=e_i, buf=buf: e.dma_start(
                out=dscr[e_i], in_=wex[buf][:]),
                 wait=[(dw01[buf], dw01[buf].n)], inc=(dwb, 16))
            echo_cnt[e_i] = dwb.n
        else:
            p.do("sync", lambda e, e_i=e_i, buf=buf: e.dma_start(
                out=wex[buf][:], in_=dscr[e_i]),
                 wait=list(extra_wait) + [(dwb, dwb.n)], inc=(dwb, 16))
            wex_ready[e_i] = (dwb, dwb.n)

    def emit_ln(b):
        for i in range(NT):
            buf = i % 2
            row0 = b * N + 128 * i
            w_ = [(sS, sS.n)]
            if i < 2:
                w_ += [(sT, sT.n), (sV, sV.n), (dtap, dtap.n)]
            p.do("sync", lambda e, row0=row0, buf=buf: e.dma_start(
                out=x_sb[buf][:], in_=dx_f[row0:row0 + 128, :]),
                 wait=w_, inc=(dx01[buf], 16))
            p.do("vector", lambda e, buf=buf: e.bn_stats(lnstats[:, 0:6], x_sb[buf][:, 0:512]),
                 wait=[(dx01[buf], dx01[buf].n), (sS, sS.n)], inc=(sV, 1))
            p.do("vector", lambda e, buf=buf: e.bn_stats(lnstats[:, 6:12], x_sb[buf][:, 512:1024]),
                 inc=(sV, 1))
            p.do("vector", lambda e: e.bn_aggr(lnmv[:], lnstats[:]), inc=(sV, 1), drain=True)
            p.do("scalar", lambda e: e.activation(out=lnv[:], in_=lnmv[:, 1:2], func=AF.Ln, bias=cb_eps[:]),
                 wait=[(sV, sV.n)], inc=(sS, 1))
            p.do("scalar", lambda e: e.activation(out=rstd[:], in_=lnv[:], func=AF.Exp, scale=-0.5),
                 inc=(sS, 1), drain=True)
            p.do("vector", lambda e: e.tensor_scalar(nmr[:], lnmv[:, 0:1], rstd[:], -1.0,
                                                     ALU.mult, ALU.mult),
                 wait=[(sS, sS.n)], inc=(sV, 1), drain=True)
            p.do("scalar", lambda e, buf=buf: e.activation(out=xn_tok[buf][:], in_=x_sb[buf][:],
                                                           func=AF.Identity, scale=rstd[:], bias=nmr[:]),
                 wait=[(sV, sV.n)], inc=(sS, 1), drain=True)
            for r in range(8):
                w_ = [(sS, sS.n), (sV, sV.n)] if r == 0 else []
                p.do("tensor", lambda e, buf=buf, r=r: e.transpose(
                    pQb[:, 128 * r:128 * (r + 1)], xn_tok[buf][:, 128 * r:128 * (r + 1)], ident[:]),
                     wait=w_, inc=(sT, 1) if r == 7 else None)
            p.do("vector", lambda e, i=i: e.tensor_copy(
                xnT_r[:, :, 128 * i:128 * (i + 1)],
                pQb.ap().rearrange("p (r c) -> p r c", r=8)),
                 wait=[(sT, sT.n)], inc=(sV, 1))

    def emit_ffn(b):
        p.do("gpsimd", lambda e: e.dma_start(out=eb_bf[:], in_=deb),
             wait=[(sT, sT.n), (sV, sV.n), (dtap, dtap.n), (dkc, dkc.n)], inc=(dkc, 16))
        dkc_eb = dkc.n
        bev_hist = {}
        gtev_hist = {}
        pendB = [None]
        for i in range(NT):
            PL = [pOut, pQ][i % 2]
            PLb = [pOutb, pQb][i % 2]
            for r in range(8):
                w_ = [(sV, sV.n), (dkc, dkc_w)] if i == 0 and r == 0 else []
                if r == 0 and i >= 2:
                    w_.append((sV, gtev_hist[i - 2]))
                p.do("tensor", lambda e, i=i, r=r, PL=PL: e.matmul(
                    PL[:, 0:NE], xnT_r[:, r, 128 * i:128 * (i + 1)], rw_bf[:, NE * r:NE * (r + 1)],
                    start=(r == 0), stop=False), wait=w_)
            p.do("tensor", lambda e, PL=PL: e.matmul(PL[:, 0:NE], ones1[:], rb_row[:],
                                                     start=False, stop=True), inc=(sT, 1))
            sT_lg = sT.n
            if pendB[0] is not None:
                pendB[0]()
                pendB[0] = None
            p.do("vector", lambda e, PL=PL: e.tensor_reduce(mstats[:, 0:1], PL[:, 0:NE], AX.X,
                                                            ALU.max, negate=True),
                 wait=[(sT, sT_lg)], inc=(sV, 1), drain=True)
            p.do("scalar", lambda e, PL=PL: e.activation(out=pexp[:], in_=PL[:, 0:NE], func=AF.Exp,
                                                         bias=mstats[:, 0:1], accum_out=sume[:]),
                 wait=[(sV, sV.n), (sT, sT_lg)], inc=(sS, 1))
            p.do("vector", lambda e: e.reciprocal(gate[:], sume[:]),
                 wait=[(sS, sS.n)], inc=(sV, 1))
            p.do("vector", lambda e: e.max(mx8[:], pexp[:]), inc=(sV, 1), drain=True)
            p.do("vector", lambda e: e.max_index(idx8[:], mx8[:], pexp[:]), inc=(sV, 1), drain=True)
            p.do("vector", lambda e: e.tensor_copy(idxf[:], idx8[:, 0:1]), inc=(sV, 1), drain=True)
            p.do("vector", lambda e, i=i: e.tensor_scalar(G_bf[:, NE * i:NE * (i + 1)], iota8[:],
                                                          idxf[:], gate[:], ALU.is_equal, ALU.mult),
                 inc=(sV, 1), drain=True)
            sV_g = sV.n

            def emit_B(i=i, sV_g=sV_g, PLb=PLb):
                gbuf = i % 2
                p.do("tensor", lambda e, i=i, PLb=PLb: e.transpose(
                    PLb[0:NE, 896:1024], G_bf[:, NE * i:NE * (i + 1)], ident[:]),
                     wait=[(sV, sV_g)], inc=(sT, 1))
                p.do("vector", lambda e, gbuf=gbuf, PLb=PLb: e.tensor_copy(
                    gt_sb[gbuf][:], PLb[0:NE, 896:1024]),
                     wait=[(sT, sT.n)], inc=(sV, 1))
                gtev_hist[i] = sV.n
                pB = [pU0, pU1][i % 2]
                for c in range(2):
                    w_ = [(sV, sV.n), (dkc, dkc_eb)] if c == 0 else []
                    if c == 0 and i >= 2:
                        w_.append((sS, bev_hist[i - 2]))
                    p.do("tensor", lambda e, gbuf=gbuf, c=c, pB=pB: e.matmul(
                        pB[:, 512 * c:512 * (c + 1)], gt_sb[gbuf][:],
                        eb_bf[:, 512 * c:512 * (c + 1)],
                        start=True, stop=True), wait=w_, inc=(sT, 1) if c == 1 else None)
                p.do("scalar", lambda e, i=i, pB=pB: e.activation(out=acc[:, D * i:D * (i + 1)],
                                                                  in_=pB[:], func=AF.Copy),
                     wait=[(sT, sT.n)], inc=(sS, 1))
                bev_hist[i] = sS.n

            pendB[0] = emit_B
        pendB[0]()
        pendB[0] = None
        tab_left = list(range(NT)) if b == 0 else []
        sV_stt = {}

        def x_load(i):
            buf = i % 2
            p.do("sync", lambda e, i=i, buf=buf: e.dma_start(
                out=x_sb[buf][:], in_=dx_f[b * N + 128 * i:b * N + 128 * i + 128, :]),
                 wait=[(dox01[buf], dox01[buf].n)], inc=(dx01[buf], 16))

        def out_hook(j):
            if j == 0:
                x_load(0)
            if j + 1 < NT:
                x_load(j + 1)
            buf = j % 2
            p.do("vector", lambda e, buf=buf, j=j: e.tensor_add(
                x_sb[buf][:], x_sb[buf][:], acc[:, D * j:D * (j + 1)]),
                 wait=[(dx01[buf], dx01[buf].n), (sV, sV.n)], inc=(sV, 1), drain=(j == 0))
            p.do("sync", lambda e, j=j, buf=buf: e.dma_start(
                out=dout_f[b * N + 128 * j:b * N + 128 * j + 128, :], in_=x_sb[buf][:]),
                 wait=[(sV, sV.n)], inc=(dox01[buf], 16))
        for e_i in range(NE):
            buf = e_i % 2
            for i in range(NT):
                pEx = [pU1, pU2][i % 2]
                w0 = [wex_ready[e_i]]
                key = (e_i, i - 2) if i >= 2 else (e_i - 1, NT - 2 + i)
                if key in sV_stt:
                    w0.append((sV, sV_stt[key]))
                else:
                    w0.append((sV, sV.n))
                    w0.append((sS, sS.n))
                for r in range(8):
                    for c in range(2):
                        p.do("tensor", lambda e, i=i, r=r, c=c, buf=buf, pEx=pEx: e.matmul(
                            pEx[:, 512 * c:512 * (c + 1)],
                            xnT_r[:, r, 128 * i:128 * (i + 1)],
                            wex[buf][:, D * r + 512 * c: D * r + 512 * (c + 1)],
                            start=(r == 0), stop=(r == 7), skip_group_check=True),
                             wait=w0 if (r == 0 and c == 0) else [],
                             inc=(sT, 1) if (r == 7 and c == 1) else None)
                p.do("vector", lambda e, i=i, e_i=e_i, pEx=pEx: e.scalar_tensor_tensor(
                    acc[:, D * i:D * (i + 1)], pEx[:], G_bf[:, NE * i + e_i:NE * i + e_i + 1],
                    acc[:, D * i:D * (i + 1)], ALU.mult, ALU.add),
                     wait=[(sT, sT.n)], inc=(sV, 1), drain=(i == 0))
                sV_stt[(e_i, i)] = sV.n
                if b == 0 and i in (NT // 3, (2 * NT) // 3) and tab_left:
                    ti = tab_left.pop(0)
                    emit_tab(ti, [(sV, sV_freq)] if ti == 0 else ())
                if e_i == NE - 1:
                    out_hook(i)
            if e_i + 2 < NE:
                w_le = [(sT, sT.n)]
                if b == 0:
                    w_le.append((dwb, dwb.n))
                load_expert(b, e_i + 2, w_le)
        kv_gate[0] = sV_stt[(NE - 1, NT - 1)]

    def emit_rot(i, g, dst, sub_wait=None):
        cs = cos_tab.ap()[:, 512 * i + 256 * g: 512 * i + 256 * (g + 1)]
        sn = sin_tab.ap()[:, 512 * i + 256 * g: 512 * i + 256 * (g + 1)]
        dr = dst.ap().rearrange("p (h f) -> p h f", h=8)
        ta_r = ta.ap().rearrange("p (h f) -> p h f", h=8)
        tb_r = tb.ap().rearrange("p (h f) -> p h f", h=8)
        tc_r = tc.ap().rearrange("p (h f) -> p h f", h=8)
        td_r = td.ap().rearrange("p (h f) -> p h f", h=8)
        sS0 = sS.n
        p.do("gpsimd", lambda e: e.tensor_mul(tc[:], t2b[:], cs),
             wait=[(sS, sS0), (sV, sV.n)], inc=(sG, 1))
        p.do("gpsimd", lambda e: e.tensor_mul(td[:], t1b[:], sn), inc=(sG, 1))
        sG_rot = sG.n
        p.do("vector", lambda e: e.tensor_mul(ta[:], t1b[:], cs),
             wait=[(sS, sS0)], inc=(sV, 1))
        p.do("vector", lambda e: e.tensor_mul(tb[:], t2b[:], sn), inc=(sV, 1))
        c_tb = sV.n
        w_sub = [(sV, c_tb)] + ([sub_wait] if sub_wait else [])
        p.do("vector", lambda e: e.tensor_sub(dr[:, :, 0:32], ta_r, tb_r),
             wait=w_sub, inc=(sV, 1))
        p.do("vector", lambda e: e.tensor_add(dr[:, :, 32:64], tc_r, td_r),
             wait=[(sG, sG_rot)], inc=(sV, 1))

    w_ready = {"q": [], "kv": [], "wo": []}

    def build_wqT(g, wT=None, b=0):
        if b == 1:
            p.do("sync", lambda e, g=g: e.dma_start(out=wqT[:], in_=dwscr[g, :, 0:4096]),
                 wait=[(sT, sT.n), (dwb, dwb.n)], inc=(dwb, 16))
            w_ready["q"] = [(dwb, dwb.n)]
            return
        def srcq(s):
            return dwqkv[512 * g + 128 * s:512 * g + 128 * (s + 1), :]

        def evq(s):
            return wqT_r[:, :, 128 * s:128 * (s + 1)]

        build_wT(srcq, 4, D, evq,
                 extra_first_wait=[(sT, wT if wT is not None else sT.n),
                                   (sV, sV_ln[0]), (dwb, dwb.n)])
        p.do("sync", lambda e, g=g: e.dma_start(out=dwscr[g, :, 0:4096], in_=wqT[:]),
             wait=[(sV, sV.n)], inc=(dwb, 16))
        w_ready["q"] = []

    def build_wkvT(g, wT=None, b=0):
        if b == 1:
            p.do("sync", lambda e, g=g: e.dma_start(out=wkvT[:], in_=dwscr[g, :, 4096:12288]),
                 wait=[(sT, sT.n), (dwb, dwb.n), (doa01[0], doa01[0].n),
                       (doa01[1], doa01[1].n)], inc=(dwb, 16))
            w_ready["kv"] = [(dwb, dwb.n)]
            return
        def src(s):
            part = 0 if s < 4 else 1
            sl = s % 4
            src_row = D + part * D + 512 * g + 128 * sl
            return dwqkv[src_row:src_row + 128, :]

        def ev(s):
            part = 0 if s < 4 else 1
            sl = s % 4
            coff = 512 * part + 128 * sl
            return wkvT_r[:, :, coff:coff + 128]

        build_wT(src, 8, D, ev,
                 extra_first_wait=[(sT, wT if wT is not None else sT.n),
                                   (sV, sV_ln[0]), (dwb, dwb.n)],
                 extra_tr_wait=[(doa01[0], doa01[0].n), (doa01[1], doa01[1].n)])
        p.do("sync", lambda e, g=g: e.dma_start(out=dwscr[g, :, 4096:12288], in_=wkvT[:]),
             wait=[(sV, sV.n)], inc=(dwb, 16))
        w_ready["kv"] = []

    def build_woT(g, b=0):
        if b == 1:
            p.do("sync", lambda e, g=g: e.dma_start(out=woT[:], in_=dwscr[g, :, 12288:16384]),
                 wait=[(sT, sT.n), (dwb, dwb.n)], inc=(dwb, 16))
            w_ready["wo"] = [(dwb, dwb.n)]
            return
        build_wT(lambda s: dwo[128 * s:128 * (s + 1), 512 * g:512 * (g + 1)], 8, 512,
                 lambda s: woT_r[:, :, 128 * s:128 * (s + 1)],
                 extra_first_wait=[(sT, sT.n), (sV, sV_ln[0]), (dwb, dwb.n)])
        p.do("sync", lambda e, g=g: e.dma_start(out=dwscr[g, :, 12288:16384], in_=woT[:]),
             wait=[(sV, sV.n)], inc=(dwb, 16))
        w_ready["wo"] = []

    q12_r3 = q12.ap().rearrange("p (h f) -> p h f", h=8)
    t1b_r3 = t1b.ap().rearrange("p (h f) -> p h f", h=8)
    t2b_r3 = t2b.ap().rearrange("p (h f) -> p h f", h=8)
    ksq4 = attnT[0].ap()[:, 0:512].rearrange("p (h x f) -> p x h f", h=8, x=2)
    ksq3 = attnT[0].ap()[:, 0:512].rearrange("p (h f) -> p h f", h=8)
    qsq4 = r12.ap().rearrange("p (h x f) -> p x h f", h=8, x=2)
    qsq3 = r12.ap().rearrange("p (h f) -> p h f", h=8)

    kv_gate = [0]
    norm_gate = {}
    attn_marks = {}

    def emit_attn_prep(b):
        p.do("vector", lambda e: e.memset(v_aug_r[:, :, :, 64:65], 1.0),
             wait=[(sT, sT.n), (dtap, dtap.n), (dwb, dwb.n)], inc=(sV, 1))
        p.do("vector", lambda e: e.memset(q12_r3[:, :, 65:67], 1.0), inc=(sV, 1))

    def emit_kv(b, g, tile_hook=None):
        pPP = [pU0, pU1, pU2]
        hist = {}
        sV_kv0 = sV.n
        p.do("vector", lambda e: e.memset(kmax[:], 0.0),
             wait=[(sV, sV.n)], inc=(sV, 1))
        sse = {}
        c_rot = {}
        c_sq = {}
        c_red = {}
        c_tp = {}
        sT_grp = sT.n

        def kv_transposes(j):
            for h in range(8):
                w_ = [(sV, hist[j - 1][1] if j >= 1 else sV.n)] if h == 0 else []
                p.do("tensor", lambda e, h=h: e.transpose(
                    pQb[0:64, 128 * h:128 * (h + 1)], r12[:, 64 * h:64 * (h + 1)], ident[:]),
                     wait=w_, inc=(sT, 1) if h == 7 else None)
            c_tp[j] = sT.n

        def kv_copy(j):
            w_ = [(sT, c_tp[j])]
            if j == 0:
                w_.append((sV, sV_kv0))
            p.do("vector", lambda e, j=j: e.tensor_copy(
                k_aug_r[0:64, :, 128 * j:128 * (j + 1)], pQb_r[0:64, :, :]),
                 wait=w_, inc=(sV, 1))
            hist[j] = (sse[j], sV.n)

        for j in range(NT):
            pk = pPP[j % 3]
            if j < 3:
                if g == 0:
                    w0 = [(sS, sS.n), (sV, kv_gate[0])]
                else:
                    w0 = [(sS, norm_gate.get("s", sS.n)), (sV, norm_gate.get("v", sV.n))]
                w0 += [(dwb, dwb.n)] + list(w_ready["kv"])
            else:
                w0 = [(sS, hist[j - 3][0]), (sV, hist[j - 3][1])]
            for r in range(8):
                for c in range(2):
                    p.do("tensor", lambda e, j=j, r=r, c=c, pk=pk: e.matmul(
                        pk[:, 512 * c:512 * (c + 1)],
                        xnT_r[:, r, 128 * j:128 * (j + 1)],
                        wkvT[:, D * r + 512 * c:D * r + 512 * (c + 1)],
                        start=(r == 0), stop=(r == 7), skip_group_check=True),
                         wait=w0 if (r == 0 and c == 0) else [],
                         inc=(sT, 1) if (r == 7 and c == 1) else None)
            sT_kv = sT.n
            wv_ = [(sT, sT_kv)]
            if j >= 1:
                wv_.append((sV, c_rot[j - 1]))
                wv_.append((sG, c_sq[j - 1]))
            else:
                wv_.append((sG, sG.n))
            p.do("scalar", lambda e, j=j, pk=pk: e.activation(
                out=v_aug_r[:, j, :, 0:64],
                in_=pk.ap()[:, 512:1024].rearrange("p (h c) -> p h c", h=8), func=AF.Copy),
                 wait=wv_, inc=(sS, 1))
            p.do("scalar", lambda e, pk=pk: e.activation(out=t1b[:], in_=pk.ap()[:, 0:512:2],
                                                         func=AF.Copy), inc=(sS, 1))
            p.do("scalar", lambda e, pk=pk: e.activation(out=t2b[:], in_=pk.ap()[:, 1:512:2],
                                                         func=AF.Copy), inc=(sS, 1))
            sse[j] = sS.n
            if j >= 1:
                kv_transposes(j - 1)
            emit_rot(j, g, r12,
                     sub_wait=(sT, c_tp[j - 1]) if j >= 1 else None)
            c_rot[j] = sV.n
            wsq = [(sS, sse[j])]
            if j >= 1:
                wsq.append((sV, c_red[j - 1]))
            else:
                wsq.append((sT, sT_grp))
            p.do("gpsimd", lambda e: e.tensor_tensor(out=ksq4[:, 0], in0=t1b_r3[:],
                                                     in1=t1b_r3[:], op=ALU.mult),
                 wait=wsq, inc=(sG, 1))
            p.do("gpsimd", lambda e: e.tensor_tensor(out=ksq4[:, 1], in0=t2b_r3[:],
                                                     in1=t2b_r3[:], op=ALU.mult),
                 inc=(sG, 1))
            c_sq[j] = sG.n
            p.do("vector", lambda e: e.tensor_reduce(ktmp[:], ksq3[:], AX.X, ALU.add),
                 wait=[(sG, c_sq[j])], inc=(sV, 1))
            c_red[j] = sV.n
            p.do("vector", lambda e: e.tensor_tensor(out=kmax[:], in0=kmax[:], in1=ktmp[:],
                                                     op=ALU.max),
                 wait=[(sV, c_red[j])], inc=(sV, 1))
            if j >= 1:
                kv_copy(j - 1)
        kv_transposes(NT - 1)
        kv_copy(NT - 1)
        p.do("vector", lambda e: e.memset(k_aug[64:68, :], 1.0),
             wait=[(sT, sT.n), (sV, sV.n), (dtap, dtap.n)], inc=(sV, 1))
        p.do("gpsimd", lambda e: e.iota(kciota[:], pattern=[[1, N]], base=0, channel_multiplier=0,
                                        allow_small_or_imprecise_dtypes=True),
             wait=[(sT, sT.n), (sV, sV.n), (dtap, dtap.n)], inc=(sG, 1))
        p.do("vector", lambda e: e.tensor_scalar(
            kc_all[:], kciota[:], c_vec[:], None, ALU.mult),
             wait=[(sG, sG.n), (sV, sV_kc)], inc=(sV, 1), drain=True)
        p.do("vector", lambda e: e.tensor_copy(kc_hibf[:], kc_all[:]),
             inc=(sV, 1), drain=True)
        p.do("vector", lambda e: e.tensor_sub(kc_all[:], kc_all[:], kc_hibf[:]),
             inc=(sV, 1), drain=True)
        p.do("gpsimd", lambda e, g=g: e.dma_start(
            out=k_aug.ap()[65:66, :].rearrange("p (h c) -> p h c", h=HGN),
            in_=kc_hibf.ap()[HGN * g:HGN * (g + 1), :]),
             wait=[(sV, sV.n), (sT, sT.n), (dkc, dkc.n)], inc=(dkc, 16))
        p.do("gpsimd", lambda e, g=g: e.dma_start(
            out=k_aug.ap()[66:67, :].rearrange("p (h c) -> p h c", h=HGN),
            in_=kc_all.ap()[HGN * g:HGN * (g + 1), :]),
             wait=[(dkc, dkc.n)], inc=(dkc, 16))
        p.do("vector", lambda e: e.tensor_copy(km_b[:], kmax[:]), inc=(sV, 1), drain=True)
        p.do("tensor", lambda e: e.transpose(pQb[0:8, 0:128], km_b[:], ident[:]),
             wait=[(sV, sV.n)], inc=(sT, 1))
        p.do("vector", lambda e: e.tensor_reduce(m2b[:], pQb[0:8, 0:128], AX.X, ALU.max),
             wait=[(sT, sT.n)], inc=(sV, 1))
        p.do("tensor", lambda e: e.transpose(pQb[0:1, 0:8], m2b[:], ident[0:8, 0:8]),
             wait=[(sV, sV.n)], inc=(sT, 1))
        p.do("scalar", lambda e: e.activation(out=srow_bf[:], in_=pQb[0:1, 0:8], func=AF.Copy),
             wait=[(sT, sT.n)], inc=(sS, 1))
        p.do("tensor", lambda e: e.matmul(pOut[:, 0:8], ones1[:], srow_bf[:],
                                          start=True, stop=True),
             wait=[(sS, sS.n)], inc=(sT, 1))
        p.do("tensor", lambda e, g=g: e.matmul(pOut[:, 8:16], ones_f[:],
                                               crow_f[0:1, HGN * g:HGN * (g + 1)],
                                               start=True, stop=True,
                                               skip_group_check=True),
             wait=[(sS, sS_crow)], inc=(sT, 1))
        p.do("scalar", lambda e: e.activation(out=bmc[:, 0:8], in_=pOut[:, 0:8], func=AF.Copy),
             wait=[(sT, sT.n)], inc=(sS, 1))
        p.do("scalar", lambda e: e.activation(out=bmc[:, 8:16], in_=pOut[:, 8:16],
                                              func=AF.Copy, scale=1.0001), inc=(sS, 1))

    def emit_attn(b, g):
        unit_i = [0]

        def next_unit():
            u = units[unit_i[0] % 6]
            unit_i[0] += 1
            return u

        first = [True]
        pend = [None]
        prev_av_end = [{}]
        qa = {}
        qev = {}
        rotv = {}
        sqg = {}
        red = {}
        tBc = {}
        c67 = {}
        g64 = {}
        sqc = {}
        tp = {}
        st7 = {}

        def q_mms(i):
            if i == 0:
                w0 = [(sS, sS.n), (sV, sV.n), (sG, sG.n)] + list(w_ready["q"])
            else:
                w0 = [(sV, qa[i - 1]), (sS, qev[i - 1])]
            for r in range(8):
                p.do("tensor", lambda e, i=i, r=r: e.matmul(
                    pQ[:], xnT_r[:, r, 128 * i:128 * (i + 1)],
                    wqT[:, 512 * r: 512 * (r + 1)],
                    start=(r == 0), stop=(r == 7)),
                     wait=w0 if r == 0 else [], inc=(sT, 1) if r == 7 else None)

        def q_evacs(i):
            w1 = [(sT, sT.n)]
            if i >= 1:
                w1 += [(sV, rotv[i - 1]), (sG, sqg[i - 1])]
            else:
                w1 += [(sV, sV.n), (sG, sG.n)]
            p.do("scalar", lambda e: e.activation(out=t1b[:], in_=pQ.ap()[:, 0:512:2],
                                                  func=AF.Copy, scale=0.125),
                 wait=w1, inc=(sS, 1))
            p.do("scalar", lambda e: e.activation(out=t2b[:], in_=pQ.ap()[:, 1:512:2],
                                                  func=AF.Copy, scale=0.125), inc=(sS, 1))
            qev[i] = sS.n

        def q_tail1(i):
            wkt = [(sS, qev[i])]
            if i >= 1:
                wkt.append((sV, tBc[i - 1]))
            p.do("vector", lambda e, i=i: e.tensor_scalar(
                ktmp[:], bmc[:, 8:16], tokc[:, i:i + 1], None, ALU.mult),
                 wait=wkt, inc=(sV, 1))
            emit_rot(i, g, q12,
                     sub_wait=(sT, tp[i - 1]) if i >= 1 else None)
            rotv[i] = sV.n
            wsq = [(sS, qev[i])]
            if i >= 1:
                wsq.append((sV, red[i - 1]))
            else:
                wsq.append((sT, sT.n))
            p.do("gpsimd", lambda e: e.tensor_tensor(out=qsq4[:, 0], in0=t1b_r3[:],
                                                     in1=t1b_r3[:], op=ALU.mult),
                 wait=wsq, inc=(sG, 1))
            p.do("gpsimd", lambda e: e.tensor_tensor(out=qsq4[:, 1], in0=t2b_r3[:],
                                                     in1=t2b_r3[:], op=ALU.mult),
                 inc=(sG, 1))
            sqg[i] = sG.n

        def q_tail2(i):
            wrd = [(sG, sqg[i])]
            if i >= 1:
                wrd += [(sS, sqc[i - 1]), (sV, tBc[i - 1])]
            p.do("vector", lambda e: e.tensor_reduce(qn2[:], qsq3[:], AX.X, ALU.add),
                 wait=wrd, inc=(sV, 1))
            red[i] = sV.n
            wbq = [(sV, red[i])]
            if i >= 1:
                wbq.append((sS, sqc[i - 1]))
            p.do("gpsimd", lambda e: e.tensor_tensor(out=bq2[:], in0=qn2[:], in1=bmc[:, 0:8],
                                                     op=ALU.mult),
                 wait=wbq, inc=(sG, 1))
            p.do("scalar", lambda e: e.activation(out=qn2[:], in_=bq2[:], func=AF.Sqrt),
                 wait=[(sG, sG.n)], inc=(sS, 1))
            p.do("scalar", lambda e: e.activation(out=dummy1[:], in_=cb_eps[0:1, :],
                                                  func=AF.Exp), inc=(sS, 1))
            sqc[i] = sS.n
            wtb = [(sS, sqc[i])]
            if i >= 1:
                wtb += [(sG, g64[i - 1]), (sV, c67[i - 1])]
            p.do("vector", lambda e: e.scalar_tensor_tensor(
                tB[:], qn2[:], -1.05, ktmp[:], ALU.mult, ALU.subtract),
                 wait=wtb, inc=(sV, 1))
            tBc[i] = sV.n
            wc64 = [(sV, tBc[i])]
            if i >= 1:
                wc64.append((sT, tp[i - 1]))
            p.do("gpsimd", lambda e: e.tensor_copy(q12_r3[:, :, 64:65], tB[:]),
                 wait=wc64, inc=(sG, 1))
            g64[i] = sG.n
            wc67 = [(sV, tBc[i]), (sG, g64[i])]
            if i >= 1:
                wc67.append((sT, tp[i - 1]))
            p.do("vector", lambda e: e.tensor_sub(q12_r3[:, :, 67:68], tB[:],
                                                  q12_r3[:, :, 64:65]),
                 wait=wc67, inc=(sV, 1))
            c67[i] = sV.n
            for h in range(8):
                w_ = [(sV, c67[i]), (sS, qev[i])] if h == 0 else []
                p.do("tensor", lambda e, h=h: e.transpose(
                    pQb[0:68, 128 * h:128 * (h + 1)], q12[:, 68 * h:68 * (h + 1)], ident[:]),
                     wait=w_, inc=(sT, 1) if h == 7 else None)
            tp[i] = sT.n

        def q_copy(i):
            w_ = [(sT, tp[i])]
            if i >= 1:
                w_.append((sT, st7[i - 1]))
            p.do("vector", lambda e: e.tensor_copy(q_aug[0:68, :], pQb.ap()[0:68, 0:1024]),
                 wait=w_, inc=(sV, 1))
            qa[i] = sV.n

        q_mms(0)
        q_evacs(0)
        q_tail1(0)
        q_tail2(0)
        q_copy(0)

        for i in range(NT):
            L = 128 * (i + 1)
            nu = (L + 511) // 512
            sS_exp = {}
            sG_mask = {}
            av_end = {}
            sV_ev = [sV.n]

            def emit_st(h, sS_exp=sS_exp, sG_mask=sG_mask, av_end=av_end, i=i, L=L, nu=nu):
                ab = attnT[(i * 8 + h) % 2]
                for u_i in range(nu):
                    c0 = 512 * u_i
                    w_ = min(512, L - c0)
                    u = next_unit()
                    for jj in range(w_ // 128):
                        j = (c0 // 128) + jj
                        wl = []
                        if u_i == 0 and jj == 0:
                            wl.append((sV, qa[i]))
                            if first[0]:
                                wl.append((dkc, dkc.n))
                                wl.append((dtap, dtap.n))
                                first[0] = False
                        if jj == 0 and u["free"]:
                            wl.append(u["free"])
                        p.do("tensor", lambda e, u=u, h=h, j=j, jj=jj: e.matmul(
                            u["t"][:, u["c"] + 128 * jj: u["c"] + 128 * (jj + 1)],
                            k_aug[0:68, N * h + 128 * j: N * h + 128 * (j + 1)],
                            q_aug[0:68, 128 * h:128 * (h + 1)],
                            start=True, stop=True), wait=wl,
                             inc=(sT, 1) if jj == w_ // 128 - 1 else None)
                    wl2 = [(sT, sT.n)]
                    if u_i == 0 and h >= 2 and (h - 2) in av_end:
                        wl2.append((sT, av_end[h - 2]))
                    if u_i == 0 and h == 1 and 7 in prev_av_end[0]:
                        wl2.append((sT, prev_av_end[0][7]))
                    p.do("scalar", lambda e, u=u, ab=ab, c0=c0, w_=w_: e.activation(
                        out=ab[:, c0:c0 + w_], in_=u["t"][:, u["c"]:u["c"] + w_], func=AF.Exp),
                         wait=wl2, inc=(sS, 1))
                    u["free"] = (sS, sS.n)
                sS_exp[h] = sS.n
                p.do("gpsimd", lambda e, ab=ab, L=L: e.tensor_mul(
                    ab[:, L - 128:L], ab[:, L - 128:L], mask_st[:]),
                     wait=[(sS, sS_exp[h])], inc=(sG, 1))
                sG_mask[h] = sG.n

            def emit_av(h, sS_exp=sS_exp, sG_mask=sG_mask, av_end=av_end,
                        sV_ev=sV_ev, i=i, L=L, nu=nu):
                ab = attnT[(i * 8 + h) % 2]
                for j in range(i + 1):
                    wl = [(sG, sG_mask[h]), (sS, sS_exp[h]), (sV, sV_ev[0])] if j == 0 else []
                    p.do("tensor", lambda e, j=j, h=h, i=i, ab=ab: e.matmul(
                        pOut[0:65, 128 * (h % 4):128 * (h % 4 + 1)],
                        v_aug.ap()[:, (j * HGN + h) * 65:(j * HGN + h) * 65 + 65],
                        ab[:, 128 * j:128 * (j + 1)],
                        start=(j == 0), stop=(j == i)), wait=wl,
                         inc=(sT, 1) if j == i else None)
                av_end[h] = sT.n
                if h % 4 == 3:
                    sT_av = sT.n
                    hg = h // 4
                    p0 = (h - 3) // 2
                    pOut_g = pOut.ap()[0:64, :].rearrange("p (pr a c) -> p pr a c", pr=2, a=2)
                    p.do("vector", lambda e, p0=p0, i=i, pOut_g=pOut_g: e.tensor_copy(
                        attnout_r[0:64, p0:p0 + 2, 128 * i:128 * (i + 1)], pOut_g[:, :, 0, :]),
                         wait=[(sT, sT_av)], inc=(sV, 1))
                    p.do("vector", lambda e, p0=p0, i=i, pOut_g=pOut_g: e.tensor_copy(
                        attnout_r[64:128, p0:p0 + 2, 128 * i:128 * (i + 1)], pOut_g[:, :, 1, :]),
                         inc=(sV, 1))
                    p.do("vector", lambda e, i=i, hg=hg: e.tensor_copy(
                        rs_stage.ap()[0:1, 1024 * i + 512 * hg:1024 * i + 512 * (hg + 1)],
                        pOut[64:65, 0:512]),
                         wait=[(dox01[0], dox01[0].n), (dox01[1], dox01[1].n)],
                         inc=(sV, 1))
                    sV_ev[0] = sV.n

            for h in range(8):
                emit_st(h)
                if h == 7:
                    st7[i] = sT.n
                if h == 0 and pend[0] is not None:
                    pend[0]()
                    pend[0] = None
                    sV_ev[0] = sV.n
                    if i == 8:
                        attn_marks["rs_half"] = sV.n
                if i + 1 < NT:
                    if h == 1:
                        q_mms(i + 1)
                    elif h == 2:
                        q_evacs(i + 1)
                    elif h == 3:
                        q_tail1(i + 1)
                    elif h == 5:
                        q_tail2(i + 1)
                if h >= 1:
                    emit_av(h - 1)
            if i + 1 < NT:
                q_copy(i + 1)
            pend[0] = lambda emit_av=emit_av: emit_av(7)
            prev_av_end[0] = av_end
        if pend[0] is not None:
            pend[0]()
            pend[0] = None

    def emit_norm_wo(b, g):
        sV_n0 = sV.n
        nm = (NT + 7) // 8
        half = min(8, NT) * 1024
        p.do("scalar", lambda e: e.activation(out=rs_stage[0:1, 0:half],
                                              in_=rs_stage[0:1, 0:half], func=AF.Ln),
             wait=[(sV, attn_marks.get("rs_half", sV.n)), (dtap, dtap.n)], inc=(sS, 1))
        p.do("scalar", lambda e: e.activation(out=rs_stage[0:1, 0:half],
                                              in_=rs_stage[0:1, 0:half], func=AF.Exp,
                                              scale=-1.0), inc=(sS, 1), drain=True)
        sS_rsm = {0: sS.n}
        if nm > 1:
            p.do("scalar", lambda e: e.activation(out=rs_stage[0:1, half:NT * 1024],
                                                  in_=rs_stage[0:1, half:NT * 1024],
                                                  func=AF.Ln),
                 wait=[(sV, sV_n0)], inc=(sS, 1), drain=True)
            p.do("scalar", lambda e: e.activation(out=rs_stage[0:1, half:NT * 1024],
                                                  in_=rs_stage[0:1, half:NT * 1024],
                                                  func=AF.Exp, scale=-1.0),
                 inc=(sS, 1), drain=True)
        sS_rsm[1] = sS.n
        doa_entry = [doa01[0].n, doa01[1].n]
        sV_mult = {}
        sV_m = {}
        rs5 = rs_stage.ap().rearrange("p (i g w q) -> p i g w q", i=NT, g=2, w=4)
        seq = 0
        for m in range(nm):
            for h in range(8):
                hg = h // 4
                w4 = h % 4
                rows = slice(0, 64) if h % 2 == 0 else slice(64, 128)
                pblk = h // 2
                pnorm = [pU1, pU2][seq % 2]
                wcols = min(1024, 128 * NT - 1024 * m)
                for c in range((wcols + 511) // 512):
                    i0 = 8 * m + 4 * c
                    nb = min(4, NT - i0)
                    rhs = rs5[0:1, i0:i0 + nb, hg, w4, :]
                    wl = []
                    if c == 0:
                        if seq % 8 < 2:
                            wl = [(sV, sV_n0), (sS, sS_rsm[m]), (sT, sT.n), (dtap, dtap.n)]
                        if seq >= 2:
                            wl.append((sV, sV_mult[seq - 2]))
                    p.do("tensor", lambda e, rhs=rhs, c=c, pnorm=pnorm, nb=nb: e.matmul(
                        pnorm[0:64, 512 * c:512 * c + 128 * nb], ones1[0:1, 0:64], rhs,
                        start=True, stop=True), wait=wl,
                         inc=(sT, 1) if 512 * c + 128 * nb >= wcols else None)
                p.do("vector", lambda e, rows=rows, pblk=pblk, m=m, pnorm=pnorm, wcols=wcols: e.tensor_mul(
                    attnout_r[rows, pblk, 1024 * m:1024 * m + wcols],
                    attnout_r[rows, pblk, 1024 * m:1024 * m + wcols], pnorm[0:64, 0:wcols]),
                     wait=[(sT, sT.n)], inc=(sV, 1), drain=(seq == 0))
                sV_mult[seq] = sV.n
                seq += 1
            sV_m[m] = sV.n
        sV_norm = sV.n
        if stop_after == "normmul":
            return
        wev_hist = {}
        for i in range(NT):
            buf = i % 2
            tgts = [pU0[:, 0:512], pU0[:, 512:1024]] if i % 2 == 0 else \
                   [pOut[:, 0:512], pQ[:, 0:512]]
            for pr in range(4):
                for c in range(2):
                    wl = []
                    if pr == 0 and c == 0:
                        wl.append((sV, sV_m[min(i // 8, nm - 1)]))
                        wl += list(w_ready["wo"]) if i == 0 else []
                        if i >= 2:
                            wl.append((sS, wev_hist[i - 2]))
                    p.do("tensor", lambda e, i=i, pr=pr, c=c, tgts=tgts: e.matmul(
                        tgts[c],
                        attnout_r[:, pr, 128 * i:128 * (i + 1)],
                        woT.ap()[:, D * pr + 512 * c:D * pr + 512 * (c + 1)],
                        start=(pr == 0), stop=(pr == 3),
                        skip_group_check=True), wait=wl,
                         inc=(sT, 1) if (pr == 3 and c == 1) else None)
            wl = [(sT, sT.n), (doa01[buf], doa01[buf].n)]
            p.do("scalar", lambda e, buf=buf, tgts=tgts: e.activation(
                out=out_sb[buf][:, 0:512], in_=tgts[0], func=AF.Copy),
                 wait=wl, inc=(sS, 1))
            p.do("scalar", lambda e, buf=buf, tgts=tgts: e.activation(
                out=out_sb[buf][:, 512:1024], in_=tgts[1], func=AF.Copy),
                 inc=(sS, 1))
            wev_hist[i] = sS.n
            if i == NT - 2:
                norm_gate["s"] = sS.n
                norm_gate["v"] = sV_m[nm - 1]
            row0 = b * N + 128 * i
            if accum_mode == "add":
                p.do("gpsimd", lambda e, row0=row0, buf=buf: e.dma_start(
                    out=dout_f[row0:row0 + 128, :], in_=out_sb[buf][:], accum_op=ALU.add),
                     wait=[(sS, sS.n), (dox01[0], dox01[0].n), (dox01[1], dox01[1].n),
                           (doa01[0], doa_entry[0]), (doa01[1], doa_entry[1])],
                     inc=(doa01[buf], 16))
            else:
                p.do("gpsimd", lambda e, row0=row0, buf=buf: e.dma_start(
                    out=dout_f[row0:row0 + 128, :], in_=out_sb[buf][:]),
                     wait=[(sS, sS.n), (dox01[0], dox01[0].n), (dox01[1], dox01[1].n),
                           (doa01[0], doa_entry[0]), (doa01[1], doa_entry[1])],
                     inc=(doa01[buf], 16))

    STOP = ["prep", "ln", "ffn", "attnprep", "wkv", "kv", "attn", "norm", "all"].index({"wobuild": "norm", "normmul": "norm"}.get(stop_after, stop_after))
    sV_ln = [0]
    for b in range(NB if STOP >= 8 else 1):
        if STOP < 1: break
        load_expert(b, 0, [(sT, sT.n), (sS, sS.n), (sV, sV.n)])
        load_expert(b, 1, [])
        emit_ln(b)
        sV_ln[0] = sV.n
        sT_ln = sT.n
        tap(f"xnT_b{b}", xnT.ap(), [(sV, sV.n)])
        if STOP < 2: break
        build_wqT(0, sT_ln, b)
        sT_wq = sT.n
        build_wkvT(0, sT_wq, b)
        sT_bld = sT.n
        emit_ffn(b)
        tap(f"acc_b{b}", acc.ap(), [(sV, sV.n)])
        tap(f"G_b{b}", G_bf.ap(), [(sV, sV.n)])
        if STOP < 3: break
        emit_attn_prep(b)
        for g in range(NG if STOP >= 8 else 1):
            if STOP < 4: break
            if g == 1:
                build_wqT(g, sT_bld, b)
                build_wkvT(g, sT.n, b)
            if STOP < 5: break
            emit_kv(b, g)
            tap(f"kaug_b{b}g{g}", k_aug.ap()[0:66, :], [(sV, sV.n), (dkc, dkc.n)])
            tap(f"vaug_b{b}g{g}", v_aug.ap(), [(sS, sS.n)])
            if STOP < 6: break
            build_woT(g, b)
            emit_attn(b, g)
            tap(f"qaug_b{b}g{g}", q_aug.ap()[0:66, :], [(sV, sV.n), (sS, sS.n)])
            tap(f"attnout_b{b}g{g}", attnout.ap(), [(sV, sV.n)])
            tap(f"rs_b{b}g{g}", rs_stage.ap(), [(sV, sV.n)])
            if STOP < 7: break
            if stop_after == "wobuild":
                break
            emit_norm_wo(b, g)
            tap(f"attnoutn_b{b}g{g}", attnout.ap(), [(sV, sV.n)])

    p.do("gpsimd", lambda e: e.nop(),
         wait=[(doa01[0], doa01[0].n), (doa01[1], doa01[1].n),
               (dox01[0], dox01[0].n), (dox01[1], dox01[1].n)] +
              ([(dtap, dtap.n)] if tap_t else []))
    p.emit()
    ctx.close()
    return nc


_BUILT = {}


def _get_nc(n_tiles, taps=()):
    key = (n_tiles, tuple(taps))
    if key not in _BUILT:
        _BUILT[key] = build(n_tiles, taps)
    return _BUILT[key]


def kernel(L0_attn_x, L0_attn_Wqkv, L0_attn_Wo, L0_attn_log_xi,
           L0_attn_pi_gate_logit, L0_attn_e_gate_logit,
           L0_ffn_router_w, L0_ffn_router_b, L0_ffn_expert_w, L0_ffn_expert_b):
    x = np.ascontiguousarray(np.asarray(L0_attn_x, dtype=np.float32))
    B, Nseq, Dm = x.shape
    nt = Nseq // 128
    nc = _get_nc(nt)
    common = {
        "wqkv": np.ascontiguousarray(np.asarray(L0_attn_Wqkv, dtype=np.float32)),
        "wo": np.ascontiguousarray(np.asarray(L0_attn_Wo, dtype=np.float32)),
        "log_xi": np.ascontiguousarray(np.asarray(L0_attn_log_xi, dtype=np.float32)),
        "pi_gate": np.ascontiguousarray(np.asarray(L0_attn_pi_gate_logit, dtype=np.float32)),
        "e_gate": np.ascontiguousarray(np.asarray(L0_attn_e_gate_logit, dtype=np.float32)),
        "router_w": np.ascontiguousarray(np.asarray(L0_ffn_router_w, dtype=np.float32)),
        "router_b": np.ascontiguousarray(np.asarray(L0_ffn_router_b, dtype=np.float32)),
        "expert_w": np.ascontiguousarray(np.asarray(L0_ffn_expert_w, dtype=np.float32)),
        "expert_b": np.ascontiguousarray(np.asarray(L0_ffn_expert_b, dtype=np.float32)),
    }
    ncr = B // NB
    in_maps = []
    for c in range(ncr):
        m = dict(common)
        m["x"] = np.ascontiguousarray(x[NB * c:NB * (c + 1)])
        in_maps.append(m)
    res = run_bass_kernel_spmd(nc, in_maps, core_ids=list(range(ncr)))
    out = np.concatenate([res.results[c]["out"] for c in range(ncr)], axis=0)
    return out.astype(np.float32)
